# version 02
# baseline (speedup 1.0000x reference)
import contextlib
import numpy as np
import concourse.bass as bass
import concourse.mybir as mybir
from concourse.bass_utils import run_bass_kernel_spmd
from concourse.bass_core_rust import IndirectOffsetOnAxis

F32 = mybir.dt.float32
BF16 = mybir.dt.bfloat16
I8 = mybir.dt.int8
U32 = mybir.dt.uint32
ALU = mybir.AluOpType
AF = mybir.ActivationFunctionType
AX = mybir.AxisListType

C = 2048
T = 2048
FT = C // 128
NCORE = 8
INTER = 8192
NIT = INTER // 128
LCH = 64
LSUB = 16
NCH = T // LCH
NPAIR = 16
HB = 16
KSEL = 128
EPS = 1e-5
SCL_A = 0.125
SCL_B = 0.08838834764831845
BIG = 1.0e30

_BUILD_UPTO = "full"
_SCAN_DBG = False
_FFN_DBG = None
_AB_NORES = False
_NO_DBGCOPY = False

ENGS = ("pe", "dve", "act", "pool", "sp")
NDSEM = 8
_SPSEMS = tuple("spd%d" % i for i in range(NDSEM))
_PQSEMS = tuple("pqd%d" % i for i in range(NDSEM))
SEMS = ("pe", "dve", "act", "pool_c", "cc0", "cc1", "cc2") + _SPSEMS + _PQSEMS
_OWN = {("pe", "pe"), ("dve", "dve"), ("act", "act"), ("pool", "pool_c")}


class K:

    def __init__(self, nc):
        self.nc = nc
        self.prog = {e: [] for e in ENGS}
        self.cnt = {s: 0 for s in SEMS}
        self.waited = {e: {s: 0 for s in SEMS} for e in ENGS}
        self.sems = {}
        self.wtok = {}
        self.rtok = {}
        self.rr = {"sp": 0, "pool": 0}
        self._pend_rd = set()
        self._pend_wr = set()

    def _wait(self, eng, sem, val):
        if val <= 0 or self.waited[eng][sem] >= val:
            return
        assert val <= self.cnt[sem], f"wait on future {sem}={val} (now {self.cnt[sem]})"
        self.waited[eng][sem] = val
        self.prog[eng].append(("w", sem, val))

    def emit(self, eng, fn, deps=(), pub=False, dma=False, sem=None, inc=None):
        for d in deps:
            if d is not None:
                self._wait(eng, d[0], d[1])
        if sem is None and (pub or dma):
            assert not dma, "dma emits must pass sem explicitly (use K.dma)"
            sem = {"pe": "pe", "dve": "dve", "act": "act", "pool": "pool_c"}[eng]
        if sem is not None:
            step = inc if inc is not None else (16 if dma else 1)
            self.cnt[sem] += step
            self.prog[eng].append(("i", fn, sem, step))
            return (sem, self.cnt[sem])
        self.prog[eng].append(("i", fn, None, 0))
        return None

    def op(self, eng, fn, rd=(), wr=(), deps=(), pub=None, dma=False, sem=None, inc=None,
           war=()):
        dl = list(deps)
        for b in rd:
            dl.append(self.wtok.get(b))
        for b in war:
            dl.append(self.wtok.get(b))
            dl.extend(self.rtok.get(b, ()))
        for b in wr:
            assert b not in self._pend_rd and b not in self._pend_wr, \
                f"write to {b} while unpublished pe access pending"
            dl.append(self.wtok.get(b))
            dl.extend(self.rtok.get(b, ()))
        if pub is None:
            pub = (eng != "pe")
        tok = self.emit(eng, fn, deps=dl, pub=pub, dma=dma, sem=sem, inc=inc)
        if tok is None:
            self._pend_rd.update(rd)
            self._pend_wr.update(wr)
            return None
        if eng == "pe":
            for b in self._pend_rd:
                self.rtok.setdefault(b, []).append(tok)
            for b in self._pend_wr:
                self.wtok[b] = tok
                self.rtok[b] = []
            self._pend_rd.clear()
            self._pend_wr.clear()
        for b in rd:
            self.rtok.setdefault(b, []).append(tok)
        for b in wr:
            self.wtok[b] = tok
            self.rtok[b] = []
        return tok

    def dma_fn(self, q, fn, rd=(), wr=(), deps=()):
        i = self.rr[q]
        self.rr[q] = (i + 1) % NDSEM
        sem = ("spd%d" % i) if q == "sp" else ("pqd%d" % i)
        self._wait(q, sem, self.cnt[sem])
        return self.op(q, fn, rd=rd, wr=wr, deps=deps, dma=True, sem=sem)

    def dma(self, q, out, in_, rd=(), wr=(), deps=()):
        i = self.rr[q]
        self.rr[q] = (i + 1) % NDSEM
        sem = ("spd%d" % i) if q == "sp" else ("pqd%d" % i)
        self._wait(q, sem, self.cnt[sem])
        return self.op(q, lambda e, o=out, i_=in_: e.dma_start(out=o, in_=i_),
                       rd=rd, wr=wr, deps=deps, dma=True, sem=sem)

    def mm(self, out, lhsT, rhs, start, stop, rd=(), wr=(), deps=(), pub=False, war=(),
           tp=None):
        return self.op("pe", lambda e, o=out, l=lhsT, r=rhs, s1=start, s2=stop, tp=tp:
                       e.matmul(o, l, r, start=s1, stop=s2, tile_position=tp),
                       rd=rd, wr=wr, deps=deps, pub=pub, war=war)

    def barrier(self):
        assert not self._pend_rd and not self._pend_wr
        snap = dict(self.cnt)
        for e in ENGS:
            for s, v in snap.items():
                if (e, s) in _OWN:
                    continue
                self._wait(e, s, v)
        self.wtok.clear()
        self.rtok.clear()

    def finalize(self):
        self.barrier()
        deps = [(s, self.cnt[s]) for s in _SPSEMS + _PQSEMS]
        self.emit("sp", lambda e: e.nop(), deps=deps)

    def replay(self, block):
        def mk(eng):
            def body(e):
                for item in self.prog[eng]:
                    if item[0] == "w":
                        e.wait_ge(self.sems[item[1]], item[2])
                    else:
                        _, fn, sem, step = item
                        ins = fn(e)
                        if sem is not None:
                            ins.then_inc(self.sems[sem], step)
            return body
        block.tensor(mk("pe"))
        block.vector(mk("dve"))
        block.scalar(mk("act"))
        block.gpsimd(mk("pool"))
        block.sync(mk("sp"))



def alloc_ln_bufs(nc, pst, tag):
    return {
        "tag": tag,
        "xc": pst.enter_context(nc.sbuf_tensor(tag + "_xc", [128, FT * 512], F32)),
        "sq": pst.enter_context(nc.sbuf_tensor(tag + "_sq", [128, 512], F32)),
        "srow": pst.enter_context(nc.sbuf_tensor(tag + "_srow", [1, 1536], F32)),
        "bc": pst.enter_context(nc.sbuf_tensor(tag + "_bc", [128, 1024], F32)),
        "ps_s": pst.enter_context(nc.psum_tensor(tag + "_pss", [1, 512], F32)),
        "ps_q": pst.enter_context(nc.psum_tensor(tag + "_psq", [1, 512], F32)),
        "ps_b": pst.enter_context(nc.psum_tensor(tag + "_psb", [128, 1024], F32)),
    }


def ln_chunk(k, B, P, ch, rsrc, out_cb, gain=None, bias=None, pre_cb=None):
    tg = B["tag"]
    xc, sq, srow, bc = B["xc"], B["sq"], B["srow"], B["bc"]
    X = lambda ft: (tg + ".x%d" % ft)
    for ft in range(FT):
        k.dma("sp", xc[:, ft * 512:(ft + 1) * 512],
              rsrc[ft * 128:(ft + 1) * 128, ch * 512:(ch + 1) * 512], wr=[X(ft)])
    if pre_cb is not None:
        pre_cb(ch)
    for ft in range(FT):
        xa = xc[:, ft * 512:(ft + 1) * 512]
        k.op("act", lambda e, a=xa: e.activation(sq[:, :], a, AF.Square),
             rd=[X(ft)], wr=[tg + ".sq"])
        k.mm(B["ps_s"][0:1, :], P["onesc"][:, 0:1], xa, ft == 0, ft == FT - 1,
             rd=[X(ft), "onesc"], wr=[tg + ".pss"] if ft == FT - 1 else [],
             war=[tg + ".pss"] if ft == 0 else [], pub=True)
        k.mm(B["ps_q"][0:1, :], P["onesc"][:, 0:1], sq[:, :], ft == 0, ft == FT - 1,
             rd=[tg + ".sq", "onesc"], wr=[tg + ".psq"] if ft == FT - 1 else [],
             war=[tg + ".psq"] if ft == 0 else [], pub=True)
    k.op("dve", lambda e: e.tensor_scalar_mul(srow[0:1, 0:512], B["ps_s"][0:1, :], 1.0 / C),
         rd=[tg + ".pss"], wr=[tg + ".m"])
    k.op("dve", lambda e: e.tensor_scalar_mul(srow[0:1, 512:1024], B["ps_q"][0:1, :], 1.0 / C),
         rd=[tg + ".psq"], wr=[tg + ".q"])
    k.op("dve", lambda e: e.tensor_mul(srow[0:1, 1024:1536], srow[0:1, 0:512], srow[0:1, 0:512]),
         rd=[tg + ".m"], wr=[tg + ".v"])
    k.op("dve", lambda e: e.tensor_sub(srow[0:1, 1024:1536], srow[0:1, 512:1024], srow[0:1, 1024:1536]),
         rd=[tg + ".q", tg + ".v"], wr=[tg + ".v"])
    k.op("dve", lambda e: e.tensor_scalar_add(srow[0:1, 1024:1536], srow[0:1, 1024:1536], EPS),
         rd=[tg + ".v"], wr=[tg + ".v"])
    k.op("act", lambda e: e.activation(srow[0:1, 1024:1536], srow[0:1, 1024:1536], AF.Ln),
         rd=[tg + ".v"], wr=[tg + ".v"])
    k.op("act", lambda e: e.activation(srow[0:1, 512:1024], srow[0:1, 1024:1536], AF.Exp, scale=-0.5),
         rd=[tg + ".v"], wr=[tg + ".q"])
    k.op("dve", lambda e: e.tensor_mul(srow[0:1, 0:512], srow[0:1, 0:512], srow[0:1, 512:1024]),
         rd=[tg + ".q", tg + ".m"], wr=[tg + ".m"])
    k.mm(B["ps_b"][:, 0:512], P["onesr"][0:1, :], srow[0:1, 512:1024], True, True,
         rd=[tg + ".m", tg + ".q", "onesr"], war=[tg + ".psb"], pub=True)
    k.mm(B["ps_b"][:, 512:1024], P["onesr"][0:1, :], srow[0:1, 0:512], True, True,
         rd=[tg + ".m", tg + ".q", "onesr"], wr=[tg + ".psb"], pub=True)
    k.op("act", lambda e: e.copy(bc[:, :], B["ps_b"][:, :]),
         rd=[tg + ".psb"], wr=[tg + ".bc"])
    for ft in range(FT):
        xa = xc[:, ft * 512:(ft + 1) * 512]
        k.op("dve", lambda e, a=xa: e.tensor_mul(a, a, bc[:, 0:512]),
             rd=[X(ft), tg + ".bc"], wr=[X(ft)])
        k.op("dve", lambda e, a=xa: e.tensor_sub(a, a, bc[:, 512:1024]),
             rd=[tg + ".bc"], wr=[X(ft)])
        if gain is not None:
            k.op("dve", lambda e, a=xa, ft=ft: e.tensor_scalar(
                a, a, gain[:, ft:ft + 1], bias[:, ft:ft + 1], ALU.mult, ALU.add),
                rd=["acols"], wr=[X(ft)])
        out_cb(ft, xa)



def build_program(upto="full"):
    nc = bass.Bass("TRN2", target_bir_lowering=False, debug=False)
    dt = nc.dram_tensor

    xfm = dt("xfm", [C, T], F32, kind="ExternalInput").ap()
    xprev = dt("xprev", [128, FT], F32, kind="ExternalInput").ap()
    a_qw = dt("a_qw", [128, FT * C], F32, kind="ExternalInput").ap()
    a_kw = dt("a_kw", [128, FT * C], F32, kind="ExternalInput").ap()
    a_vw = dt("a_vw", [128, FT * C], F32, kind="ExternalInput").ap()
    a_ow = dt("a_ow", [128, FT * C], F32, kind="ExternalInput").ap()
    a_maa_w1 = dt("a_maa_w1", [128, FT * 128], F32, kind="ExternalInput").ap()
    a_maa_w2 = dt("a_maa_w2", [4, 32, C], F32, kind="ExternalInput").ap()
    a_td_w1 = dt("a_td_w1", [128, FT * 64], F32, kind="ExternalInput").ap()
    a_td_w2 = dt("a_td_w2", [64, C], F32, kind="ExternalInput").ap()
    a_cols = dt("a_cols", [128, FT * 12], F32, kind="ExternalInput").ap()
    a_vb_row = dt("a_vb_row", [1, C], F32, kind="ExternalInput").ap()
    b_wqT = dt("b_wqT", [128, FT * C], F32, kind="ExternalInput").ap()
    b_wkT = dt("b_wkT", [128, FT * C], F32, kind="ExternalInput").ap()
    b_wvT = dt("b_wvT", [128, FT * C], F32, kind="ExternalInput").ap()
    b_woT = dt("b_woT", [128, FT * C], F32, kind="ExternalInput").ap()
    b_sw1T = dt("b_sw1T", [128, 128], F32, kind="ExternalInput").ap()
    b_sw2c = dt("b_sw2c", [128, 1], F32, kind="ExternalInput").ap()
    b_sb1c = dt("b_sb1c", [128, 1], F32, kind="ExternalInput").ap()
    b_sb2 = dt("b_sb2", [1, 1], F32, kind="ExternalInput").ap()
    f_wg = dt("f_wg", [NIT, 128, FT * 128], I8, kind="ExternalInput").ap()
    f_wu = dt("f_wu", [NIT, 128, FT * 128], I8, kind="ExternalInput").ap()
    f_wd = dt("f_wd", [FT, 128, NIT * 128], I8, kind="ExternalInput").ap()
    f_sg = dt("f_sg", [128, NIT], F32, kind="ExternalInput").ap()
    f_su = dt("f_su", [128, NIT], F32, kind="ExternalInput").ap()
    f_sd = dt("f_sd", [128, FT], F32, kind="ExternalInput").ap()
    c_eye = dt("c_eye", [128, 128], F32, kind="ExternalInput").ap()
    c_maskD = dt("c_maskD", [128, 64], F32, kind="ExternalInput").ap()
    c_maskS = dt("c_maskS", [128, 64], F32, kind="ExternalInput").ap()
    c_J = dt("c_J", [128, 128], F32, kind="ExternalInput").ap()
    c_Lmask = dt("c_Lmask", [128, 128], F32, kind="ExternalInput").ap()
    c_Hmask = dt("c_Hmask", [128, 128], F32, kind="ExternalInput").ap()
    c_Hsel = dt("c_Hsel", [16, 128], F32, kind="ExternalInput").ap()
    c_iota = dt("c_iota", [128, 512], F32, kind="ExternalInput").ap()
    c_perm = dt("c_perm", [128, 1], U32, kind="ExternalInput").ap()
    c_par = dt("c_par", [128, 1], F32, kind="ExternalInput").ap()

    yout = dt("yfm", [C, T], F32, kind="ExternalOutput").ap()

    ln0 = dt("s_ln0", [C, T], F32).ap()
    xmix = dt("s_xmix", [4, C, T], BF16).ap()
    qa = dt("s_qa", [C, T], BF16).ap()
    kga = dt("s_kga", [C, T], BF16).ap()
    wlog = dt("s_wlog", [C, T], F32).ap()
    agate = dt("s_agate", [C, T], BF16).ap()
    va = dt("s_va", [NPAIR, T, 128], BF16).ap()
    qqg = dt("s_qqg", [C, T], BF16).ap()
    ya = dt("s_ya", [C, T], F32).ap()
    r1 = dt("s_r1", [C, T], F32).ap()
    r2 = dt("s_r2", [C, T], F32).ap()
    r3 = dt("s_r3", [C, T], F32).ap()
    fuse = dt("s_fuse", [INTER, T], BF16).ap()
    qb = dt("s_qb", [C, T], BF16).ap()
    kbt = dt("s_kbt", [HB, T, 128], BF16).ap()
    vbt = dt("s_vbt", [HB, T, 128], BF16).ap()
    dbg = dt("s_dbg", [C, T], F32).ap()
    sfin_t = dt("s_sfin", [NPAIR * 128, 128], F32)
    sfin_cc_t = dt("s_sfin_cc", [2 * NPAIR * 128, 128], F32)
    scl_t = dt("s_scl", [HB, T], F32)
    scl_cc_t = dt("s_scl_cc", [2 * HB, T], F32)
    kvsel_t = dt("s_kvsel", [HB * 256, 128], F32)
    kvsel_cc_t = dt("s_kvsel_cc", [HB * 256, 128], F32)

    D = dict(xfm=xfm, xprev=xprev, a_qw=a_qw, a_kw=a_kw, a_vw=a_vw, a_ow=a_ow,
             a_maa_w1=a_maa_w1, a_maa_w2=a_maa_w2, a_td_w1=a_td_w1, a_td_w2=a_td_w2,
             a_cols=a_cols, a_vb_row=a_vb_row, b_wqT=b_wqT, b_wkT=b_wkT, b_wvT=b_wvT,
             b_woT=b_woT, b_sw1T=b_sw1T, b_sw2c=b_sw2c, b_sb1c=b_sb1c, b_sb2=b_sb2,
             f_wg=f_wg, f_wu=f_wu, f_wd=f_wd, f_sg=f_sg, f_su=f_su, f_sd=f_sd,
             c_eye=c_eye, c_maskD=c_maskD, c_maskS=c_maskS, c_J=c_J, c_Lmask=c_Lmask,
             c_Hmask=c_Hmask, c_Hsel=c_Hsel, c_iota=c_iota, c_perm=c_perm, c_par=c_par,
             yout=yout, ln0=ln0, xmix=xmix, qa=qa, kga=kga, wlog=wlog, agate=agate,
             va=va, qqg=qqg, ya=ya, r1=r1, r2=r2, r3=r3, fuse=fuse, qb=qb, kbt=kbt,
             vbt=vbt, dbg=dbg, sfin_t=sfin_t, sfin_cc_t=sfin_cc_t, scl_t=scl_t,
             scl_cc_t=scl_cc_t, kvsel_t=kvsel_t, kvsel_cc_t=kvsel_cc_t)

    k = K(nc)
    with contextlib.ExitStack() as st:
        for s in SEMS:
            k.sems[s] = st.enter_context(nc.semaphore(s))
        block = st.enter_context(nc.Block())

        P = {}

        def ap(name, shape, dtyp):
            P[name] = st.enter_context(nc.sbuf_tensor(name, shape, dtyp))
            return P[name]

        for name, shape, dtyp in [
                ("ones1", [1, 1], F32), ("onesr", [1, 128], F32), ("onesc", [128, 1], F32),
                ("acols", [128, FT * 12], F32), ("eyeb", [128, 128], BF16),
                ("eyef", [128, 128], F32), ("maskD", [128, 64], F32),
                ("maskS", [128, 64], F32), ("Jb", [128, 128], F32),
                ("Lm", [128, 128], F32), ("Hm", [128, 128], F32),
                ("Hs", [16, 128], F32), ("iotg", [128, 512], F32),
                ("permt", [128, 1], U32), ("parc", [128, 1], F32),
                ("sgc", [128, NIT], F32), ("suc", [128, NIT], F32),
                ("sdc", [128, FT], F32), ("sw1b", [128, 128], BF16),
                ("sw2b", [128, 1], BF16), ("sb1c", [128, 1], F32),
                ("sb2c", [1, 1], F32), ("xprevt", [128, FT], F32),
                ("lnprev", [128, FT], F32), ("vbias", [128, C], F32)]:
            ap(name, shape, dtyp)

        for name, src in [("acols", a_cols), ("Jb", c_J), ("Lm", c_Lmask),
                          ("Hm", c_Hmask), ("iotg", c_iota), ("maskD", c_maskD),
                          ("maskS", c_maskS), ("sgc", f_sg), ("suc", f_su),
                          ("sdc", f_sd), ("sb1c", b_sb1c), ("sb2c", b_sb2),
                          ("xprevt", xprev), ("eyef", c_eye), ("parc", c_par),
                          ("Hs", c_Hsel), ("permt", c_perm)]:
            k.dma("sp", P[name][:, :], src[:, :], wr=[name])
        k.dma("pool", P["eyeb"][:, :], c_eye[:, :], wr=["eyeb"])
        k.dma("pool", P["sw1b"][:, :], b_sw1T[:, :], wr=["sw1b"])
        k.dma("pool", P["sw2b"][:, :], b_sw2c[:, :], wr=["sw2b"])
        k.op("dve", lambda e: e.memset(P["ones1"][:, :], 1.0), wr=["ones1"])
        k.op("dve", lambda e: e.memset(P["onesr"][:, :], 1.0), wr=["onesr"])
        k.op("dve", lambda e: e.memset(P["onesc"][:, :], 1.0), wr=["onesc"])
        with nc.sbuf_tensor("vb_row", [1, C], F32) as vb_row, \
                nc.psum_tensor("vb_ps", [128, 512], F32) as vb_ps:
            k.dma("sp", vb_row[0:1, :], a_vb_row[0:1, :], wr=["vbrow"])
            for chv in range(4):
                k.mm(vb_ps[:, :], P["onesr"][0:1, :], vb_row[0:1, chv * 512:(chv + 1) * 512],
                     True, True, rd=["onesr", "vbrow"], wr=["vbps"], pub=True)
                k.op("act", lambda e, chv=chv: e.copy(P["vbias"][:, chv * 512:(chv + 1) * 512],
                                                      vb_ps[:, :]),
                     rd=["vbps"], wr=["vbias"])
        k.barrier()

        _phase_ln0(k, nc, P, D)
        if upto == "ln0":
            _dbg_copy(k, ln0, yout, F32); k.finalize(); k.replay(block); return nc
        _phase_mix(k, nc, P, D)
        if upto == "mix":
            _dbg_copy(k, xmix[0], yout, BF16); k.finalize(); k.replay(block); return nc
        _phase_proj_a(k, nc, P, D)
        if upto == "proj_a":
            _dbg_copy(k, wlog, yout, F32); k.finalize(); k.replay(block); return nc
        _phase_scan(k, nc, P, D)
        if upto == "scan":
            _dbg_copy(k, dbg if globals().get("_SCAN_DBG") else ya, yout, F32); k.finalize(); k.replay(block); return nc
        _phase_attn_a_out(k, nc, P, D)
        if upto == "attn_a":
            _dbg_copy(k, r1, yout, F32); k.finalize(); k.replay(block); return nc
        _phase_ffn(k, nc, P, D, r1, r2, "F1")
        if upto == "fuse1":
            if globals().get("_FFN_DBG"):
                _dbg_copy(k, dbg, yout, F32)
            else:
                _dbg_copy(k, fuse[0:C, :], yout, BF16)
            k.finalize(); k.replay(block); return nc
        if upto == "ffn1":
            _dbg_copy(k, r2, yout, F32); k.finalize(); k.replay(block); return nc
        _phase_attn_b(k, nc, P, D, upto)
        if upto in ("xb", "scores", "sel", "ksel"):
            if not globals().get("_NO_DBGCOPY"):
                _dbg_copy(k, dbg, yout, F32)
            k.finalize(); k.replay(block); return nc
        if upto == "attn_b":
            _dbg_copy(k, r3, yout, F32); k.finalize(); k.replay(block); return nc
        _phase_ffn(k, nc, P, D, r3, yout, "F2")
        k.finalize()
        k.replay(block)
    return nc


def _dbg_copy(k, src, dstap, sdtype):
    nc = k.nc
    k.barrier()
    with nc.sbuf_tensor("dbg_t", [128, 2048], F32) as tt:
        for ft in range(FT):
            q = "pool" if sdtype != F32 else "sp"
            k.dma(q, tt[:, :], src[ft * 128:(ft + 1) * 128, 0:T], wr=["dbgt"])
            k.dma("sp", dstap[ft * 128:(ft + 1) * 128, 0:T], tt[:, :], rd=["dbgt"])
    k.barrier()


def _phase_ln0(k, nc, P, D):
    with contextlib.ExitStack() as pst:
        B = alloc_ln_bufs(nc, pst, "A")
        for ch in range(4):
            def out_cb(ft, xa, ch=ch):
                k.dma("sp", D["ln0"][ft * 128:(ft + 1) * 128, ch * 512:(ch + 1) * 512],
                      xa, rd=["A.x%d" % ft])
            ln_chunk(k, B, P, ch, D["xfm"], out_cb)
        sq, srow, bc = B["sq"], B["srow"], B["bc"]
        k.op("act", lambda e: e.activation(sq[:, 0:FT], P["xprevt"][:, :], AF.Square),
             rd=["xprevt"], wr=["A.sq"])
        k.op("dve", lambda e: e.reduce_sum(sq[:, 16:17], P["xprevt"][:, :], axis=AX.X),
             rd=["xprevt"], wr=["A.c1"])
        k.op("dve", lambda e: e.reduce_sum(sq[:, 17:18], sq[:, 0:FT], axis=AX.X),
             rd=["A.sq"], wr=["A.c2"])
        k.mm(B["ps_s"][0:1, 0:2], P["onesc"][:, 0:1], sq[:, 16:18], True, True,
             rd=["A.c1", "A.c2", "onesc"], wr=["A.pss"], pub=True)
        k.op("dve", lambda e: e.tensor_scalar_mul(srow[0:1, 0:2], B["ps_s"][0:1, 0:2], 1.0 / C),
             rd=["A.pss"], wr=["A.m"])
        k.op("dve", lambda e: e.tensor_mul(srow[0:1, 2:3], srow[0:1, 0:1], srow[0:1, 0:1]),
             rd=["A.m"], wr=["A.v"])
        k.op("dve", lambda e: e.tensor_sub(srow[0:1, 2:3], srow[0:1, 1:2], srow[0:1, 2:3]),
             rd=["A.m"], wr=["A.v"])
        k.op("dve", lambda e: e.tensor_scalar_add(srow[0:1, 2:3], srow[0:1, 2:3], EPS),
             wr=["A.v"])
        k.op("act", lambda e: e.activation(srow[0:1, 2:3], srow[0:1, 2:3], AF.Ln), wr=["A.v"])
        k.op("act", lambda e: e.activation(srow[0:1, 2:3], srow[0:1, 2:3], AF.Exp, scale=-0.5),
             wr=["A.v"])
        k.op("dve", lambda e: e.tensor_mul(srow[0:1, 3:4], srow[0:1, 0:1], srow[0:1, 2:3]),
             rd=["A.m", "A.v"], wr=["A.mr"])
        k.mm(B["ps_b"][:, 0:2], P["onesr"][0:1, :], srow[0:1, 2:4], True, True,
             rd=["A.v", "A.mr", "onesr"], wr=["A.psb"], pub=True)
        k.op("act", lambda e: e.copy(bc[:, 0:2], B["ps_b"][:, 0:2]), rd=["A.psb"], wr=["A.bc"])
        k.op("dve", lambda e: e.tensor_scalar(P["lnprev"][:, :], P["xprevt"][:, :],
                                              bc[:, 0:1], bc[:, 1:2], ALU.mult, ALU.subtract),
             rd=["A.bc", "xprevt"], wr=["lnprev"])
        k.barrier()


def _phase_mix(k, nc, P, D):
    acols = P["acols"]
    with contextlib.ExitStack() as pst:
        lx = pst.enter_context(nc.sbuf_tensor("M_lx", [128, FT * 513], F32))
        dx = pst.enter_context(nc.sbuf_tensor("M_dx", [128, FT * 512], F32))
        xxxb = pst.enter_context(nc.sbuf_tensor("M_xxx", [128, FT * 512], BF16))
        w1b = pst.enter_context(nc.sbuf_tensor("M_w1", [128, FT * 128], BF16))
        w2b = pst.enter_context(nc.sbuf_tensor("M_w2", [128, FT * 128], BF16))
        mmb = pst.enter_context(nc.sbuf_tensor("M_mm", [128, 512], BF16))
        tmp = pst.enter_context(nc.sbuf_tensor("M_tmp", [128, 512], F32))
        outt = pst.enter_context(nc.sbuf_tensor("M_out", [128, 512], BF16))
        ps_mm = pst.enter_context(nc.psum_tensor("M_psmm", [128, 512], F32))
        ps_m = pst.enter_context(nc.psum_tensor("M_psm", [128, 512], F32))

        k.dma("pool", w1b[:, :], D["a_maa_w1"][:, :], wr=["M.w1"])
        for r in range(4):
            k.dma("pool", w2b[32 * r:32 * (r + 1), :], D["a_maa_w2"][r, :, :],
                  wr=["M.w2_%d" % r])
        for ch in range(4):
            for ft in range(FT):
                base = ft * 513
                if ch == 0:
                    k.dma("sp", lx[:, base + 1:base + 513],
                          D["ln0"][ft * 128:(ft + 1) * 128, 0:512], wr=["M.lx%d" % ft])
                    k.op("dve", lambda e, b=base, ft=ft:
                         e.tensor_copy(lx[:, b:b + 1], P["lnprev"][:, ft:ft + 1]),
                         rd=["lnprev"], wr=["M.lx%d" % ft])
                else:
                    k.dma("sp", lx[:, base:base + 513],
                          D["ln0"][ft * 128:(ft + 1) * 128, ch * 512 - 1:(ch + 1) * 512],
                          wr=["M.lx%d" % ft])
            for ft in range(FT):
                base, o = ft * 513, ft * 512
                k.op("dve", lambda e, b=base, o=o:
                     e.tensor_sub(dx[:, o:o + 512], lx[:, b:b + 512], lx[:, b + 1:b + 513]),
                     rd=["M.lx%d" % ft], wr=["M.dx%d" % ft])
                k.op("dve", lambda e, b=base, o=o, ft=ft: e.scalar_tensor_tensor(
                    tmp[:, :], dx[:, o:o + 512], acols[:, ft:ft + 1], lx[:, b + 1:b + 513],
                    ALU.mult, ALU.add), rd=["M.dx%d" % ft, "M.lx%d" % ft, "acols"], wr=["M.tmp"])
                k.op("dve", lambda e, o=o: e.tensor_copy(xxxb[:, o:o + 512], tmp[:, :]),
                     rd=["M.tmp"], wr=["M.xxx%d" % ft])
            for ft in range(FT):
                k.mm(ps_mm[:, :], w1b[:, ft * 128:(ft + 1) * 128],
                     xxxb[:, ft * 512:(ft + 1) * 512], ft == 0, ft == FT - 1,
                     rd=["M.w1", "M.xxx%d" % ft],
                     wr=["M.psmm"] if ft == FT - 1 else [],
                     war=["M.psmm"] if ft == 0 else [], pub=(ft == FT - 1))
            k.op("act", lambda e: e.activation(mmb[:, :], ps_mm[:, :], AF.Tanh),
                 rd=["M.psmm"], wr=["M.mmb"])
            for r in range(4):
                for ft in range(FT):
                    k.mm(ps_m[:, :], w2b[32 * r:32 * (r + 1), ft * 128:(ft + 1) * 128],
                         mmb[32 * r:32 * (r + 1), :], True, True,
                         rd=["M.w2_%d" % r, "M.mmb"], wr=["M.psm"], pub=True,
                         tp=(32 * r, 0) if r == 3 else None)
                    k.op("dve", lambda e, r=r, ft=ft: e.tensor_scalar_add(
                        tmp[:, :], ps_m[:, :], acols[:, (1 + r) * FT + ft:(1 + r) * FT + ft + 1]),
                        rd=["M.psm", "acols"], wr=["M.tmp"])
                    k.op("dve", lambda e, o=ft * 512: e.tensor_mul(tmp[:, :], tmp[:, :], dx[:, o:o + 512]),
                         rd=["M.dx%d" % ft], wr=["M.tmp"])
                    k.op("dve", lambda e, b=ft * 513: e.tensor_add(
                        outt[:, :], tmp[:, :], lx[:, b + 1:b + 513]),
                        rd=["M.tmp", "M.lx%d" % ft], wr=["M.out"])
                    k.dma("sp", D["xmix"][r, ft * 128:(ft + 1) * 128, ch * 512:(ch + 1) * 512],
                          outt[:, :], rd=["M.out"])
        k.barrier()


def _phase_proj_a(k, nc, P, D):
    acols = P["acols"]
    with contextlib.ExitStack() as pst:
        xs = pst.enter_context(nc.sbuf_tensor("P_x", [128, FT * T], BF16))
        wb = pst.enter_context(nc.sbuf_tensor("P_w", [128, FT * C], BF16))
        ob = pst.enter_context(nc.sbuf_tensor("P_ob", [128, 512], BF16))
        o2 = pst.enter_context(nc.sbuf_tensor("P_o2", [128, 512], BF16))
        of = pst.enter_context(nc.sbuf_tensor("P_of", [128, 512], F32))
        gt = pst.enter_context(nc.sbuf_tensor("P_g", [128, 512], BF16))
        gf = pst.enter_context(nc.sbuf_tensor("P_gf", [128, 512], F32))
        td1b = pst.enter_context(nc.sbuf_tensor("P_td1", [128, FT * 64], BF16))
        td2b = pst.enter_context(nc.sbuf_tensor("P_td2", [64, C], BF16))
        tdb = pst.enter_context(nc.sbuf_tensor("P_tdb", [64, 512], BF16))
        ps = [pst.enter_context(nc.psum_tensor(f"P_ps{i}", [128, 512], F32)) for i in range(2)]
        ps_td = pst.enter_context(nc.psum_tensor("P_pstd", [64, 512], F32))

        def load_plane(r):
            for ft in range(FT):
                k.dma("pool", xs[:, ft * T:(ft + 1) * T],
                      D["xmix"][r, ft * 128:(ft + 1) * 128, :], wr=["P.x%d" % ft])

        load_plane(3)
        k.dma("pool", td1b[:, :], D["a_td_w1"][:, :], wr=["P.td1"])
        k.dma("pool", td2b[:, :], D["a_td_w2"][:, :], wr=["P.td2"])
        for ch in range(4):
            for ft in range(FT):
                k.mm(ps_td[:, :], td1b[:, ft * 64:(ft + 1) * 64],
                     xs[:, ft * T + ch * 512:ft * T + (ch + 1) * 512], ft == 0, ft == FT - 1,
                     rd=["P.td1", "P.x%d" % ft], wr=["P.pstd"] if ft == FT - 1 else [],
                     war=["P.pstd"] if ft == 0 else [], pub=(ft == FT - 1))
            k.op("act", lambda e: e.activation(tdb[:, :], ps_td[:, :], AF.Tanh),
                 rd=["P.pstd"], wr=["P.tdb"])
            for ft2 in range(FT):
                b = ft2 % 2
                k.mm(ps[b][:, :], td2b[0:64, ft2 * 128:(ft2 + 1) * 128], tdb[:, :], True, True,
                     rd=["P.td2", "P.tdb"], wr=["P.ps%d" % b], pub=True)
                k.op("act", lambda e, b=b, ft2=ft2: e.activation(
                    of[:, :], ps[b][:, :], AF.Exp,
                    bias=acols[:, 8 * FT + ft2:8 * FT + ft2 + 1]),
                    rd=["P.ps%d" % b, "acols"], wr=["P.of"])
                k.op("dve", lambda e: e.tensor_scalar(of[:, :], of[:, :], -1.0, -5.0,
                                                      ALU.mult, ALU.max), wr=["P.of"])
                k.dma("sp", D["wlog"][ft2 * 128:(ft2 + 1) * 128, ch * 512:(ch + 1) * 512],
                      of[:, :], rd=["P.of"])
                k.op("act", lambda e: e.activation(gf[:, :], of[:, :], AF.Exp),
                     rd=["P.of"], wr=["P.gf"])
                k.op("dve", lambda e: e.tensor_scalar(gt[:, :], gf[:, :], -1.0, 1.0,
                                                      ALU.mult, ALU.add),
                     rd=["P.gf"], wr=["P.gt"])
                k.dma("sp", D["agate"][ft2 * 128:(ft2 + 1) * 128, ch * 512:(ch + 1) * 512],
                      gt[:, :], rd=["P.gt"])
        k.barrier()

        load_plane(0)
        k.dma("pool", wb[:, :], D["a_qw"][:, :], wr=["P.wb"])
        for ft2 in range(FT):
            for ch in range(4):
                b = (ft2 * 4 + ch) % 2
                for ft in range(FT):
                    k.mm(ps[b][:, :], wb[:, ft * C + ft2 * 128:ft * C + (ft2 + 1) * 128],
                         xs[:, ft * T + ch * 512:ft * T + (ch + 1) * 512],
                         ft == 0, ft == FT - 1, rd=["P.wb", "P.x%d" % ft],
                         wr=["P.ps%d" % b] if ft == FT - 1 else [],
                         war=["P.ps%d" % b] if ft == 0 else [], pub=(ft == FT - 1))
                k.op("act", lambda e, b=b, ft2=ft2: e.activation(
                    ob[:, :], ps[b][:, :], AF.Identity,
                    bias=acols[:, 5 * FT + ft2:5 * FT + ft2 + 1]),
                    rd=["P.ps%d" % b, "acols"], wr=["P.obf"])
                k.dma("sp", D["qa"][ft2 * 128:(ft2 + 1) * 128, ch * 512:(ch + 1) * 512],
                      ob[:, :], rd=["P.obf"])
        k.barrier()

        load_plane(1)
        k.dma("pool", wb[:, :], D["a_kw"][:, :], wr=["P.wb"])
        for ft2 in range(FT):
            for ch in range(4):
                b = (ft2 * 4 + ch) % 2
                for ft in range(FT):
                    k.mm(ps[b][:, :], wb[:, ft * C + ft2 * 128:ft * C + (ft2 + 1) * 128],
                         xs[:, ft * T + ch * 512:ft * T + (ch + 1) * 512],
                         ft == 0, ft == FT - 1, rd=["P.wb", "P.x%d" % ft],
                         wr=["P.ps%d" % b] if ft == FT - 1 else [],
                         war=["P.ps%d" % b] if ft == 0 else [], pub=(ft == FT - 1))
                k.op("act", lambda e, b=b, ft2=ft2: e.activation(
                    of[:, :], ps[b][:, :], AF.Identity,
                    bias=acols[:, 6 * FT + ft2:6 * FT + ft2 + 1]),
                    rd=["P.ps%d" % b, "acols"], wr=["P.of"])
                k.dma("pool", gt[:, :],
                      D["agate"][ft2 * 128:(ft2 + 1) * 128, ch * 512:(ch + 1) * 512],
                      wr=["P.gt"])
                k.op("dve", lambda e: e.tensor_mul(ob[:, :], of[:, :], gt[:, :]),
                     rd=["P.of", "P.gt"], wr=["P.obf"])
                k.dma("sp", D["kga"][ft2 * 128:(ft2 + 1) * 128, ch * 512:(ch + 1) * 512],
                      ob[:, :], rd=["P.obf"])
        k.barrier()

        load_plane(2)
        k.dma("pool", wb[:, :], D["a_vw"][:, :], wr=["P.wb"])
        for tt in range(FT):
            for dv in range(4):
                b = (tt * 4 + dv) % 2
                for ft in range(FT):
                    k.mm(ps[b][:, :], xs[:, ft * T + tt * 128:ft * T + (tt + 1) * 128],
                         wb[:, ft * C + dv * 512:ft * C + (dv + 1) * 512],
                         ft == 0, ft == FT - 1, rd=["P.wb", "P.x%d" % ft],
                         wr=["P.ps%d" % b] if ft == FT - 1 else [],
                         war=["P.ps%d" % b] if ft == 0 else [], pub=(ft == FT - 1))
                k.op("dve", lambda e, b=b, dv=dv: e.tensor_add(
                    of[:, :], ps[b][:, :], P["vbias"][:, dv * 512:(dv + 1) * 512]),
                    rd=["P.ps%d" % b, "vbias"], wr=["P.of"])
                k.op("dve", lambda e: e.tensor_copy(o2[:, :], of[:, :]),
                     rd=["P.of"], wr=["P.o2"])
                for sl in range(4):
                    p = dv * 4 + sl
                    k.dma("sp", D["va"][p, tt * 128:(tt + 1) * 128, :],
                          o2[:, sl * 128:(sl + 1) * 128], rd=["P.o2"])
        k.barrier()


def _phase_scan(k, nc, P, D):
    sfin = D["sfin_t"].ap()
    with contextlib.ExitStack() as pst:
        def sb(name, shape, dtyp):
            return pst.enter_context(nc.sbuf_tensor(name, shape, dtyp))
        qs = sb("S_q", [128, T], BF16)
        ks = sb("S_k", [128, T], BF16)
        ws = sb("S_w", [128, T], F32)
        Bg = sb("S_Bg", [128, T], F32)
        zs = sb("S_z", [128, T], F32)
        ys = sb("S_y", [128, T], F32)
        qq = sb("S_qq", [128, T], BF16)
        eBg = sb("S_eBg", [128, T], F32)
        vbd = sb("S_vbd", [128, NCH * 128], BF16)
        vsl = sb("S_vsl", [64, NCH * 128], BF16)
        argQ = sb("S_aq", [128, 64], F32)
        argH = sb("S_ah", [128, 64], F32)
        argM = sb("S_am", [128, 64], F32)
        argG = sb("S_ag", [128, 129], F32)
        eQ = sb("S_eq", [128, 64], F32)
        eK = sb("S_ek", [128, 64], F32)
        eH = sb("S_eh", [128, 64], F32)
        eM = sb("S_em", [128, 64], F32)
        eG = sb("S_eg", [128, 129], F32)
        qcat = sb("S_qcat", [128, 64], BF16)
        lhsKT = sb("S_lkt", [128, 128], BF16)
        lhsKH = sb("S_lkh", [128, 128], BF16)
        lhsKG = sb("S_lkg", [128, 128], BF16)
        qgz = sb("S_qgz", [128, 64], BF16)
        qglob = sb("S_qgl", [128, 64], BF16)
        kglob = sb("S_kgl", [128, 64], BF16)
        khT = sb("S_khT", [64, 128], BF16)
        ptf = sb("S_ptf", [128, 64], F32)
        ptf2 = sb("S_ptf2", [128, 64], F32)
        PTb = sb("S_PTb", [128, 64], BF16)
        Sf = sb("S_Sf", [128, 128], F32)
        Sbd = sb("S_Sbd", [128, 128], BF16)
        psD = pst.enter_context(nc.psum_tensor("S_psD", [128, 64], F32))
        psS2 = pst.enter_context(nc.psum_tensor("S_psS", [128, 64], F32))
        psQ = pst.enter_context(nc.psum_tensor("S_psQ", [128, 64], F32))
        psO = pst.enter_context(nc.psum_tensor("S_psO", [128, 64], F32))
        psT = pst.enter_context(nc.psum_tensor("S_psT", [64, 128], BF16))
        psS = pst.enter_context(nc.psum_tensor("S_psSt", [128, 128], F32))

        for t_ in (lhsKT, lhsKH, lhsKG, qgz, Sbd):
            k.op("dve", lambda e, t=t_: e.memset(t[:, :], 0.0),
                 wr=["S.init"])
        k.op("dve", lambda e: e.memset(
            vbd[0:64, :].rearrange("s (c d) -> s c d", d=128)[:, :, 64:128], 0.0),
            wr=["S.init"])
        k.op("dve", lambda e: e.memset(
            vbd[64:128, :].rearrange("s (c d) -> s c d", d=128)[:, :, 0:64], 0.0),
            wr=["S.init"])
        k.op("dve", lambda e: e.memset(zs[:, :], 0.0), wr=["S.z"])
        k.barrier()

        zcol = zs[:, 0:1]
        for p in range(NPAIR):
            r0 = p * 128
            k.dma("pool", qs[:, :], D["qa"][r0:r0 + 128, :], wr=["S.qs"])
            k.dma("pool", ks[:, :], D["kga"][r0:r0 + 128, :], wr=["S.ks"])
            k.dma("sp", ws[:, :], D["wlog"][r0:r0 + 128, :], wr=["S.ws"])
            k.dma("sp", vbd[0:64, :].rearrange("s (c d) -> s c d", d=128)[:, :, 0:64],
                  D["va"][p].rearrange("(c s) d -> s c d", s=64, c=32)[:, :, 0:64], wr=["S.vbdA"])
            k.dma("sp", vbd[64:128, :].rearrange("s (c d) -> s c d", d=128)[:, :, 64:128],
                  D["va"][p].rearrange("(c s) d -> s c d", s=64, c=32)[:, :, 64:128], wr=["S.vbdB"])
            k.dma("sp", vsl[0:64, :].rearrange("s (c d) -> s c d", d=128),
                  D["va"][p].rearrange("(c s) d -> s c d", s=64, c=32), wr=["S.vsl"])
            k.op("dve", lambda e: e.tensor_tensor_scan(
                Bg[:, :], ws[:, :], zs[:, :], 0.0, ALU.add, ALU.add),
                rd=["S.ws", "S.z"], wr=["S.Bg"])
            k.op("act", lambda e: e.activation(eBg[:, :], Bg[:, :], AF.Exp),
                 rd=["S.Bg"], wr=["S.eBg"])
            k.op("dve", lambda e: e.scalar_tensor_tensor(
                qq[:, :], qs[:, :], SCL_A, eBg[:, :], ALU.mult, ALU.mult),
                rd=["S.eBg", "S.qs"], wr=["S.qqb"])
            k.dma("sp", D["qqg"][r0:r0 + 128, :], qq[:, :], rd=["S.qqb"])
            k.op("dve", lambda e: e.memset(Sf[:, :], 0.0), wr=["S.Sf"])
            k.op("dve", lambda e: e.memset(Sbd[:, :], 0.0), wr=["S.Sbd"])

            for c in range(NCH):
                c0 = c * LCH
                scol = Bg[:, c0 - 1:c0] if c > 0 else zcol
                mcol = Bg[:, c0 + 31:c0 + 32]
                ecol = Bg[:, c0 + 63:c0 + 64]
                for i in range(4):
                    bcol = Bg[:, c0 + 16 * i - 1:c0 + 16 * i] if (c0 + 16 * i) > 0 else zcol
                    ecj = Bg[:, c0 + 16 * (i + 1) - 1:c0 + 16 * (i + 1)]
                    k.op("pool", lambda e, i=i, bcol=bcol, c0=c0: e.tensor_scalar_sub(
                        argQ[:, 16 * i:16 * (i + 1)], Bg[:, c0 + 16 * i:c0 + 16 * (i + 1)], bcol),
                        rd=["S.Bg", "S.z"], wr=["S.argQ"] if i == 3 else ["S.argQp"])
                    k.op("pool", lambda e, i=i, ecj=ecj, c0=c0: e.tensor_scalar_sub(
                        argH[:, 16 * i:16 * (i + 1)], Bg[:, c0 + 16 * i:c0 + 16 * (i + 1)], ecj),
                        rd=["S.Bg"], wr=["S.argH"] if i == 3 else ["S.argHp"])
                k.op("pool", lambda e, c0=c0, mcol=mcol: e.tensor_scalar_sub(
                    argM[:, :], Bg[:, c0:c0 + 64], mcol), rd=["S.Bg"], wr=["S.argM"])
                k.op("pool", lambda e: e.tensor_scalar_mul(argM[:, 0:32], argM[:, 0:32], -1.0),
                     wr=["S.argM"])
                k.op("pool", lambda e, c0=c0, scol=scol: e.tensor_scalar_sub(
                    argG[:, 0:64], Bg[:, c0:c0 + 64], scol), rd=["S.Bg", "S.z"], wr=["S.argG1"])
                k.op("pool", lambda e, c0=c0, ecol=ecol: e.tensor_scalar_sub(
                    argG[:, 64:128], Bg[:, c0:c0 + 64], ecol), rd=["S.Bg"], wr=["S.argG2"])
                k.op("pool", lambda e: e.tensor_scalar_mul(argG[:, 64:128], argG[:, 64:128], -1.0),
                     wr=["S.argG2"])
                k.op("pool", lambda e, scol=scol, ecol=ecol: e.tensor_scalar_sub(
                    argG[:, 128:129], ecol, scol), rd=["S.Bg", "S.z"], wr=["S.argG3"])
                k.op("act", lambda e: e.activation(eQ[:, :], argQ[:, :], AF.Exp),
                     rd=["S.argQ", "S.argQp"], wr=["S.eQ"])
                k.op("act", lambda e: e.activation(eK[:, :], argQ[:, :], AF.Exp, scale=-1.0),
                     rd=["S.argQ", "S.argQp"], wr=["S.eK"])
                k.op("act", lambda e: e.activation(eH[:, :], argH[:, :], AF.Exp, scale=-1.0),
                     rd=["S.argH", "S.argHp"], wr=["S.eH"])
                k.op("act", lambda e: e.activation(eM[:, :], argM[:, :], AF.Exp),
                     rd=["S.argM"], wr=["S.eM"])
                k.op("act", lambda e: e.activation(eG[:, :], argG[:, :], AF.Exp),
                     rd=["S.argG1", "S.argG2", "S.argG3"], wr=["S.eG"])
                k.op("dve", lambda e, c0=c0: e.tensor_mul(qcat[:, :], qs[:, c0:c0 + 64], eQ[:, :]),
                     rd=["S.eQ", "S.qs"], wr=["S.qcat"])
                k.op("dve", lambda e, c0=c0: e.tensor_mul(
                    lhsKT[0:64, 0:64], ks[0:64, c0:c0 + 64], eK[0:64, :]),
                    rd=["S.eK", "S.ks"], wr=["S.lKT"])
                k.op("dve", lambda e, c0=c0: e.tensor_mul(
                    lhsKT[64:128, 64:128], ks[64:128, c0:c0 + 64], eK[64:128, :]),
                    rd=["S.eK", "S.ks"], wr=["S.lKT"])
                k.op("dve", lambda e, c0=c0: e.tensor_mul(
                    lhsKH[0:64, 0:64], ks[0:64, c0:c0 + 64], eH[0:64, :]),
                    rd=["S.eH", "S.ks"], wr=["S.lKH"])
                k.op("dve", lambda e, c0=c0: e.tensor_mul(
                    lhsKH[64:128, 64:128], ks[64:128, c0:c0 + 64], eH[64:128, :]),
                    rd=["S.eH", "S.ks"], wr=["S.lKH"])
                k.op("dve", lambda e, c0=c0: e.tensor_mul(
                    lhsKG[0:64, 0:32], ks[0:64, c0:c0 + 32], eM[0:64, 0:32]),
                    rd=["S.eM", "S.ks"], wr=["S.lKG"])
                k.op("dve", lambda e, c0=c0: e.tensor_mul(
                    lhsKG[64:128, 64:96], ks[64:128, c0:c0 + 32], eM[64:128, 0:32]),
                    rd=["S.eM", "S.ks"], wr=["S.lKG"])
                k.op("dve", lambda e, c0=c0: e.tensor_mul(
                    qgz[:, 32:64], qs[:, c0 + 32:c0 + 64], eM[:, 32:64]),
                    rd=["S.eM", "S.qs"], wr=["S.qgz"])
                k.op("dve", lambda e, c0=c0: e.tensor_mul(
                    qglob[:, :], qs[:, c0:c0 + 64], eG[:, 0:64]),
                    rd=["S.eG", "S.qs"], wr=["S.qglob"])
                k.op("dve", lambda e, c0=c0: e.tensor_mul(
                    kglob[:, :], ks[:, c0:c0 + 64], eG[:, 64:128]),
                    rd=["S.eG", "S.ks"], wr=["S.kglob"])
                k.mm(psD[:, :], lhsKT[:, :], qcat[:, :], True, True,
                     rd=["S.lKT", "S.qcat"], wr=["S.psD"], pub=True)
                k.mm(psS2[:, :], lhsKH[:, :], qcat[:, :], True, True,
                     rd=["S.lKH", "S.qcat"], wr=["S.psS2"], pub=True)
                k.mm(psQ[:, :], lhsKG[:, :], qgz[:, :], True, True,
                     rd=["S.lKG", "S.qgz"], wr=["S.psQ"], pub=True)
                k.op("dve", lambda e: e.tensor_mul(ptf[:, :], psD[:, :], P["maskD"][:, :]),
                     rd=["S.psD", "maskD"], wr=["S.ptf"])
                k.op("dve", lambda e: e.tensor_mul(ptf2[:, :], psS2[:, :], P["maskS"][:, :]),
                     rd=["S.psS2", "maskS"], wr=["S.ptf2"])
                k.op("dve", lambda e: e.tensor_add(ptf[:, :], ptf[:, :], ptf2[:, :]),
                     rd=["S.ptf2"], wr=["S.ptf"])
                k.op("dve", lambda e: e.tensor_add(PTb[:, :], ptf[:, :], psQ[:, :]),
                     rd=["S.ptf", "S.psQ"], wr=["S.PTb"])
                k.mm(psO[:, :], Sbd[:, :], qglob[:, :], True, False,
                     rd=["S.Sbd", "S.qglob"], wr=[], pub=False)
                k.mm(psO[:, :], vbd[:, c * 128:(c + 1) * 128], PTb[:, :], False, True,
                     rd=["S.vbdA", "S.vbdB", "S.PTb"], wr=["S.psO"], pub=True)
                k.op("act", lambda e, c0=c0: e.activation(
                    ys[:, c0:c0 + 64], psO[:, :], AF.Copy, scale=SCL_A),
                    rd=["S.psO"], wr=["S.ys"])
                k.op("pe", lambda e: e.transpose(psT[:, :], kglob[:, :], P["eyeb"][:, :]),
                     rd=["S.kglob", "eyeb"], wr=["S.psT"], pub=True)
                k.op("act", lambda e: e.copy(khT[:, :], psT[:, :]),
                     rd=["S.psT"], wr=["S.khT"])
                k.mm(psS[:, :], khT[0:64, :], vsl[0:64, c * 128:(c + 1) * 128], True, True,
                     rd=["S.khT", "S.vsl"], wr=["S.psS"], pub=True)
                k.op("dve", lambda e: e.tensor_scalar_mul(Sf[:, :], Sf[:, :], eG[:, 128:129]),
                     rd=["S.eG"], wr=["S.Sf"])
                k.op("dve", lambda e: e.tensor_add(Sf[0:64, 0:64], Sf[0:64, 0:64], psS[0:64, 0:64]),
                     rd=["S.psS"], wr=["S.Sf"])
                k.op("dve", lambda e: e.tensor_add(Sf[64:128, 64:128], Sf[64:128, 64:128],
                                                   psS[64:128, 64:128]),
                     rd=["S.psS"], wr=["S.Sf"])
                k.op("dve", lambda e: e.tensor_copy(Sbd[0:64, 0:64], Sf[0:64, 0:64]),
                     rd=["S.Sf"], wr=["S.Sbd"])
                k.op("dve", lambda e: e.tensor_copy(Sbd[64:128, 64:128], Sf[64:128, 64:128]),
                     rd=["S.Sf"], wr=["S.Sbd"])
                if globals().get("_SCAN_DBG") and p == 0 and c in (0, 1):
                    dbg = D["dbg"]
                    k.dma("sp", dbg[0:128, 256 * c:256 * c + 64], Bg[:, c * 64:(c + 1) * 64],
                          rd=["S.Bg"])
                    k.dma("sp", dbg[0:128, 256 * c + 64:256 * c + 128], ys[:, c * 64:(c + 1) * 64],
                          rd=["S.ys"])
                    k.dma("pool", dbg[128:256, 256 * c:256 * c + 64], PTb[:, :], rd=["S.PTb"])
                    k.dma("pool", dbg[128:256, 256 * c + 64:256 * c + 128], qglob[:, :],
                          rd=["S.qglob"])
                    k.dma("sp", dbg[256:384, 128 * c:128 * (c + 1)], Sf[:, :], rd=["S.Sf"])
                    k.dma("pool", dbg[384:512, 256 * c:256 * c + 64], kglob[:, :], rd=["S.kglob"])
                    k.dma("pool", dbg[384:448, 256 * c + 64:256 * c + 192],
                          khT[0:64, :], rd=["S.khT"])
                    k.dma("pool", dbg[512:640, 256 * c:256 * c + 64], qcat[:, :], rd=["S.qcat"])
                    k.dma("pool", dbg[512:640, 256 * c + 64:256 * c + 192], lhsKT[:, :],
                          rd=["S.lKT"])
                    k.dma("sp", dbg[640:768, 128 * c:128 * (c + 1)], eG[:, 0:128], rd=["S.eG"])
            k.dma("sp", D["ya"][r0:r0 + 128, :], ys[:, :], rd=["S.ys"])
            k.dma("sp", sfin[r0:r0 + 128, :], Sf[:, :], rd=["S.Sf"])
        k.barrier()


def _phase_attn_a_out(k, nc, P, D):
    sfin_cc = D["sfin_cc_t"].ap()
    k.emit("pool", lambda e: e.collective_compute(
        "AllGather", ALU.bypass, replica_groups=[[0, 1], [2, 3], [4, 5], [6, 7]],
        ins=[D["sfin_t"].ap().opt()], outs=[D["sfin_cc_t"].ap().opt()]),
        sem="cc0", inc=1)
    for e in ENGS:
        k._wait(e, "cc0", 1)

    with contextlib.ExitStack() as pst:
        B = alloc_ln_bufs(nc, pst, "Y")
        Sin = pst.enter_context(nc.sbuf_tensor("Y_Sin", [128, NPAIR * 128], BF16))
        Sf32 = pst.enter_context(nc.sbuf_tensor("Y_Sf32", [128, 128], F32))
        qqc = pst.enter_context(nc.sbuf_tensor("Y_qq", [128, 512], BF16))
        ynb = pst.enter_context(nc.sbuf_tensor("Y_ynb", [128, FT * 512], BF16))
        wo = pst.enter_context(nc.sbuf_tensor("Y_wo", [128, FT * C], BF16))
        rx = pst.enter_context(nc.sbuf_tensor("Y_rx", [128, 512], F32))
        ro = pst.enter_context(nc.sbuf_tensor("Y_ro", [128, 512], F32))
        psC = pst.enter_context(nc.psum_tensor("Y_psC", [128, 512], F32))
        psP = [pst.enter_context(nc.psum_tensor(f"Y_psP{i}", [128, 512], F32)) for i in range(2)]

        k.dma("pool", wo[:, :], D["a_ow"][:, :], wr=["Y.wo"])
        for p in range(NPAIR):
            k.dma("sp", Sf32[:, :], sfin_cc[p * 128:(p + 1) * 128, :], wr=["Y.Sf32"])
            k.op("dve", lambda e, p=p: e.tensor_scalar_mul(
                Sin[:, p * 128:(p + 1) * 128], Sf32[:, :], P["parc"][:, 0:1]),
                rd=["Y.Sf32", "parc"], wr=["Y.Sin"])

        for ch in range(4):
            def pre(ch=ch):
                for p in range(NPAIR):
                    k.dma("pool", qqc[:, :],
                          D["qqg"][p * 128:(p + 1) * 128, ch * 512:(ch + 1) * 512],
                          wr=["Y.qqc"])
                    k.mm(psC[:, :], Sin[:, p * 128:(p + 1) * 128], qqc[:, :], True, True,
                         rd=["Y.Sin", "Y.qqc"], wr=["Y.psC"], pub=True)
                    k.op("dve", lambda e, p=p: e.tensor_add(
                        B["xc"][:, p * 512:(p + 1) * 512], B["xc"][:, p * 512:(p + 1) * 512],
                        psC[:, :]), rd=["Y.psC"], wr=["Y.x%d" % p])

            def out_cb(ft, xa):
                k.op("dve", lambda e, a=xa, ft=ft: e.tensor_copy(
                    ynb[:, ft * 512:(ft + 1) * 512], a), rd=["Y.x%d" % ft], wr=["Y.yn%d" % ft])

            ln_chunk(k, B, P, ch, D["ya"], out_cb,
                     gain=P["acols"][:, 9 * FT:10 * FT], bias=P["acols"][:, 10 * FT:11 * FT],
                     pre_cb=lambda _ch: pre())
            for ft2 in range(FT):
                b = ft2 % 2
                for ft in range(FT):
                    k.mm(psP[b][:, :], wo[:, ft * C + ft2 * 128:ft * C + (ft2 + 1) * 128],
                         ynb[:, ft * 512:(ft + 1) * 512], ft == 0, ft == FT - 1,
                         rd=["Y.wo", "Y.yn%d" % ft],
                         wr=["Y.psP%d" % b] if ft == FT - 1 else [],
                         war=["Y.psP%d" % b] if ft == 0 else [], pub=(ft == FT - 1))
                k.dma("sp", rx[:, :], D["xfm"][ft2 * 128:(ft2 + 1) * 128, ch * 512:(ch + 1) * 512],
                      wr=["Y.rx"])
                k.op("dve", lambda e, b=b: e.tensor_add(ro[:, :], psP[b][:, :], rx[:, :]),
                     rd=["Y.psP%d" % b, "Y.rx"], wr=["Y.ro"])
                k.dma("sp", D["r1"][ft2 * 128:(ft2 + 1) * 128, ch * 512:(ch + 1) * 512],
                      ro[:, :], rd=["Y.ro"])
        k.barrier()


def _phase_ffn(k, nc, P, D, rin, rout, tag):
    with contextlib.ExitStack() as pst:
        B = alloc_ln_bufs(nc, pst, tag)
        xln = pst.enter_context(nc.sbuf_tensor(tag + "_xln", [128, FT * T], BF16))
        wgb = [pst.enter_context(nc.sbuf_tensor(f"{tag}_wg{i}", [128, FT * 128], BF16))
               for i in range(2)]
        wub = [pst.enter_context(nc.sbuf_tensor(f"{tag}_wu{i}", [128, FT * 128], BF16))
               for i in range(2)]
        gbt = pst.enter_context(nc.sbuf_tensor(tag + "_gb", [128, T], BF16))
        ubt = pst.enter_context(nc.sbuf_tensor(tag + "_ub", [128, 512], BF16))
        fbt = pst.enter_context(nc.sbuf_tensor(tag + "_fb", [128, T], BF16))
        psG = [pst.enter_context(nc.psum_tensor(f"{tag}_psG{i}", [128, 512], F32))
               for i in range(2)]
        psU = [pst.enter_context(nc.psum_tensor(f"{tag}_psU{i}", [128, 512], F32))
               for i in range(2)]

        for ch in range(4):
            def out_cb(ft, xa, ch=ch):
                k.op("dve", lambda e, a=xa, ft=ft, ch=ch: e.tensor_copy(
                    xln[:, ft * T + ch * 512:ft * T + (ch + 1) * 512], a),
                    rd=[tag + ".x%d" % ft], wr=[tag + ".xln%d" % ft])
            ln_chunk(k, B, P, ch, rin, out_cb)

        if globals().get("_FFN_DBG") == "xln" and tag == "F1":
            for ft in range(FT):
                k.dma("pool", D["dbg"][ft * 128:(ft + 1) * 128, :],
                      xln[:, ft * T:(ft + 1) * T], rd=[tag + ".xln%d" % ft])
        for it in range(NIT):
            b = it % 2
            k.dma("pool", wgb[b][:, :], D["f_wg"][it, :, :], wr=[tag + ".wg%d" % b])
            k.dma("pool", wub[b][:, :], D["f_wu"][it, :, :], wr=[tag + ".wu%d" % b])
            if globals().get("_FFN_DBG") == "wslab" and tag == "F1" and it < FT:
                k.dma("pool", D["dbg"][it * 128:(it + 1) * 128, :], wgb[b][:, :],
                      rd=[tag + ".wg%d" % b])
            for ch in range(4):
                for ft in range(FT):
                    k.mm(psG[b][:, :], wgb[b][:, ft * 128:(ft + 1) * 128],
                         xln[:, ft * T + ch * 512:ft * T + (ch + 1) * 512],
                         ft == 0, ft == FT - 1,
                         rd=[tag + ".wg%d" % b, tag + ".xln%d" % ft],
                         wr=[tag + ".psG%d" % b] if ft == FT - 1 else [],
                         war=[tag + ".psG%d" % b] if ft == 0 else [], pub=(ft == FT - 1))
                    k.mm(psU[b][:, :], wub[b][:, ft * 128:(ft + 1) * 128],
                         xln[:, ft * T + ch * 512:ft * T + (ch + 1) * 512],
                         ft == 0, ft == FT - 1,
                         rd=[tag + ".wu%d" % b, tag + ".xln%d" % ft],
                         wr=[tag + ".psU%d" % b] if ft == FT - 1 else [],
                         war=[tag + ".psU%d" % b] if ft == 0 else [], pub=(ft == FT - 1))
                if globals().get("_FFN_DBG") == "graw" and tag == "F1" and it < FT:
                    k.op("act", lambda e, pg=psG[b]: e.copy(ubt[:, :], pg[:, :]),
                         rd=[tag + ".psG%d" % b], wr=[tag + ".ub"])
                    k.dma("pool", D["dbg"][it * 128:(it + 1) * 128, ch * 512:(ch + 1) * 512],
                          ubt[:, :], rd=[tag + ".ub"])
                k.op("act", lambda e, ch=ch, it=it, pg=psG[b]: e.activation(
                    gbt[:, ch * 512:(ch + 1) * 512], pg[:, :], AF.Silu,
                    scale=P["sgc"][:, it:it + 1]),
                    rd=[tag + ".psG%d" % b, "sgc"], wr=[tag + ".gb%d" % ch])
                k.op("act", lambda e, it=it, pu=psU[b]: e.activation(
                    ubt[:, :], pu[:, :], AF.Copy, scale=P["suc"][:, it:it + 1]),
                    rd=[tag + ".psU%d" % b, "suc"], wr=[tag + ".ub"])
                k.op("dve", lambda e, ch=ch: e.tensor_mul(
                    fbt[:, ch * 512:(ch + 1) * 512], gbt[:, ch * 512:(ch + 1) * 512], ubt[:, :]),
                    rd=[tag + ".gb%d" % ch, tag + ".ub"], wr=[tag + ".fb"])
            if globals().get("_FFN_DBG") == "gate" and tag == "F1" and it < FT:
                k.dma("pool", D["dbg"][it * 128:(it + 1) * 128, :],
                      gbt[:, :], rd=[tag + ".gb0", tag + ".gb1", tag + ".gb2", tag + ".gb3"])
            k.dma("sp", D["fuse"][it * 128:(it + 1) * 128, :], fbt[:, :], rd=[tag + ".fb"])
        k.barrier()

    with contextlib.ExitStack() as pst:
        psGd = [pst.enter_context(nc.psum_tensor(f"{tag}_psGd{i}", [128, 512], F32))
                for i in range(2)]
        fdb = pst.enter_context(nc.sbuf_tensor(tag + "_fdb", [128, NIT * 512], BF16))
        wdb = [pst.enter_context(nc.sbuf_tensor(f"{tag}_wd{i}", [128, NIT * 128], BF16))
               for i in range(2)]
        rxt = pst.enter_context(nc.sbuf_tensor(tag + "_rx", [128, 512], F32))
        rot = pst.enter_context(nc.sbuf_tensor(tag + "_ro", [128, 512], F32))
        for ch in range(4):
            k.dma("sp", fdb[:, :].rearrange("p (j f) -> p j f", f=512),
                  D["fuse"].rearrange("(j p) t -> p j t", p=128, j=64)[:, :, ch * 512:(ch + 1) * 512],
                  wr=[tag + ".fdb"])
            for dtile in range(FT):
                b = dtile % 2
                if ch == 0 or True:
                    pass
                if dtile < 2 and ch == 0:
                    k.dma("pool", wdb[b][:, :], D["f_wd"][dtile, :, :], wr=[tag + ".wd%d" % b])
                for jt in range(NIT):
                    k.mm(psGd[b][:, :], wdb[b][:, jt * 128:(jt + 1) * 128],
                         fdb[:, jt * 512:(jt + 1) * 512], jt == 0, jt == NIT - 1,
                         rd=[tag + ".wd%d" % b, tag + ".fdb"],
                         wr=[tag + ".psG%d" % b] if jt == NIT - 1 else [],
                         war=[tag + ".psG%d" % b] if jt == 0 else [], pub=(jt == NIT - 1))
                nd = dtile + 1 if dtile + 1 < FT else (0 if ch + 1 < 4 else None)
                if nd is not None and not (ch == 0 and dtile == 0):
                    k.dma("pool", wdb[nd % 2][:, :], D["f_wd"][nd, :, :],
                          wr=[tag + ".wd%d" % (nd % 2)])
                k.op("act", lambda e, dtile=dtile, pg=psGd[b]: e.activation(
                    rot[:, :], pg[:, :], AF.Copy, scale=P["sdc"][:, dtile:dtile + 1]),
                    rd=[tag + ".psG%d" % b, "sdc"], wr=[tag + ".rot"])
                k.dma("pool", rxt[:, :],
                      rin[dtile * 128:(dtile + 1) * 128, ch * 512:(ch + 1) * 512],
                      wr=[tag + ".rxt"])
                k.op("dve", lambda e: e.tensor_add(rot[:, :], rot[:, :], rxt[:, :]),
                     rd=[tag + ".rxt"], wr=[tag + ".rot"])
                k.dma("sp", rout[dtile * 128:(dtile + 1) * 128, ch * 512:(ch + 1) * 512],
                      rot[:, :], rd=[tag + ".rot"])
        k.barrier()


def _phase_attn_b(k, nc, P, D, upto):
    with contextlib.ExitStack() as pst:
        B = alloc_ln_bufs(nc, pst, "Xb")
        xbt = pst.enter_context(nc.sbuf_tensor("Xb_x", [128, FT * T], BF16))
        wb = pst.enter_context(nc.sbuf_tensor("Xb_w", [128, FT * C], BF16))
        ob = pst.enter_context(nc.sbuf_tensor("Xb_ob", [128, 512], BF16))
        o2 = pst.enter_context(nc.sbuf_tensor("Xb_o2", [128, 512], BF16))
        h1b = pst.enter_context(nc.sbuf_tensor("Xb_h1", [128, 512], BF16))
        scro = pst.enter_context(nc.sbuf_tensor("Xb_scro", [1, 512], F32))
        ps = [pst.enter_context(nc.psum_tensor(f"Xb_ps{i}", [128, 512], F32)) for i in range(2)]
        psH = pst.enter_context(nc.psum_tensor("Xb_psH", [128, 512], F32))
        psR = pst.enter_context(nc.psum_tensor("Xb_psR", [1, 512], F32))

        for ch in range(4):
            def out_cb(ft, xa, ch=ch):
                k.op("dve", lambda e, a=xa, ft=ft, ch=ch: e.tensor_copy(
                    xbt[:, ft * T + ch * 512:ft * T + (ch + 1) * 512], a),
                    rd=["Xb.x%d" % ft], wr=["Xb.xln%d" % ft])
            ln_chunk(k, B, P, ch, D["r2"], out_cb)

        def gemm_fm(wkey, wap, out_cb):
            k.dma("pool", wb[:, :], wap[:, :], wr=["Xb.wb"])
            for ft2 in range(FT):
                for ch in range(4):
                    b = (ft2 * 4 + ch) % 2
                    for ft in range(FT):
                        k.mm(ps[b][:, :], wb[:, ft * C + ft2 * 128:ft * C + (ft2 + 1) * 128],
                             xbt[:, ft * T + ch * 512:ft * T + (ch + 1) * 512],
                             ft == 0, ft == FT - 1, rd=["Xb.wb", "Xb.xln%d" % ft],
                             wr=["Xb.ps%d" % b] if ft == FT - 1 else [],
                             war=["Xb.ps%d" % b] if ft == 0 else [], pub=(ft == FT - 1))
                    out_cb(ft2, ch, b)

        def out_q(ft2, ch, b):
            k.op("act", lambda e, b=b: e.copy(ob[:, :], ps[b][:, :]),
                 rd=["Xb.ps%d" % b], wr=["Xb.ob"])
            k.dma("sp", D["qb"][ft2 * 128:(ft2 + 1) * 128, ch * 512:(ch + 1) * 512],
                  ob[:, :], rd=["Xb.ob"])
        gemm_fm("q", D["b_wqT"], out_q)

        def out_k(ft2, ch, b):
            k.op("act", lambda e, b=b: e.copy(ob[:, :], ps[b][:, :]),
                 rd=["Xb.ps%d" % b], wr=["Xb.ob"])
            k.mm(psH[:, :], P["sw1b"][:, :], ob[:, :], True, True,
                 rd=["sw1b", "Xb.ob"], wr=["Xb.psH"], pub=True)
            k.op("act", lambda e: e.activation(h1b[:, :], psH[:, :], AF.Relu,
                                               bias=P["sb1c"][:, 0:1]),
                 rd=["Xb.psH", "sb1c"], wr=["Xb.h1"])
            k.mm(psR[0:1, :], P["sw2b"][:, 0:1], h1b[:, :], True, True,
                 rd=["sw2b", "Xb.h1"], wr=["Xb.psR"], pub=True)
            k.op("act", lambda e: e.add(scro[0:1, :], psR[0:1, :], P["sb2c"][0:1, 0:1]),
                 rd=["Xb.psR", "sb2c"], wr=["Xb.scro"])
            k.dma("sp", D["scl_t"].ap()[ft2:ft2 + 1, ch * 512:(ch + 1) * 512],
                  scro[0:1, :], rd=["Xb.scro"])
        gemm_fm("k", D["b_wkT"], out_k)

        def gemm_tm(wap, outdram):
            k.dma("pool", wb[:, :], wap[:, :], wr=["Xb.wb"])
            for tt in range(FT):
                for dv in range(4):
                    b = (tt * 4 + dv) % 2
                    for ft in range(FT):
                        k.mm(ps[b][:, :], xbt[:, ft * T + tt * 128:ft * T + (tt + 1) * 128],
                             wb[:, ft * C + dv * 512:ft * C + (dv + 1) * 512],
                             ft == 0, ft == FT - 1, rd=["Xb.wb", "Xb.xln%d" % ft],
                             wr=["Xb.ps%d" % b] if ft == FT - 1 else [],
                             war=["Xb.ps%d" % b] if ft == 0 else [], pub=(ft == FT - 1))
                    k.op("dve", lambda e, b=b: e.tensor_copy(o2[:, :], ps[b][:, :]),
                         rd=["Xb.ps%d" % b], wr=["Xb.o2"])
                    for sl in range(4):
                        hh = dv * 4 + sl
                        k.dma("sp", outdram[hh, tt * 128:(tt + 1) * 128, :],
                              o2[:, sl * 128:(sl + 1) * 128], rd=["Xb.o2"])
        gemm_tm(D["b_wkT"], D["kbt"])
        gemm_tm(D["b_wvT"], D["vbt"])
        k.barrier()

    k.emit("pool", lambda e: e.collective_compute(
        "AllGather", ALU.bypass, replica_groups=[[0, 1], [2, 3], [4, 5], [6, 7]],
        ins=[D["scl_t"].ap().opt()], outs=[D["scl_cc_t"].ap().opt()]),
        sem="cc1", inc=1)
    for e in ENGS:
        k._wait(e, "cc1", 1)
    _emit_selection(k, nc, P, D, upto)
    if upto in ("xb", "scores", "sel", "ksel"):
        return
    k.emit("pool", lambda e: e.collective_compute(
        "AllReduce", ALU.add, replica_groups=[[0, 1], [2, 3], [4, 5], [6, 7]],
        ins=[D["kvsel_t"].ap().opt()], outs=[D["kvsel_cc_t"].ap().opt()]),
        sem="cc2", inc=1)
    for e in ENGS:
        k._wait(e, "cc2", 1)

    kvsel_cc = D["kvsel_cc_t"].ap()
    with contextlib.ExitStack() as pst:
        ybt = pst.enter_context(nc.sbuf_tensor("Ab_y", [128, FT * T], BF16))
        qh = pst.enter_context(nc.sbuf_tensor("Ab_qh", [128, T], BF16))
        kselb = pst.enter_context(nc.sbuf_tensor("Ab_ks", [128, 128], BF16))
        vselb = pst.enter_context(nc.sbuf_tensor("Ab_vs", [128, 128], BF16))
        kT = pst.enter_context(nc.sbuf_tensor("Ab_kT", [128, 128], BF16))
        expb = pst.enter_context(nc.sbuf_tensor("Ab_ex", [128, 128], BF16))
        expT = pst.enter_context(nc.sbuf_tensor("Ab_exT", [128, T], BF16))
        colsb = pst.enter_context(nc.sbuf_tensor("Ab_cols", [128, 4], F32))
        psP_ = pst.enter_context(nc.psum_tensor("Ab_psP", [128, 128], F32))
        psTb = pst.enter_context(nc.psum_tensor("Ab_psT", [128, 128], BF16))
        psA = pst.enter_context(nc.psum_tensor("Ab_psA", [128, 512], F32))

        for h in range(HB):
            k.dma("pool", qh[:, :], D["qb"][h * 128:(h + 1) * 128, :], wr=["Ab.qh"])
            k.dma("pool", kselb[:, :], kvsel_cc[h * 256:h * 256 + 128, :], wr=["Ab.ks"])
            k.dma("pool", vselb[:, :], kvsel_cc[h * 256 + 128:h * 256 + 256, :], wr=["Ab.vs"])
            k.op("pe", lambda e: e.transpose(psTb[:, :], kselb[:, :], P["eyeb"][:, :]),
                 rd=["Ab.ks", "eyeb"], wr=["Ab.psT"], pub=True)
            k.op("act", lambda e: e.copy(kT[:, :], psTb[:, :]), rd=["Ab.psT"], wr=["Ab.kT"])
            for tt in range(FT):
                k.mm(psP_[:, :], qh[:, tt * 128:(tt + 1) * 128], kT[:, :], True, True,
                     rd=["Ab.qh", "Ab.kT"], wr=["Ab.psP"], pub=True)
                k.op("dve", lambda e: e.reduce_max(colsb[:, 0:1], psP_[:, :], axis=AX.X),
                     rd=["Ab.psP"], wr=["Ab.c0"])
                k.op("dve", lambda e: e.tensor_scalar_mul(colsb[:, 1:2], colsb[:, 0:1], -SCL_B),
                     rd=["Ab.c0"], wr=["Ab.c1"])
                k.op("act", lambda e: e.activation(expb[:, :], psP_[:, :], AF.Exp,
                                                   bias=colsb[:, 1:2], scale=SCL_B,
                                                   accum_out=colsb[:, 2:3]),
                     rd=["Ab.psP", "Ab.c1"], wr=["Ab.exp", "Ab.c2"])
                k.op("dve", lambda e: e.reciprocal(colsb[:, 3:4], colsb[:, 2:3]),
                     rd=["Ab.c2"], wr=["Ab.c3"])
                k.op("dve", lambda e: e.tensor_scalar_mul(expb[:, :], expb[:, :], colsb[:, 3:4]),
                     rd=["Ab.c3"], wr=["Ab.exp"])
                k.op("pe", lambda e: e.transpose(psTb[:, :], expb[:, :], P["eyeb"][:, :]),
                     rd=["Ab.exp", "eyeb"], wr=["Ab.psT"], pub=True)
                k.op("act", lambda e, tt=tt: e.copy(expT[:, tt * 128:(tt + 1) * 128], psTb[:, :]),
                     rd=["Ab.psT"], wr=["Ab.exT%d" % (tt // 4)])
            for g in range(4):
                k.mm(psA[:, :], vselb[:, :], expT[:, g * 512:(g + 1) * 512], True, True,
                     rd=["Ab.vs", "Ab.exT%d" % g], wr=["Ab.psA"], pub=True)
                k.op("act", lambda e, h=h, g=g: e.copy(
                    ybt[:, h * T + g * 512:h * T + (g + 1) * 512], psA[:, :]),
                    rd=["Ab.psA"], wr=["Ab.y%d" % h])
        wb2 = pst.enter_context(nc.sbuf_tensor("Ab_wo", [128, FT * C], BF16))
        rxt = pst.enter_context(nc.sbuf_tensor("Ab_rx", [128, 512], F32))
        rot = pst.enter_context(nc.sbuf_tensor("Ab_ro", [128, 512], F32))
        psO2 = [pst.enter_context(nc.psum_tensor(f"Ab_psO{i}", [128, 512], F32))
                for i in range(2)]
        k.dma("pool", wb2[:, :], D["b_woT"][:, :], wr=["Ab.wo"])
        for ft2 in range(FT):
            for ch in range(4):
                b = (ft2 * 4 + ch) % 2
                for ft in range(FT):
                    k.mm(psO2[b][:, :], wb2[:, ft * C + ft2 * 128:ft * C + (ft2 + 1) * 128],
                         ybt[:, ft * T + ch * 512:ft * T + (ch + 1) * 512],
                         ft == 0, ft == FT - 1, rd=["Ab.wo", "Ab.y%d" % ft],
                         wr=["Ab.psO%d" % b] if ft == FT - 1 else [],
                         war=["Ab.psO%d" % b] if ft == 0 else [], pub=(ft == FT - 1))
                k.dma("sp", rxt[:, :], D["r2"][ft2 * 128:(ft2 + 1) * 128, ch * 512:(ch + 1) * 512],
                      wr=["Ab.rx"])
                if globals().get("_AB_NORES"):
                    k.op("dve", lambda e, po=psO2[b]: e.tensor_scalar_mul(rot[:, :], po[:, :], 1.0),
                         rd=["Ab.psO%d" % b, "Ab.rx"], wr=["Ab.rot"])
                else:
                    k.op("dve", lambda e, po=psO2[b]: e.tensor_add(rot[:, :], po[:, :], rxt[:, :]),
                         rd=["Ab.psO%d" % b, "Ab.rx"], wr=["Ab.rot"])
                k.dma("sp", D["r3"][ft2 * 128:(ft2 + 1) * 128, ch * 512:(ch + 1) * 512],
                      rot[:, :], rd=["Ab.rot"])
        k.barrier()


def _emit_selection(k, nc, P, D, upto):
    with contextlib.ExitStack() as pst:
        def sb(name, shape, dtyp=F32):
            return pst.enter_context(nc.sbuf_tensor(name, shape, dtyp))
        selt = sb("L_sel", [128, 512])
        mb = sb("L_mb", [128, 512])
        ms = sb("L_ms", [128, 512])
        elig = sb("L_el", [128, 512])
        eligv = sb("L_elv", [128, 512])
        ind = sb("L_ind", [128, 512])
        cs = sb("L_cs", [128, 512])
        fill1 = sb("L_f1", [128, 512])
        rankp = sb("L_rk", [128, 512])
        rankT = sb("L_rkT", [128, 512])
        w = sb("L_w", [128, 16])
        cc = sb("L_cc", [128, 4])
        mrow = sb("L_mrow", [4, 128])
        mred = sb("L_mred", [4, 16])
        mcolsT = sb("L_mcT", [16, 4])
        mtall = sb("L_mt", [128, FT * 128], BF16)
        kvsb = sb("L_kv", [128, FT * 128], BF16)
        oselk = sb("L_ok", [128, 128])
        oselv = sb("L_ov", [128, 128])
        psB = pst.enter_context(nc.psum_tensor("L_psB", [128, 4], F32))
        psX = pst.enter_context(nc.psum_tensor("L_psX", [128, 128], F32))
        psKV = pst.enter_context(nc.psum_tensor("L_psKV", [128, 128], F32))

        scl_cc = D["scl_cc_t"].ap()
        k.op("dve", lambda e: e.memset(w[:, :], 0.0), wr=["L.w"])
        k.dma_fn("pool", lambda e: e.indirect_dma_start(
            out=selt[:, :], out_offset=None,
            in_=scl_cc.rearrange("r (c f) -> (r c) f", f=512),
            in_offset=IndirectOffsetOnAxis(P["permt"][:, 0:1], 0)),
            rd=["permt"], wr=["L.sel"])
        if upto == "scores":
            for ft in range(FT):
                k.dma("sp", D["dbg"][ft * 128:(ft + 1) * 128, 0:512], selt[:, :], rd=["L.sel"])
            return

        W = lambda i: w[:, i:i + 1]
        def hsum2(src2, n, outslot, rdk, wrk):
            k.mm(psB[:, 0:n], P["Hm"][:, :], src2, True, True,
                 rd=[rdk, "Hm"], wr=["L.psB"], pub=True)
            k.op("act", lambda e: e.copy(w[:, outslot:outslot + n], psB[:, 0:n]),
                 rd=["L.psB"], wr=[wrk])

        k.op("dve", lambda e: e.reduce_max(cc[:, 0:1], selt[:, :], axis=AX.X),
             rd=["L.sel"], wr=["L.cc"])
        k.op("dve", lambda e: e.tensor_scalar_mul(ms[:, :], selt[:, :], -1.0),
             rd=["L.sel"], wr=["L.ms"])
        k.op("dve", lambda e: e.reduce_max(cc[:, 1:2], ms[:, :], axis=AX.X),
             rd=["L.ms"], wr=["L.cc"])
        k.op("pe", lambda e: e.transpose(psX[0:2, 0:128], cc[:, 0:2], P["eyef"][:, :]),
             rd=["L.cc", "eyef"], wr=["L.psX"], pub=True)
        k.op("act", lambda e: e.copy(mrow[0:2, :], psX[0:2, 0:128]),
             rd=["L.psX"], wr=["L.mrow"])
        k.op("dve", lambda e: e.reduce_max(
            mred[0:2, :], mrow[0:2, :].rearrange("p (h s) -> p h s", s=8), axis=AX.X),
            rd=["L.mrow"], wr=["L.mred"])
        k.op("pe", lambda e: e.transpose(psX[0:16, 0:2], mred[0:2, 0:16], P["eyef"][0:2, 0:2]),
             rd=["L.mred", "eyef"], wr=["L.psX"], pub=True)
        k.op("act", lambda e: e.copy(mcolsT[0:16, 0:2], psX[0:16, 0:2]),
             rd=["L.psX"], wr=["L.mcT"])
        k.mm(psB[:, 0:2], P["Hs"][0:16, :], mcolsT[0:16, 0:2], True, True,
             rd=["L.mcT", "Hs"], wr=["L.psB"], pub=True)
        k.op("dve", lambda e: e.tensor_scalar(W(1), psB[:, 1:2], -1.0, -1.0, ALU.mult, ALU.add),
             rd=["L.psB"], wr=["L.w"])
        k.op("act", lambda e: e.copy(W(2), psB[:, 0:1]), rd=["L.psB"], wr=["L.w"])

        k.op("dve", lambda e: e.reduce_sum(cc[:, 0:1], selt[:, :], axis=AX.X),
             rd=["L.sel"], wr=["L.cc"])
        hsum2(cc[:, 0:1], 1, 5, "L.cc", "L.w")
        k.op("dve", lambda e: e.tensor_scalar(W(0), W(5), -float(KSEL), 1.0 / 4096.0,
                                              ALU.add, ALU.mult), rd=["L.w"], wr=["L.w"])
        for _ in range(16):
            k.op("dve", lambda e: e.tensor_scalar(mb[:, :], selt[:, :], W(0), None, ALU.is_gt),
                 rd=["L.sel", "L.w"], wr=["L.mb"])
            k.op("dve", lambda e: e.tensor_mul(ms[:, :], mb[:, :], selt[:, :]),
                 rd=["L.mb", "L.sel"], wr=["L.ms"])
            k.op("dve", lambda e: e.reduce_sum(cc[:, 0:1], ms[:, :], axis=AX.X),
                 rd=["L.ms"], wr=["L.cc"])
            k.op("dve", lambda e: e.reduce_sum(cc[:, 1:2], mb[:, :], axis=AX.X),
                 rd=["L.mb"], wr=["L.cc"])
            k.mm(psB[:, 0:2], P["Hm"][:, :], cc[:, 0:2], True, True,
                 rd=["L.cc", "Hm"], wr=["L.psB"], pub=True)
            k.op("dve", lambda e: e.tensor_scalar_add(W(11), psB[:, 0:1], -float(KSEL)),
                 rd=["L.psB"], wr=["L.w"])
            k.op("dve", lambda e: e.reciprocal(W(12), psB[:, 1:2]), rd=["L.psB"], wr=["L.w"])
            k.op("dve", lambda e: e.tensor_mul(W(0), W(11), W(12)), rd=["L.w"], wr=["L.w"])

        k.op("dve", lambda e: e.tensor_scalar(mb[:, :], selt[:, :], W(0), None, ALU.is_gt),
             rd=["L.sel", "L.w"], wr=["L.mb"])
        k.op("dve", lambda e: e.reduce_sum(cc[:, 0:1], mb[:, :], axis=AX.X),
             rd=["L.mb"], wr=["L.cc"])
        k.mm(psB[:, 0:1], P["Hm"][:, :], cc[:, 0:1], True, True,
             rd=["L.cc", "Hm"], wr=["L.psB"], pub=True)
        k.op("dve", lambda e: e.tensor_scalar(W(13), psB[:, 0:1], float(KSEL), None, ALU.is_ge),
             rd=["L.psB"], wr=["L.w"])

        for _ in range(40):
            k.op("dve", lambda e: e.tensor_add(W(3), W(1), W(2)), rd=["L.w"], wr=["L.w"])
            k.op("dve", lambda e: e.tensor_scalar_mul(W(3), W(3), 0.5), wr=["L.w"])
            k.op("dve", lambda e: e.tensor_scalar(mb[:, :], selt[:, :], W(3), None, ALU.is_gt),
                 rd=["L.sel", "L.w"], wr=["L.mb"])
            k.op("dve", lambda e: e.reduce_sum(cc[:, 0:1], mb[:, :], axis=AX.X),
                 rd=["L.mb"], wr=["L.cc"])
            k.mm(psB[:, 0:1], P["Hm"][:, :], cc[:, 0:1], True, True,
                 rd=["L.cc", "Hm"], wr=["L.psB"], pub=True)
            k.op("dve", lambda e: e.tensor_scalar(W(6), psB[:, 0:1], float(KSEL), None, ALU.is_ge),
                 rd=["L.psB"], wr=["L.w"])
            k.op("dve", lambda e: e.tensor_sub(W(11), W(3), W(1)), wr=["L.w"])
            k.op("dve", lambda e: e.tensor_mul(W(11), W(11), W(6)), wr=["L.w"])
            k.op("dve", lambda e: e.tensor_add(W(1), W(1), W(11)), wr=["L.w"])
            k.op("dve", lambda e: e.tensor_sub(W(11), W(3), W(2)), wr=["L.w"])
            k.op("dve", lambda e: e.tensor_scalar(W(12), W(6), -1.0, 1.0, ALU.mult, ALU.add),
                 wr=["L.w"])
            k.op("dve", lambda e: e.tensor_mul(W(11), W(11), W(12)), wr=["L.w"])
            k.op("dve", lambda e: e.tensor_add(W(2), W(2), W(11)), wr=["L.w"])

        k.op("dve", lambda e: e.tensor_sub(W(11), W(2), W(0)), rd=["L.w"], wr=["L.w"])
        k.op("dve", lambda e: e.tensor_mul(W(11), W(11), W(13)), wr=["L.w"])
        k.op("dve", lambda e: e.tensor_add(W(7), W(0), W(11)), wr=["L.w"])
        k.op("dve", lambda e: e.tensor_scalar(ind[:, :], selt[:, :], W(7), None, ALU.is_gt),
             rd=["L.sel", "L.w"], wr=["L.ind"])
        k.op("dve", lambda e: e.reduce_sum(cc[:, 0:1], ind[:, :], axis=AX.X),
             rd=["L.ind"], wr=["L.cc"])
        k.mm(psB[:, 0:1], P["Hm"][:, :], cc[:, 0:1], True, True,
             rd=["L.cc", "Hm"], wr=["L.psB"], pub=True)
        k.op("dve", lambda e: e.tensor_scalar(W(8), psB[:, 0:1], -1.0, float(KSEL),
                                              ALU.mult, ALU.add), rd=["L.psB"], wr=["L.w"])
        k.op("dve", lambda e: e.tensor_scalar(elig[:, :], selt[:, :], W(7), None, ALU.is_le),
             rd=["L.sel", "L.w"], wr=["L.el"])

        def gcs(src, srckey, outap):
            k.op("dve", lambda e, s=src, o=outap: e.tensor_tensor_scan(
                o, s, s, 0.0, ALU.add, ALU.bypass), rd=[srckey], wr=["L.cs"])
            k.op("dve", lambda e, s=src: e.reduce_sum(cc[:, 0:1], s, axis=AX.X),
                 rd=[srckey], wr=["L.cc"])
            k.mm(psB[:, 0:1], P["Lm"][:, :], cc[:, 0:1], True, True,
                 rd=["L.cc", "Lm"], wr=["L.psB"], pub=True)
            k.op("act", lambda e: e.copy(W(9), psB[:, 0:1]), rd=["L.psB"], wr=["L.w"])
            k.op("dve", lambda e, o=outap: e.tensor_scalar_add(o, o, W(9)),
                 rd=["L.w"], wr=["L.cs"])
        gcs(elig[:, :], "L.el", cs[:, :])
        k.op("dve", lambda e: e.tensor_scalar(ms[:, :], cs[:, :], W(8), None, ALU.is_le),
             rd=["L.cs", "L.w"], wr=["L.ms"])
        k.op("dve", lambda e: e.tensor_mul(ms[:, :], ms[:, :], elig[:, :]),
             rd=["L.el"], wr=["L.ms"])
        k.op("dve", lambda e: e.tensor_copy(eligv[:, :], elig[:, :]), rd=["L.el"], wr=["L.elv"])
        k.op("dve", lambda e: e.memset(fill1[:, :], 0.0), wr=["L.f1"])
        k.op("act", lambda e: e.copy(W(15), W(8)), rd=["L.w"], wr=["L.w"])
        for _ in range(4):
            k.op("dve", lambda e: e.tensor_scalar(cs[:, :], eligv[:, :], 1.0e9, -1.0e9,
                                                  ALU.mult, ALU.add),
                 rd=["L.elv"], wr=["L.cs"])
            k.op("dve", lambda e: e.tensor_mul(mb[:, :], selt[:, :], eligv[:, :]),
                 rd=["L.sel", "L.elv"], wr=["L.mb"])
            k.op("dve", lambda e: e.tensor_add(mb[:, :], mb[:, :], cs[:, :]),
                 rd=["L.cs"], wr=["L.mb"])
            k.op("dve", lambda e: e.reduce_max(cc[:, 0:1], mb[:, :], axis=AX.X),
                 rd=["L.mb"], wr=["L.cc"])
            k.op("pe", lambda e: e.transpose(psX[0:1, 0:128], cc[:, 0:1], P["eyef"][:, :]),
                 rd=["L.cc", "eyef"], wr=["L.psX"], pub=True)
            k.op("act", lambda e: e.copy(mrow[0:1, :], psX[0:1, 0:128]),
                 rd=["L.psX"], wr=["L.mrow"])
            k.op("dve", lambda e: e.reduce_max(
                mred[0:1, :], mrow[0:1, :].rearrange("p (h s) -> p h s", s=8), axis=AX.X),
                rd=["L.mrow"], wr=["L.mred"])
            k.op("pe", lambda e: e.transpose(psX[0:16, 0:1], mred[0:1, 0:16], P["eyef"][0:1, 0:1]),
                 rd=["L.mred", "eyef"], wr=["L.psX"], pub=True)
            k.op("act", lambda e: e.copy(mcolsT[0:16, 0:1], psX[0:16, 0:1]),
                 rd=["L.psX"], wr=["L.mcT"])
            k.mm(psB[:, 0:1], P["Hs"][0:16, :], mcolsT[0:16, 0:1], True, True,
                 rd=["L.mcT", "Hs"], wr=["L.psB"], pub=True)
            k.op("act", lambda e: e.copy(W(10), psB[:, 0:1]), rd=["L.psB"], wr=["L.w"])
            k.op("dve", lambda e: e.tensor_scalar(mb[:, :], selt[:, :], W(10), None, ALU.is_equal),
                 rd=["L.sel", "L.w"], wr=["L.mb"])
            k.op("dve", lambda e: e.tensor_mul(mb[:, :], mb[:, :], eligv[:, :]),
                 rd=["L.elv"], wr=["L.mb"])
            gcs(mb[:, :], "L.mb", cs[:, :])
            k.op("dve", lambda e: e.tensor_scalar(ind[:, 0:0 + 512] if False else cs[:, :],
                                                  cs[:, :], W(15), None, ALU.is_le),
                 rd=["L.cs", "L.w"], wr=["L.cs"])
            k.op("dve", lambda e: e.tensor_mul(cs[:, :], cs[:, :], mb[:, :]),
                 rd=["L.mb"], wr=["L.cs"])
            k.op("dve", lambda e: e.tensor_add(fill1[:, :], fill1[:, :], cs[:, :]),
                 rd=["L.cs"], wr=["L.f1"])
            k.op("dve", lambda e: e.reduce_sum(cc[:, 0:1], cs[:, :], axis=AX.X),
                 rd=["L.cs"], wr=["L.cc"])
            k.mm(psB[:, 0:1], P["Hm"][:, :], cc[:, 0:1], True, True,
                 rd=["L.cc", "Hm"], wr=["L.psB"], pub=True)
            k.op("dve", lambda e: e.tensor_sub(W(15), W(15), psB[:, 0:1]),
                 rd=["L.psB"], wr=["L.w"])
            k.op("dve", lambda e: e.tensor_sub(eligv[:, :], eligv[:, :], mb[:, :]),
                 rd=["L.mb"], wr=["L.elv"])
        k.op("dve", lambda e: e.tensor_sub(fill1[:, :], fill1[:, :], ms[:, :]),
             rd=["L.ms"], wr=["L.f1"])
        k.op("dve", lambda e: e.tensor_scalar_mul(fill1[:, :], fill1[:, :], W(13)),
             rd=["L.w"], wr=["L.f1"])
        k.op("dve", lambda e: e.tensor_add(fill1[:, :], fill1[:, :], ms[:, :]),
             rd=["L.ms"], wr=["L.f1"])
        k.op("dve", lambda e: e.tensor_add(ind[:, :], ind[:, :], fill1[:, :]),
             rd=["L.f1"], wr=["L.ind"])
        if upto == "sel":
            k.dma("sp", D["dbg"][0:128, 0:512], ind[:, :], rd=["L.ind"])
            k.dma("sp", D["dbg"][128:256, 0:16], w[:, :], rd=["L.w"])
            k.dma("sp", D["dbg"][256:384, 0:512], ms[:, :], rd=["L.ms"])
            k.dma("sp", D["dbg"][384:512, 0:512], fill1[:, :], rd=["L.f1"])
            k.dma("sp", D["dbg"][512:640, 0:512], elig[:, :], rd=["L.el"])
            return

        gcs(ind[:, :], "L.ind", cs[:, :])
        k.op("dve", lambda e: e.tensor_mul(rankp[:, :], cs[:, :], ind[:, :]),
             rd=["L.cs", "L.ind"], wr=["L.rk"])
        for b4 in range(4):
            k.op("pe", lambda e, b4=b4: e.transpose(psX[:, :], rankp[:, b4 * 128:(b4 + 1) * 128],
                                                    P["eyef"][:, :]),
                 rd=["L.rk", "eyef"], wr=["L.psX"], pub=True)
            k.op("act", lambda e, b4=b4: e.copy(rankT[:, b4 * 128:(b4 + 1) * 128], psX[:, :]),
                 rd=["L.psX"], wr=["L.rkT"])

        kvsel = D["kvsel_t"].ap()
        for h in range(HB):
            for lb in range(FT):
                col = 128 * (lb % 4) + 8 * h + (lb // 4)
                k.op("dve", lambda e, lb=lb, col=col: e.tensor_scalar(
                    mtall[:, lb * 128:(lb + 1) * 128], P["Jb"][:, :], rankT[:, col:col + 1],
                    None, ALU.is_equal), rd=["Jb", "L.rkT"], wr=["L.mt"])
            k.dma("pool", kvsb[:, :].rearrange("p (tt d) -> p tt d", d=128),
                  D["kbt"][h].rearrange("(tt p) d -> p tt d", p=128, tt=16), wr=["L.kvs"])
            for lb in range(FT):
                k.mm(psKV[:, :], mtall[:, lb * 128:(lb + 1) * 128],
                     kvsb[:, lb * 128:(lb + 1) * 128], lb == 0, lb == FT - 1,
                     rd=["L.mt", "L.kvs"], wr=["L.psKV"] if lb == FT - 1 else [],
                     war=["L.psKV"] if lb == 0 else [], pub=(lb == FT - 1))
            k.op("act", lambda e: e.copy(oselk[:, :], psKV[:, :]), rd=["L.psKV"], wr=["L.ok"])
            k.dma("sp", kvsel[h * 256:h * 256 + 128, :], oselk[:, :], rd=["L.ok"])
            k.dma("pool", kvsb[:, :].rearrange("p (tt d) -> p tt d", d=128),
                  D["vbt"][h].rearrange("(tt p) d -> p tt d", p=128, tt=16), wr=["L.kvs"])
            for lb in range(FT):
                k.mm(psKV[:, :], mtall[:, lb * 128:(lb + 1) * 128],
                     kvsb[:, lb * 128:(lb + 1) * 128], lb == 0, lb == FT - 1,
                     rd=["L.mt", "L.kvs"], wr=["L.psKV"] if lb == FT - 1 else [],
                     war=["L.psKV"] if lb == 0 else [], pub=(lb == FT - 1))
            k.op("act", lambda e: e.copy(oselv[:, :], psKV[:, :]), rd=["L.psKV"], wr=["L.ov"])
            k.dma("sp", kvsel[h * 256 + 128:h * 256 + 256, :], oselv[:, :], rd=["L.ov"])
        if upto == "ksel":
            for ft in range(FT):
                k.dma("sp", D["dbg"][ft * 128:(ft + 1) * 128, 0:128],
                      kvsel[ft * 256:ft * 256 + 128, :], deps=[("sp_d", k.cnt["sp_d"])])
        k.barrier()



def _np_col(v):
    return np.ascontiguousarray(np.asarray(v, dtype=np.float32).reshape(-1, 128).T)


def _lhsT_pack(W):
    W = np.ascontiguousarray(W)
    din, dout = W.shape
    return np.ascontiguousarray(
        W.reshape(din // 128, 128, dout).transpose(1, 0, 2).reshape(128, -1))


def _host_constants(core):
    half = core % 2
    p = np.arange(128) % 64
    tcol = np.arange(64)
    maskD = ((p[:, None] // LSUB == tcol[None, :] // LSUB) &
             (p[:, None] <= tcol[None, :])).astype(np.float32)
    maskS = np.broadcast_to(
        ((tcol[None, :] // LSUB == p[:, None] // LSUB + 1) &
         np.isin(tcol[None, :] // LSUB, (1, 3))), (128, 64)).astype(np.float32)
    J = np.broadcast_to(np.arange(1, 129, dtype=np.float32)[None, :], (128, 128)).copy()
    seg = np.arange(128) % 8
    h = np.arange(128) // 8
    gseg = np.where(seg < 4, seg + 4 * half, (seg - 4) + 4 * (1 - half))
    Lmask = ((h[:, None] == h[None, :]) & (gseg[:, None] < gseg[None, :])).astype(np.float32)
    Hmask = (h[:, None] == h[None, :]).astype(np.float32)
    Hsel = (np.arange(16)[:, None] == h[None, :]).astype(np.float32)
    iota = (gseg[:, None] * 512 + np.arange(512)[None, :]).astype(np.float32)
    perm = np.empty(128, np.uint32)
    for hh in range(16):
        for sg in range(8):
            rank = half if sg < 4 else 1 - half
            perm[8 * hh + sg] = (rank * HB + hh) * 4 + (sg % 4)
    eye = np.eye(128, dtype=np.float32)
    return dict(c_maskD=maskD, c_maskS=np.ascontiguousarray(maskS), c_J=J,
                c_Lmask=Lmask, c_Hmask=Hmask, c_Hsel=Hsel, c_iota=iota,
                c_perm=perm.reshape(128, 1), c_eye=eye,
                c_par=np.full((128, 1), float(half), np.float32))


def kernel(**inputs):
    g = lambda n: np.asarray(inputs[n])
    x = g("L0_attn_a_x")
    nc = build_program(_BUILD_UPTO)

    acols = np.concatenate([
        _np_col(g("L0_attn_a_maa_x")), _np_col(g("L0_attn_a_maa_r")),
        _np_col(g("L0_attn_a_maa_k")), _np_col(g("L0_attn_a_maa_v")),
        _np_col(g("L0_attn_a_maa_w")), _np_col(g("L0_attn_a_q_b")),
        _np_col(g("L0_attn_a_k_b")), _np_col(g("L0_attn_a_v_b")),
        _np_col(g("L0_attn_a_time_decay")), _np_col(g("L0_attn_a_ln_g")),
        _np_col(g("L0_attn_a_ln_b")), np.zeros((128, FT), np.float32)], axis=1)
    wg = np.asarray(g("L0_ffn_w_gate"))
    wu = np.asarray(g("L0_ffn_w_up"))
    wd = np.asarray(g("L0_ffn_w_down"))
    common = {
        "a_qw": _lhsT_pack(g("L0_attn_a_q_w")), "a_kw": _lhsT_pack(g("L0_attn_a_k_w")),
        "a_vw": _lhsT_pack(g("L0_attn_a_v_w")), "a_ow": _lhsT_pack(g("L0_attn_a_o_w")),
        "a_maa_w1": _lhsT_pack(g("L0_attn_a_maa_w1")),
        "a_maa_w2": g("L0_attn_a_maa_w2"),
        "a_td_w1": _lhsT_pack(g("L0_attn_a_td_w1")), "a_td_w2": g("L0_attn_a_td_w2"),
        "a_cols": acols, "a_vb_row": g("L0_attn_a_v_b").reshape(1, C),
        "b_wqT": _lhsT_pack(g("L0_attn_b_wq").T), "b_wkT": _lhsT_pack(g("L0_attn_b_wk").T),
        "b_wvT": _lhsT_pack(g("L0_attn_b_wv").T), "b_woT": _lhsT_pack(g("L0_attn_b_wo").T),
        "b_sw1T": np.ascontiguousarray(g("L0_attn_b_sw1").T),
        "b_sw2c": np.ascontiguousarray(g("L0_attn_b_sw2").reshape(128, 1)),
        "b_sb1c": np.ascontiguousarray(g("L0_attn_b_sb1").reshape(128, 1)),
        "b_sb2": g("L0_attn_b_sb2").reshape(1, 1),
        "f_wg": np.ascontiguousarray(
            wg.reshape(FT, 128, NIT, 128).transpose(2, 1, 0, 3).reshape(NIT, 128, FT * 128)),
        "f_wu": np.ascontiguousarray(
            wu.reshape(FT, 128, NIT, 128).transpose(2, 1, 0, 3).reshape(NIT, 128, FT * 128)),
        "f_wd": np.ascontiguousarray(
            wd.reshape(NIT, 128, FT, 128).transpose(2, 1, 0, 3).reshape(FT, 128, NIT * 128)),
        "f_sg": _np_col(g("L0_ffn_s_gate")), "f_su": _np_col(g("L0_ffn_s_up")),
        "f_sd": _np_col(g("L0_ffn_s_down")),
    }
    in_maps = []
    for core in range(NCORE):
        b, half = core // 2, core % 2
        m = dict(common)
        m.update(_host_constants(core))
        m["xfm"] = np.ascontiguousarray(x[b, half * T:(half + 1) * T, :].T)
        xp = np.zeros(C, np.float32) if half == 0 else np.ascontiguousarray(x[b, T - 1, :])
        m["xprev"] = _np_col(xp)
        in_maps.append(m)

    res = run_bass_kernel_spmd(nc, in_maps, core_ids=list(range(NCORE)))
    out = np.empty((4, 4096, C), np.float32)
    for core in range(NCORE):
        b, half = core // 2, core % 2
        out[b, half * T:(half + 1) * T, :] = res.results[core]["yfm"].T
    return out


# version 03 (final)
# speedup vs baseline: 1.0035x; 1.0035x over previous
import contextlib
import numpy as np
import concourse.bass as bass
import concourse.mybir as mybir
from concourse.bass_utils import run_bass_kernel_spmd
from concourse.bass_core_rust import IndirectOffsetOnAxis

F32 = mybir.dt.float32
BF16 = mybir.dt.bfloat16
I8 = mybir.dt.int8
U32 = mybir.dt.uint32
ALU = mybir.AluOpType
AF = mybir.ActivationFunctionType
AX = mybir.AxisListType

C = 2048
T = 2048
FT = C // 128
NCORE = 8
INTER = 8192
NIT = INTER // 128
LCH = 64
LSUB = 16
NCH = T // LCH
NPAIR = 16
HB = 16
KSEL = 128
EPS = 1e-5
SCL_A = 0.125
SCL_B = 0.08838834764831845
BIG = 1.0e30

_BUILD_UPTO = "full"
_SCAN_DBG = False
_FFN_DBG = None
_AB_NORES = False
_NO_DBGCOPY = False

ENGS = ("pe", "dve", "act", "pool", "sp")
NDSEM = 8
_SPSEMS = tuple("spd%d" % i for i in range(NDSEM))
_PQSEMS = tuple("pqd%d" % i for i in range(NDSEM))
SEMS = ("pe", "dve", "act", "pool_c", "cc0", "cc1", "cc2") + _SPSEMS + _PQSEMS
_OWN = {("pe", "pe"), ("dve", "dve"), ("act", "act"), ("pool", "pool_c")}


class K:

    def __init__(self, nc):
        self.nc = nc
        self.prog = {e: [] for e in ENGS}
        self.cnt = {s: 0 for s in SEMS}
        self.waited = {e: {s: 0 for s in SEMS} for e in ENGS}
        self.sems = {}
        self.wtok = {}
        self.rtok = {}
        self.rr = {"sp": 0, "pool": 0}
        self._pend_rd = set()
        self._pend_wr = set()

    def _wait(self, eng, sem, val):
        if val <= 0 or self.waited[eng][sem] >= val:
            return
        assert val <= self.cnt[sem], f"wait on future {sem}={val} (now {self.cnt[sem]})"
        self.waited[eng][sem] = val
        self.prog[eng].append(("w", sem, val))

    def emit(self, eng, fn, deps=(), pub=False, dma=False, sem=None, inc=None):
        for d in deps:
            if d is not None:
                self._wait(eng, d[0], d[1])
        if sem is None and (pub or dma):
            assert not dma, "dma emits must pass sem explicitly (use K.dma)"
            sem = {"pe": "pe", "dve": "dve", "act": "act", "pool": "pool_c"}[eng]
        if sem is not None:
            step = inc if inc is not None else (16 if dma else 1)
            self.cnt[sem] += step
            self.prog[eng].append(("i", fn, sem, step))
            return (sem, self.cnt[sem])
        self.prog[eng].append(("i", fn, None, 0))
        return None

    def op(self, eng, fn, rd=(), wr=(), deps=(), pub=None, dma=False, sem=None, inc=None,
           war=()):
        dl = list(deps)
        for b in rd:
            dl.append(self.wtok.get(b))
        for b in war:
            dl.append(self.wtok.get(b))
            dl.extend(self.rtok.get(b, ()))
        for b in wr:
            assert b not in self._pend_rd and b not in self._pend_wr, \
                f"write to {b} while unpublished pe access pending"
            dl.append(self.wtok.get(b))
            dl.extend(self.rtok.get(b, ()))
        if pub is None:
            pub = (eng != "pe")
        tok = self.emit(eng, fn, deps=dl, pub=pub, dma=dma, sem=sem, inc=inc)
        if tok is None:
            self._pend_rd.update(rd)
            self._pend_wr.update(wr)
            return None
        if eng == "pe":
            for b in self._pend_rd:
                self.rtok.setdefault(b, []).append(tok)
            for b in self._pend_wr:
                self.wtok[b] = tok
                self.rtok[b] = []
            self._pend_rd.clear()
            self._pend_wr.clear()
        for b in rd:
            self.rtok.setdefault(b, []).append(tok)
        for b in wr:
            self.wtok[b] = tok
            self.rtok[b] = []
        return tok

    def dma_fn(self, q, fn, rd=(), wr=(), deps=()):
        i = self.rr[q]
        self.rr[q] = (i + 1) % NDSEM
        sem = ("spd%d" % i) if q == "sp" else ("pqd%d" % i)
        self._wait(q, sem, self.cnt[sem])
        return self.op(q, fn, rd=rd, wr=wr, deps=deps, dma=True, sem=sem)

    def dma(self, q, out, in_, rd=(), wr=(), deps=()):
        i = self.rr[q]
        self.rr[q] = (i + 1) % NDSEM
        sem = ("spd%d" % i) if q == "sp" else ("pqd%d" % i)
        self._wait(q, sem, self.cnt[sem])
        return self.op(q, lambda e, o=out, i_=in_: e.dma_start(out=o, in_=i_),
                       rd=rd, wr=wr, deps=deps, dma=True, sem=sem)

    def mm(self, out, lhsT, rhs, start, stop, rd=(), wr=(), deps=(), pub=False, war=(),
           tp=None):
        return self.op("pe", lambda e, o=out, l=lhsT, r=rhs, s1=start, s2=stop, tp=tp:
                       e.matmul(o, l, r, start=s1, stop=s2, tile_position=tp),
                       rd=rd, wr=wr, deps=deps, pub=pub, war=war)

    def barrier(self):
        assert not self._pend_rd and not self._pend_wr
        snap = dict(self.cnt)
        for e in ENGS:
            for s, v in snap.items():
                if (e, s) in _OWN:
                    continue
                self._wait(e, s, v)
        self.wtok.clear()
        self.rtok.clear()

    def finalize(self):
        self.barrier()
        deps = [(s, self.cnt[s]) for s in _SPSEMS + _PQSEMS]
        self.emit("sp", lambda e: e.nop(), deps=deps)

    def replay(self, block):
        def mk(eng):
            def body(e):
                for item in self.prog[eng]:
                    if item[0] == "w":
                        e.wait_ge(self.sems[item[1]], item[2])
                    else:
                        _, fn, sem, step = item
                        ins = fn(e)
                        if sem is not None:
                            ins.then_inc(self.sems[sem], step)
            return body
        block.tensor(mk("pe"))
        block.vector(mk("dve"))
        block.scalar(mk("act"))
        block.gpsimd(mk("pool"))
        block.sync(mk("sp"))



def alloc_ln_bufs(nc, pst, tag):
    return {
        "tag": tag,
        "xc": pst.enter_context(nc.sbuf_tensor(tag + "_xc", [128, FT * 512], F32)),
        "sq": pst.enter_context(nc.sbuf_tensor(tag + "_sq", [128, 512], F32)),
        "srow": pst.enter_context(nc.sbuf_tensor(tag + "_srow", [1, 1536], F32)),
        "bc": pst.enter_context(nc.sbuf_tensor(tag + "_bc", [128, 1024], F32)),
        "ps_s": pst.enter_context(nc.psum_tensor(tag + "_pss", [1, 512], F32)),
        "ps_q": pst.enter_context(nc.psum_tensor(tag + "_psq", [1, 512], F32)),
        "ps_b": pst.enter_context(nc.psum_tensor(tag + "_psb", [128, 1024], F32)),
    }


def ln_chunk(k, B, P, ch, rsrc, out_cb, gain=None, bias=None, pre_cb=None):
    tg = B["tag"]
    xc, sq, srow, bc = B["xc"], B["sq"], B["srow"], B["bc"]
    X = lambda ft: (tg + ".x%d" % ft)
    for ft in range(FT):
        k.dma("sp", xc[:, ft * 512:(ft + 1) * 512],
              rsrc[ft * 128:(ft + 1) * 128, ch * 512:(ch + 1) * 512], wr=[X(ft)])
    if pre_cb is not None:
        pre_cb(ch)
    for ft in range(FT):
        xa = xc[:, ft * 512:(ft + 1) * 512]
        k.op("act", lambda e, a=xa: e.activation(sq[:, :], a, AF.Square),
             rd=[X(ft)], wr=[tg + ".sq"])
        k.mm(B["ps_s"][0:1, :], P["onesc"][:, 0:1], xa, ft == 0, ft == FT - 1,
             rd=[X(ft), "onesc"], wr=[tg + ".pss"] if ft == FT - 1 else [],
             war=[tg + ".pss"] if ft == 0 else [], pub=True)
        k.mm(B["ps_q"][0:1, :], P["onesc"][:, 0:1], sq[:, :], ft == 0, ft == FT - 1,
             rd=[tg + ".sq", "onesc"], wr=[tg + ".psq"] if ft == FT - 1 else [],
             war=[tg + ".psq"] if ft == 0 else [], pub=True)
    k.op("dve", lambda e: e.tensor_scalar_mul(srow[0:1, 0:512], B["ps_s"][0:1, :], 1.0 / C),
         rd=[tg + ".pss"], wr=[tg + ".m"])
    k.op("dve", lambda e: e.tensor_scalar_mul(srow[0:1, 512:1024], B["ps_q"][0:1, :], 1.0 / C),
         rd=[tg + ".psq"], wr=[tg + ".q"])
    k.op("dve", lambda e: e.tensor_mul(srow[0:1, 1024:1536], srow[0:1, 0:512], srow[0:1, 0:512]),
         rd=[tg + ".m"], wr=[tg + ".v"])
    k.op("dve", lambda e: e.tensor_sub(srow[0:1, 1024:1536], srow[0:1, 512:1024], srow[0:1, 1024:1536]),
         rd=[tg + ".q", tg + ".v"], wr=[tg + ".v"])
    k.op("dve", lambda e: e.tensor_scalar_add(srow[0:1, 1024:1536], srow[0:1, 1024:1536], EPS),
         rd=[tg + ".v"], wr=[tg + ".v"])
    k.op("act", lambda e: e.activation(srow[0:1, 1024:1536], srow[0:1, 1024:1536], AF.Ln),
         rd=[tg + ".v"], wr=[tg + ".v"])
    k.op("act", lambda e: e.activation(srow[0:1, 512:1024], srow[0:1, 1024:1536], AF.Exp, scale=-0.5),
         rd=[tg + ".v"], wr=[tg + ".q"])
    k.op("dve", lambda e: e.tensor_mul(srow[0:1, 0:512], srow[0:1, 0:512], srow[0:1, 512:1024]),
         rd=[tg + ".q", tg + ".m"], wr=[tg + ".m"])
    k.mm(B["ps_b"][:, 0:512], P["onesr"][0:1, :], srow[0:1, 512:1024], True, True,
         rd=[tg + ".m", tg + ".q", "onesr"], war=[tg + ".psb"], pub=True)
    k.mm(B["ps_b"][:, 512:1024], P["onesr"][0:1, :], srow[0:1, 0:512], True, True,
         rd=[tg + ".m", tg + ".q", "onesr"], wr=[tg + ".psb"], pub=True)
    k.op("act", lambda e: e.copy(bc[:, :], B["ps_b"][:, :]),
         rd=[tg + ".psb"], wr=[tg + ".bc"])
    for ft in range(FT):
        xa = xc[:, ft * 512:(ft + 1) * 512]
        k.op("dve", lambda e, a=xa: e.tensor_mul(a, a, bc[:, 0:512]),
             rd=[X(ft), tg + ".bc"], wr=[X(ft)])
        k.op("dve", lambda e, a=xa: e.tensor_sub(a, a, bc[:, 512:1024]),
             rd=[tg + ".bc"], wr=[X(ft)])
        if gain is not None:
            k.op("dve", lambda e, a=xa, ft=ft: e.tensor_scalar(
                a, a, gain[:, ft:ft + 1], bias[:, ft:ft + 1], ALU.mult, ALU.add),
                rd=["acols"], wr=[X(ft)])
        out_cb(ft, xa)



def build_program(upto="full"):
    nc = bass.Bass("TRN2", target_bir_lowering=False, debug=False)
    dt = nc.dram_tensor

    xfm = dt("xfm", [C, T], F32, kind="ExternalInput").ap()
    xprev = dt("xprev", [128, FT], F32, kind="ExternalInput").ap()
    a_qw = dt("a_qw", [128, FT * C], F32, kind="ExternalInput").ap()
    a_kw = dt("a_kw", [128, FT * C], F32, kind="ExternalInput").ap()
    a_vw = dt("a_vw", [128, FT * C], F32, kind="ExternalInput").ap()
    a_ow = dt("a_ow", [128, FT * C], F32, kind="ExternalInput").ap()
    a_maa_w1 = dt("a_maa_w1", [128, FT * 128], F32, kind="ExternalInput").ap()
    a_maa_w2 = dt("a_maa_w2", [4, 32, C], F32, kind="ExternalInput").ap()
    a_td_w1 = dt("a_td_w1", [128, FT * 64], F32, kind="ExternalInput").ap()
    a_td_w2 = dt("a_td_w2", [64, C], F32, kind="ExternalInput").ap()
    a_cols = dt("a_cols", [128, FT * 12], F32, kind="ExternalInput").ap()
    a_vb_row = dt("a_vb_row", [1, C], F32, kind="ExternalInput").ap()
    b_wqT = dt("b_wqT", [128, FT * C], F32, kind="ExternalInput").ap()
    b_wkT = dt("b_wkT", [128, FT * C], F32, kind="ExternalInput").ap()
    b_wvT = dt("b_wvT", [128, FT * C], F32, kind="ExternalInput").ap()
    b_woT = dt("b_woT", [128, FT * C], F32, kind="ExternalInput").ap()
    b_sw1T = dt("b_sw1T", [128, 128], F32, kind="ExternalInput").ap()
    b_sw2c = dt("b_sw2c", [128, 1], F32, kind="ExternalInput").ap()
    b_sb1c = dt("b_sb1c", [128, 1], F32, kind="ExternalInput").ap()
    b_sb2 = dt("b_sb2", [1, 1], F32, kind="ExternalInput").ap()
    f_wg = dt("f_wg", [NIT, 128, FT * 128], I8, kind="ExternalInput").ap()
    f_wu = dt("f_wu", [NIT, 128, FT * 128], I8, kind="ExternalInput").ap()
    f_wd = dt("f_wd", [FT, 128, NIT * 128], I8, kind="ExternalInput").ap()
    f_sg = dt("f_sg", [128, NIT], F32, kind="ExternalInput").ap()
    f_su = dt("f_su", [128, NIT], F32, kind="ExternalInput").ap()
    f_sd = dt("f_sd", [128, FT], F32, kind="ExternalInput").ap()
    c_eye = dt("c_eye", [128, 128], F32, kind="ExternalInput").ap()
    c_maskD = dt("c_maskD", [128, 64], F32, kind="ExternalInput").ap()
    c_maskS = dt("c_maskS", [128, 64], F32, kind="ExternalInput").ap()
    c_J = dt("c_J", [128, 128], F32, kind="ExternalInput").ap()
    c_Lmask = dt("c_Lmask", [128, 128], F32, kind="ExternalInput").ap()
    c_Hmask = dt("c_Hmask", [128, 128], F32, kind="ExternalInput").ap()
    c_Hsel = dt("c_Hsel", [16, 128], F32, kind="ExternalInput").ap()
    c_iota = dt("c_iota", [128, 512], F32, kind="ExternalInput").ap()
    c_perm = dt("c_perm", [128, 1], U32, kind="ExternalInput").ap()
    c_par = dt("c_par", [128, 1], F32, kind="ExternalInput").ap()

    yout = dt("yfm", [C, T], F32, kind="ExternalOutput").ap()

    ln0 = dt("s_ln0", [C, T], F32).ap()
    xmix = dt("s_xmix", [4, C, T], BF16).ap()
    qa = dt("s_qa", [C, T], BF16).ap()
    kga = dt("s_kga", [C, T], BF16).ap()
    wlog = dt("s_wlog", [C, T], F32).ap()
    agate = dt("s_agate", [C, T], BF16).ap()
    va = dt("s_va", [NPAIR, T, 128], BF16).ap()
    qqg = dt("s_qqg", [C, T], BF16).ap()
    ya = dt("s_ya", [C, T], F32).ap()
    r1 = dt("s_r1", [C, T], F32).ap()
    r2 = dt("s_r2", [C, T], F32).ap()
    r3 = dt("s_r3", [C, T], F32).ap()
    fuse = dt("s_fuse", [INTER, T], BF16).ap()
    qb = dt("s_qb", [C, T], BF16).ap()
    kbt = dt("s_kbt", [HB, T, 128], BF16).ap()
    vbt = dt("s_vbt", [HB, T, 128], BF16).ap()
    dbg = dt("s_dbg", [C, T], F32).ap()
    sfin_t = dt("s_sfin", [NPAIR * 128, 128], F32)
    sfin_cc_t = dt("s_sfin_cc", [2 * NPAIR * 128, 128], F32)
    scl_t = dt("s_scl", [HB, T], F32)
    scl_cc_t = dt("s_scl_cc", [2 * HB, T], F32)
    kvsel_t = dt("s_kvsel", [HB * 256, 128], F32)
    kvsel_cc_t = dt("s_kvsel_cc", [HB * 256, 128], F32)

    D = dict(xfm=xfm, xprev=xprev, a_qw=a_qw, a_kw=a_kw, a_vw=a_vw, a_ow=a_ow,
             a_maa_w1=a_maa_w1, a_maa_w2=a_maa_w2, a_td_w1=a_td_w1, a_td_w2=a_td_w2,
             a_cols=a_cols, a_vb_row=a_vb_row, b_wqT=b_wqT, b_wkT=b_wkT, b_wvT=b_wvT,
             b_woT=b_woT, b_sw1T=b_sw1T, b_sw2c=b_sw2c, b_sb1c=b_sb1c, b_sb2=b_sb2,
             f_wg=f_wg, f_wu=f_wu, f_wd=f_wd, f_sg=f_sg, f_su=f_su, f_sd=f_sd,
             c_eye=c_eye, c_maskD=c_maskD, c_maskS=c_maskS, c_J=c_J, c_Lmask=c_Lmask,
             c_Hmask=c_Hmask, c_Hsel=c_Hsel, c_iota=c_iota, c_perm=c_perm, c_par=c_par,
             yout=yout, ln0=ln0, xmix=xmix, qa=qa, kga=kga, wlog=wlog, agate=agate,
             va=va, qqg=qqg, ya=ya, r1=r1, r2=r2, r3=r3, fuse=fuse, qb=qb, kbt=kbt,
             vbt=vbt, dbg=dbg, sfin_t=sfin_t, sfin_cc_t=sfin_cc_t, scl_t=scl_t,
             scl_cc_t=scl_cc_t, kvsel_t=kvsel_t, kvsel_cc_t=kvsel_cc_t)

    k = K(nc)
    with contextlib.ExitStack() as st:
        for s in SEMS:
            k.sems[s] = st.enter_context(nc.semaphore(s))
        block = st.enter_context(nc.Block())

        P = {}

        def ap(name, shape, dtyp):
            P[name] = st.enter_context(nc.sbuf_tensor(name, shape, dtyp))
            return P[name]

        for name, shape, dtyp in [
                ("ones1", [1, 1], F32), ("onesr", [1, 128], F32), ("onesc", [128, 1], F32),
                ("acols", [128, FT * 12], F32), ("eyeb", [128, 128], BF16),
                ("eyef", [128, 128], F32), ("maskD", [128, 64], F32),
                ("maskS", [128, 64], F32), ("Jb", [128, 128], F32),
                ("Lm", [128, 128], F32), ("Hm", [128, 128], F32),
                ("Hs", [16, 128], F32), ("iotg", [128, 512], F32),
                ("permt", [128, 1], U32), ("parc", [128, 1], F32),
                ("sgc", [128, NIT], F32), ("suc", [128, NIT], F32),
                ("sdc", [128, FT], F32), ("sw1b", [128, 128], BF16),
                ("sw2b", [128, 1], BF16), ("sb1c", [128, 1], F32),
                ("sb2c", [1, 1], F32), ("xprevt", [128, FT], F32),
                ("lnprev", [128, FT], F32), ("vbias", [128, C], F32)]:
            ap(name, shape, dtyp)

        for name, src in [("acols", a_cols), ("Jb", c_J), ("Lm", c_Lmask),
                          ("Hm", c_Hmask), ("iotg", c_iota), ("maskD", c_maskD),
                          ("maskS", c_maskS), ("sgc", f_sg), ("suc", f_su),
                          ("sdc", f_sd), ("sb1c", b_sb1c), ("sb2c", b_sb2),
                          ("xprevt", xprev), ("eyef", c_eye), ("parc", c_par),
                          ("Hs", c_Hsel), ("permt", c_perm)]:
            k.dma("sp", P[name][:, :], src[:, :], wr=[name])
        k.dma("pool", P["eyeb"][:, :], c_eye[:, :], wr=["eyeb"])
        k.dma("pool", P["sw1b"][:, :], b_sw1T[:, :], wr=["sw1b"])
        k.dma("pool", P["sw2b"][:, :], b_sw2c[:, :], wr=["sw2b"])
        k.op("dve", lambda e: e.memset(P["ones1"][:, :], 1.0), wr=["ones1"])
        k.op("dve", lambda e: e.memset(P["onesr"][:, :], 1.0), wr=["onesr"])
        k.op("dve", lambda e: e.memset(P["onesc"][:, :], 1.0), wr=["onesc"])
        with nc.sbuf_tensor("vb_row", [1, C], F32) as vb_row, \
                nc.psum_tensor("vb_ps", [128, 512], F32) as vb_ps:
            k.dma("sp", vb_row[0:1, :], a_vb_row[0:1, :], wr=["vbrow"])
            for chv in range(4):
                k.mm(vb_ps[:, :], P["onesr"][0:1, :], vb_row[0:1, chv * 512:(chv + 1) * 512],
                     True, True, rd=["onesr", "vbrow"], wr=["vbps"], pub=True)
                k.op("act", lambda e, chv=chv: e.copy(P["vbias"][:, chv * 512:(chv + 1) * 512],
                                                      vb_ps[:, :]),
                     rd=["vbps"], wr=["vbias"])
        k.barrier()

        _phase_ln0(k, nc, P, D)
        if upto == "ln0":
            _dbg_copy(k, ln0, yout, F32); k.finalize(); k.replay(block); return nc
        _phase_mix(k, nc, P, D)
        if upto == "mix":
            _dbg_copy(k, xmix[0], yout, BF16); k.finalize(); k.replay(block); return nc
        _phase_proj_a(k, nc, P, D)
        if upto == "proj_a":
            _dbg_copy(k, wlog, yout, F32); k.finalize(); k.replay(block); return nc
        _phase_scan(k, nc, P, D)
        if upto == "scan":
            _dbg_copy(k, dbg if globals().get("_SCAN_DBG") else ya, yout, F32); k.finalize(); k.replay(block); return nc
        _phase_attn_a_out(k, nc, P, D)
        if upto == "attn_a":
            _dbg_copy(k, r1, yout, F32); k.finalize(); k.replay(block); return nc
        _phase_ffn(k, nc, P, D, r1, r2, "F1")
        if upto == "fuse1":
            if globals().get("_FFN_DBG"):
                _dbg_copy(k, dbg, yout, F32)
            else:
                _dbg_copy(k, fuse[0:C, :], yout, BF16)
            k.finalize(); k.replay(block); return nc
        if upto == "ffn1":
            _dbg_copy(k, r2, yout, F32); k.finalize(); k.replay(block); return nc
        _phase_attn_b(k, nc, P, D, upto)
        if upto in ("xb", "scores", "sel", "ksel"):
            if not globals().get("_NO_DBGCOPY"):
                _dbg_copy(k, dbg, yout, F32)
            k.finalize(); k.replay(block); return nc
        if upto == "attn_b":
            _dbg_copy(k, r3, yout, F32); k.finalize(); k.replay(block); return nc
        _phase_ffn(k, nc, P, D, r3, yout, "F2")
        k.finalize()
        k.replay(block)
    return nc


def _dbg_copy(k, src, dstap, sdtype):
    nc = k.nc
    k.barrier()
    with nc.sbuf_tensor("dbg_t", [128, 2048], F32) as tt:
        for ft in range(FT):
            q = "pool" if sdtype != F32 else "sp"
            k.dma(q, tt[:, :], src[ft * 128:(ft + 1) * 128, 0:T], wr=["dbgt"])
            k.dma("sp", dstap[ft * 128:(ft + 1) * 128, 0:T], tt[:, :], rd=["dbgt"])
    k.barrier()


def _phase_ln0(k, nc, P, D):
    with contextlib.ExitStack() as pst:
        B = alloc_ln_bufs(nc, pst, "A")
        for ch in range(4):
            def out_cb(ft, xa, ch=ch):
                k.dma("sp", D["ln0"][ft * 128:(ft + 1) * 128, ch * 512:(ch + 1) * 512],
                      xa, rd=["A.x%d" % ft])
            ln_chunk(k, B, P, ch, D["xfm"], out_cb)
        sq, srow, bc = B["sq"], B["srow"], B["bc"]
        k.op("act", lambda e: e.activation(sq[:, 0:FT], P["xprevt"][:, :], AF.Square),
             rd=["xprevt"], wr=["A.sq"])
        k.op("dve", lambda e: e.reduce_sum(sq[:, 16:17], P["xprevt"][:, :], axis=AX.X),
             rd=["xprevt"], wr=["A.c1"])
        k.op("dve", lambda e: e.reduce_sum(sq[:, 17:18], sq[:, 0:FT], axis=AX.X),
             rd=["A.sq"], wr=["A.c2"])
        k.mm(B["ps_s"][0:1, 0:2], P["onesc"][:, 0:1], sq[:, 16:18], True, True,
             rd=["A.c1", "A.c2", "onesc"], wr=["A.pss"], pub=True)
        k.op("dve", lambda e: e.tensor_scalar_mul(srow[0:1, 0:2], B["ps_s"][0:1, 0:2], 1.0 / C),
             rd=["A.pss"], wr=["A.m"])
        k.op("dve", lambda e: e.tensor_mul(srow[0:1, 2:3], srow[0:1, 0:1], srow[0:1, 0:1]),
             rd=["A.m"], wr=["A.v"])
        k.op("dve", lambda e: e.tensor_sub(srow[0:1, 2:3], srow[0:1, 1:2], srow[0:1, 2:3]),
             rd=["A.m"], wr=["A.v"])
        k.op("dve", lambda e: e.tensor_scalar_add(srow[0:1, 2:3], srow[0:1, 2:3], EPS),
             wr=["A.v"])
        k.op("act", lambda e: e.activation(srow[0:1, 2:3], srow[0:1, 2:3], AF.Ln), wr=["A.v"])
        k.op("act", lambda e: e.activation(srow[0:1, 2:3], srow[0:1, 2:3], AF.Exp, scale=-0.5),
             wr=["A.v"])
        k.op("dve", lambda e: e.tensor_mul(srow[0:1, 3:4], srow[0:1, 0:1], srow[0:1, 2:3]),
             rd=["A.m", "A.v"], wr=["A.mr"])
        k.mm(B["ps_b"][:, 0:2], P["onesr"][0:1, :], srow[0:1, 2:4], True, True,
             rd=["A.v", "A.mr", "onesr"], wr=["A.psb"], pub=True)
        k.op("act", lambda e: e.copy(bc[:, 0:2], B["ps_b"][:, 0:2]), rd=["A.psb"], wr=["A.bc"])
        k.op("dve", lambda e: e.tensor_scalar(P["lnprev"][:, :], P["xprevt"][:, :],
                                              bc[:, 0:1], bc[:, 1:2], ALU.mult, ALU.subtract),
             rd=["A.bc", "xprevt"], wr=["lnprev"])
        k.barrier()


def _phase_mix(k, nc, P, D):
    acols = P["acols"]
    with contextlib.ExitStack() as pst:
        lx = pst.enter_context(nc.sbuf_tensor("M_lx", [128, FT * 513], F32))
        dx = pst.enter_context(nc.sbuf_tensor("M_dx", [128, FT * 512], F32))
        xxxb = pst.enter_context(nc.sbuf_tensor("M_xxx", [128, FT * 512], BF16))
        w1b = pst.enter_context(nc.sbuf_tensor("M_w1", [128, FT * 128], BF16))
        w2b = pst.enter_context(nc.sbuf_tensor("M_w2", [128, FT * 128], BF16))
        mmb = pst.enter_context(nc.sbuf_tensor("M_mm", [128, 512], BF16))
        tmp = pst.enter_context(nc.sbuf_tensor("M_tmp", [128, 512], F32))
        outt = pst.enter_context(nc.sbuf_tensor("M_out", [128, 512], BF16))
        ps_mm = pst.enter_context(nc.psum_tensor("M_psmm", [128, 512], F32))
        ps_m = pst.enter_context(nc.psum_tensor("M_psm", [128, 512], F32))

        k.dma("pool", w1b[:, :], D["a_maa_w1"][:, :], wr=["M.w1"])
        for r in range(4):
            k.dma("pool", w2b[32 * r:32 * (r + 1), :], D["a_maa_w2"][r, :, :],
                  wr=["M.w2_%d" % r])
        for ch in range(4):
            for ft in range(FT):
                base = ft * 513
                if ch == 0:
                    k.dma("sp", lx[:, base + 1:base + 513],
                          D["ln0"][ft * 128:(ft + 1) * 128, 0:512], wr=["M.lx%d" % ft])
                    k.op("dve", lambda e, b=base, ft=ft:
                         e.tensor_copy(lx[:, b:b + 1], P["lnprev"][:, ft:ft + 1]),
                         rd=["lnprev"], wr=["M.lx%d" % ft])
                else:
                    k.dma("sp", lx[:, base:base + 513],
                          D["ln0"][ft * 128:(ft + 1) * 128, ch * 512 - 1:(ch + 1) * 512],
                          wr=["M.lx%d" % ft])
            for ft in range(FT):
                base, o = ft * 513, ft * 512
                k.op("dve", lambda e, b=base, o=o:
                     e.tensor_sub(dx[:, o:o + 512], lx[:, b:b + 512], lx[:, b + 1:b + 513]),
                     rd=["M.lx%d" % ft], wr=["M.dx%d" % ft])
                k.op("dve", lambda e, b=base, o=o, ft=ft: e.scalar_tensor_tensor(
                    tmp[:, :], dx[:, o:o + 512], acols[:, ft:ft + 1], lx[:, b + 1:b + 513],
                    ALU.mult, ALU.add), rd=["M.dx%d" % ft, "M.lx%d" % ft, "acols"], wr=["M.tmp"])
                k.op("dve", lambda e, o=o: e.tensor_copy(xxxb[:, o:o + 512], tmp[:, :]),
                     rd=["M.tmp"], wr=["M.xxx%d" % ft])
            for ft in range(FT):
                k.mm(ps_mm[:, :], w1b[:, ft * 128:(ft + 1) * 128],
                     xxxb[:, ft * 512:(ft + 1) * 512], ft == 0, ft == FT - 1,
                     rd=["M.w1", "M.xxx%d" % ft],
                     wr=["M.psmm"] if ft == FT - 1 else [],
                     war=["M.psmm"] if ft == 0 else [], pub=(ft == FT - 1))
            k.op("act", lambda e: e.activation(mmb[:, :], ps_mm[:, :], AF.Tanh),
                 rd=["M.psmm"], wr=["M.mmb"])
            for r in range(4):
                for ft in range(FT):
                    k.mm(ps_m[:, :], w2b[32 * r:32 * (r + 1), ft * 128:(ft + 1) * 128],
                         mmb[32 * r:32 * (r + 1), :], True, True,
                         rd=["M.w2_%d" % r, "M.mmb"], wr=["M.psm"], pub=True,
                         tp=(32 * r, 0) if r == 3 else None)
                    k.op("dve", lambda e, r=r, ft=ft: e.tensor_scalar_add(
                        tmp[:, :], ps_m[:, :], acols[:, (1 + r) * FT + ft:(1 + r) * FT + ft + 1]),
                        rd=["M.psm", "acols"], wr=["M.tmp"])
                    k.op("dve", lambda e, o=ft * 512: e.tensor_mul(tmp[:, :], tmp[:, :], dx[:, o:o + 512]),
                         rd=["M.dx%d" % ft], wr=["M.tmp"])
                    k.op("dve", lambda e, b=ft * 513: e.tensor_add(
                        outt[:, :], tmp[:, :], lx[:, b + 1:b + 513]),
                        rd=["M.tmp", "M.lx%d" % ft], wr=["M.out"])
                    k.dma("sp", D["xmix"][r, ft * 128:(ft + 1) * 128, ch * 512:(ch + 1) * 512],
                          outt[:, :], rd=["M.out"])
        k.barrier()


def _phase_proj_a(k, nc, P, D):
    acols = P["acols"]
    with contextlib.ExitStack() as pst:
        xs = pst.enter_context(nc.sbuf_tensor("P_x", [128, FT * T], BF16))
        wb = pst.enter_context(nc.sbuf_tensor("P_w", [128, FT * C], BF16))
        ob = pst.enter_context(nc.sbuf_tensor("P_ob", [128, 512], BF16))
        o2 = pst.enter_context(nc.sbuf_tensor("P_o2", [128, 512], BF16))
        of = pst.enter_context(nc.sbuf_tensor("P_of", [128, 512], F32))
        gt = pst.enter_context(nc.sbuf_tensor("P_g", [128, 512], BF16))
        gf = pst.enter_context(nc.sbuf_tensor("P_gf", [128, 512], F32))
        td1b = pst.enter_context(nc.sbuf_tensor("P_td1", [128, FT * 64], BF16))
        td2b = pst.enter_context(nc.sbuf_tensor("P_td2", [64, C], BF16))
        tdb = pst.enter_context(nc.sbuf_tensor("P_tdb", [64, 512], BF16))
        ps = [pst.enter_context(nc.psum_tensor(f"P_ps{i}", [128, 512], F32)) for i in range(2)]
        ps_td = pst.enter_context(nc.psum_tensor("P_pstd", [64, 512], F32))

        def load_plane(r):
            for ft in range(FT):
                k.dma("pool", xs[:, ft * T:(ft + 1) * T],
                      D["xmix"][r, ft * 128:(ft + 1) * 128, :], wr=["P.x%d" % ft])

        load_plane(3)
        k.dma("pool", td1b[:, :], D["a_td_w1"][:, :], wr=["P.td1"])
        k.dma("pool", td2b[:, :], D["a_td_w2"][:, :], wr=["P.td2"])
        for ch in range(4):
            for ft in range(FT):
                k.mm(ps_td[:, :], td1b[:, ft * 64:(ft + 1) * 64],
                     xs[:, ft * T + ch * 512:ft * T + (ch + 1) * 512], ft == 0, ft == FT - 1,
                     rd=["P.td1", "P.x%d" % ft], wr=["P.pstd"] if ft == FT - 1 else [],
                     war=["P.pstd"] if ft == 0 else [], pub=(ft == FT - 1))
            k.op("act", lambda e: e.activation(tdb[:, :], ps_td[:, :], AF.Tanh),
                 rd=["P.pstd"], wr=["P.tdb"])
            for ft2 in range(FT):
                b = ft2 % 2
                k.mm(ps[b][:, :], td2b[0:64, ft2 * 128:(ft2 + 1) * 128], tdb[:, :], True, True,
                     rd=["P.td2", "P.tdb"], wr=["P.ps%d" % b], pub=True)
                k.op("act", lambda e, b=b, ft2=ft2: e.activation(
                    of[:, :], ps[b][:, :], AF.Exp,
                    bias=acols[:, 8 * FT + ft2:8 * FT + ft2 + 1]),
                    rd=["P.ps%d" % b, "acols"], wr=["P.of"])
                k.op("dve", lambda e: e.tensor_scalar(of[:, :], of[:, :], -1.0, -5.0,
                                                      ALU.mult, ALU.max), wr=["P.of"])
                k.dma("sp", D["wlog"][ft2 * 128:(ft2 + 1) * 128, ch * 512:(ch + 1) * 512],
                      of[:, :], rd=["P.of"])
                k.op("act", lambda e: e.activation(gf[:, :], of[:, :], AF.Exp),
                     rd=["P.of"], wr=["P.gf"])
                k.op("dve", lambda e: e.tensor_scalar(gt[:, :], gf[:, :], -1.0, 1.0,
                                                      ALU.mult, ALU.add),
                     rd=["P.gf"], wr=["P.gt"])
                k.dma("sp", D["agate"][ft2 * 128:(ft2 + 1) * 128, ch * 512:(ch + 1) * 512],
                      gt[:, :], rd=["P.gt"])
        k.barrier()

        load_plane(0)
        k.dma("pool", wb[:, :], D["a_qw"][:, :], wr=["P.wb"])
        for ft2 in range(FT):
            for ch in range(4):
                b = (ft2 * 4 + ch) % 2
                for ft in range(FT):
                    k.mm(ps[b][:, :], wb[:, ft * C + ft2 * 128:ft * C + (ft2 + 1) * 128],
                         xs[:, ft * T + ch * 512:ft * T + (ch + 1) * 512],
                         ft == 0, ft == FT - 1, rd=["P.wb", "P.x%d" % ft],
                         wr=["P.ps%d" % b] if ft == FT - 1 else [],
                         war=["P.ps%d" % b] if ft == 0 else [], pub=(ft == FT - 1))
                k.op("act", lambda e, b=b, ft2=ft2: e.activation(
                    ob[:, :], ps[b][:, :], AF.Identity,
                    bias=acols[:, 5 * FT + ft2:5 * FT + ft2 + 1]),
                    rd=["P.ps%d" % b, "acols"], wr=["P.obf"])
                k.dma("sp", D["qa"][ft2 * 128:(ft2 + 1) * 128, ch * 512:(ch + 1) * 512],
                      ob[:, :], rd=["P.obf"])
        k.barrier()

        load_plane(1)
        k.dma("pool", wb[:, :], D["a_kw"][:, :], wr=["P.wb"])
        for ft2 in range(FT):
            for ch in range(4):
                b = (ft2 * 4 + ch) % 2
                for ft in range(FT):
                    k.mm(ps[b][:, :], wb[:, ft * C + ft2 * 128:ft * C + (ft2 + 1) * 128],
                         xs[:, ft * T + ch * 512:ft * T + (ch + 1) * 512],
                         ft == 0, ft == FT - 1, rd=["P.wb", "P.x%d" % ft],
                         wr=["P.ps%d" % b] if ft == FT - 1 else [],
                         war=["P.ps%d" % b] if ft == 0 else [], pub=(ft == FT - 1))
                k.op("act", lambda e, b=b, ft2=ft2: e.activation(
                    of[:, :], ps[b][:, :], AF.Identity,
                    bias=acols[:, 6 * FT + ft2:6 * FT + ft2 + 1]),
                    rd=["P.ps%d" % b, "acols"], wr=["P.of"])
                k.dma("pool", gt[:, :],
                      D["agate"][ft2 * 128:(ft2 + 1) * 128, ch * 512:(ch + 1) * 512],
                      wr=["P.gt"])
                k.op("dve", lambda e: e.tensor_mul(ob[:, :], of[:, :], gt[:, :]),
                     rd=["P.of", "P.gt"], wr=["P.obf"])
                k.dma("sp", D["kga"][ft2 * 128:(ft2 + 1) * 128, ch * 512:(ch + 1) * 512],
                      ob[:, :], rd=["P.obf"])
        k.barrier()

        load_plane(2)
        k.dma("pool", wb[:, :], D["a_vw"][:, :], wr=["P.wb"])
        for tt in range(FT):
            for dv in range(4):
                b = (tt * 4 + dv) % 2
                for ft in range(FT):
                    k.mm(ps[b][:, :], xs[:, ft * T + tt * 128:ft * T + (tt + 1) * 128],
                         wb[:, ft * C + dv * 512:ft * C + (dv + 1) * 512],
                         ft == 0, ft == FT - 1, rd=["P.wb", "P.x%d" % ft],
                         wr=["P.ps%d" % b] if ft == FT - 1 else [],
                         war=["P.ps%d" % b] if ft == 0 else [], pub=(ft == FT - 1))
                k.op("dve", lambda e, b=b, dv=dv: e.tensor_add(
                    of[:, :], ps[b][:, :], P["vbias"][:, dv * 512:(dv + 1) * 512]),
                    rd=["P.ps%d" % b, "vbias"], wr=["P.of"])
                k.op("dve", lambda e: e.tensor_copy(o2[:, :], of[:, :]),
                     rd=["P.of"], wr=["P.o2"])
                for sl in range(4):
                    p = dv * 4 + sl
                    k.dma("sp", D["va"][p, tt * 128:(tt + 1) * 128, :],
                          o2[:, sl * 128:(sl + 1) * 128], rd=["P.o2"])
        k.barrier()


def _phase_scan(k, nc, P, D):
    sfin = D["sfin_t"].ap()
    with contextlib.ExitStack() as pst:
        def sb(name, shape, dtyp):
            return pst.enter_context(nc.sbuf_tensor(name, shape, dtyp))
        qs2 = [sb("S_q0", [128, T], BF16), sb("S_q1", [128, T], BF16)]
        ks2 = [sb("S_k0", [128, T], BF16), sb("S_k1", [128, T], BF16)]
        ws2 = [sb("S_w0", [128, T], F32), sb("S_w1", [128, T], F32)]
        Bg = sb("S_Bg", [128, T], F32)
        zs = sb("S_z", [128, T], F32)
        ys = sb("S_y", [128, T], F32)
        qq = sb("S_qq", [128, T], BF16)
        eBg = sb("S_eBg", [128, T], F32)
        vbd2 = [sb("S_vbd0", [128, NCH * 128], BF16), sb("S_vbd1", [128, NCH * 128], BF16)]
        vsl2 = [sb("S_vsl0", [64, NCH * 128], BF16), sb("S_vsl1", [64, NCH * 128], BF16)]
        argQ = sb("S_aq", [128, 64], F32)
        argH = sb("S_ah", [128, 64], F32)
        argM = sb("S_am", [128, 64], F32)
        argG = sb("S_ag", [128, 129], F32)
        eQ = sb("S_eq", [128, 64], F32)
        eK = sb("S_ek", [128, 64], F32)
        eH = sb("S_eh", [128, 64], F32)
        eM = sb("S_em", [128, 64], F32)
        eG = sb("S_eg", [128, 129], F32)
        qcat = sb("S_qcat", [128, 64], BF16)
        lhsKT = sb("S_lkt", [128, 128], BF16)
        lhsKH = sb("S_lkh", [128, 128], BF16)
        lhsKG = sb("S_lkg", [128, 128], BF16)
        qgz = sb("S_qgz", [128, 64], BF16)
        qglob = sb("S_qgl", [128, 64], BF16)
        kglob = sb("S_kgl", [128, 64], BF16)
        khT = sb("S_khT", [64, 128], BF16)
        ptf = sb("S_ptf", [128, 64], F32)
        ptf2 = sb("S_ptf2", [128, 64], F32)
        PTb = sb("S_PTb", [128, 64], BF16)
        Sf = sb("S_Sf", [128, 128], F32)
        Sbd = sb("S_Sbd", [128, 128], BF16)
        psD = pst.enter_context(nc.psum_tensor("S_psD", [128, 64], F32))
        psS2 = pst.enter_context(nc.psum_tensor("S_psS", [128, 64], F32))
        psQ = pst.enter_context(nc.psum_tensor("S_psQ", [128, 64], F32))
        psO = pst.enter_context(nc.psum_tensor("S_psO", [128, 64], F32))
        psT = pst.enter_context(nc.psum_tensor("S_psT", [64, 128], BF16))
        psS = pst.enter_context(nc.psum_tensor("S_psSt", [128, 128], F32))

        for t_ in (lhsKT, lhsKH, lhsKG, qgz, Sbd):
            k.op("dve", lambda e, t=t_: e.memset(t[:, :], 0.0),
                 wr=["S.init"])
        for _vb in vbd2:
            k.op("dve", lambda e, v=_vb: e.memset(
                v[0:64, :].rearrange("s (c d) -> s c d", d=128)[:, :, 64:128], 0.0),
                wr=["S.init"])
            k.op("dve", lambda e, v=_vb: e.memset(
                v[64:128, :].rearrange("s (c d) -> s c d", d=128)[:, :, 0:64], 0.0),
                wr=["S.init"])
        k.op("dve", lambda e: e.memset(zs[:, :], 0.0), wr=["S.z"])
        k.barrier()

        zcol = zs[:, 0:1]
        for p in range(NPAIR):
            r0 = p * 128
            bi = p % 2
            qs, ks, ws, vbd, vsl = qs2[bi], ks2[bi], ws2[bi], vbd2[bi], vsl2[bi]
            KQ, KK, KW = "S.qs%d" % bi, "S.ks%d" % bi, "S.ws%d" % bi
            KVB, KVS = "S.vbd%d" % bi, "S.vsl%d" % bi
            k.dma("pool", qs[:, :], D["qa"][r0:r0 + 128, :], wr=[KQ])
            k.dma("pool", ks[:, :], D["kga"][r0:r0 + 128, :], wr=[KK])
            k.dma("sp", ws[:, :], D["wlog"][r0:r0 + 128, :], wr=[KW])
            k.dma("sp", vbd[0:64, :].rearrange("s (c d) -> s c d", d=128)[:, :, 0:64],
                  D["va"][p].rearrange("(c s) d -> s c d", s=64, c=32)[:, :, 0:64], wr=[KVB])
            k.dma("sp", vbd[64:128, :].rearrange("s (c d) -> s c d", d=128)[:, :, 64:128],
                  D["va"][p].rearrange("(c s) d -> s c d", s=64, c=32)[:, :, 64:128], wr=[KVB])
            k.dma("sp", vsl[0:64, :].rearrange("s (c d) -> s c d", d=128),
                  D["va"][p].rearrange("(c s) d -> s c d", s=64, c=32), wr=[KVS])
            k.op("dve", lambda e, ws=ws: e.tensor_tensor_scan(
                Bg[:, :], ws[:, :], zs[:, :], 0.0, ALU.add, ALU.add),
                rd=[KW, "S.z"], wr=["S.Bg"])
            k.op("act", lambda e: e.activation(eBg[:, :], Bg[:, :], AF.Exp),
                 rd=["S.Bg"], wr=["S.eBg"])
            k.op("dve", lambda e, qs=qs: e.scalar_tensor_tensor(
                qq[:, :], qs[:, :], SCL_A, eBg[:, :], ALU.mult, ALU.mult),
                rd=["S.eBg", KQ], wr=["S.qqb"])
            k.dma("sp", D["qqg"][r0:r0 + 128, :], qq[:, :], rd=["S.qqb"])
            k.op("dve", lambda e: e.memset(Sf[:, :], 0.0), wr=["S.Sf"])
            k.op("dve", lambda e: e.memset(Sbd[:, :], 0.0), wr=["S.Sbd"])

            for c in range(NCH):
                c0 = c * LCH
                scol = Bg[:, c0 - 1:c0] if c > 0 else zcol
                mcol = Bg[:, c0 + 31:c0 + 32]
                ecol = Bg[:, c0 + 63:c0 + 64]
                for i in range(4):
                    bcol = Bg[:, c0 + 16 * i - 1:c0 + 16 * i] if (c0 + 16 * i) > 0 else zcol
                    ecj = Bg[:, c0 + 16 * (i + 1) - 1:c0 + 16 * (i + 1)]
                    k.op("dve", lambda e, i=i, bcol=bcol, c0=c0: e.tensor_scalar_sub(
                        argQ[:, 16 * i:16 * (i + 1)], Bg[:, c0 + 16 * i:c0 + 16 * (i + 1)], bcol),
                        rd=["S.Bg", "S.z"], wr=["S.argQ"] if i == 3 else ["S.argQp"])
                    k.op("dve", lambda e, i=i, ecj=ecj, c0=c0: e.tensor_scalar_sub(
                        argH[:, 16 * i:16 * (i + 1)], Bg[:, c0 + 16 * i:c0 + 16 * (i + 1)], ecj),
                        rd=["S.Bg"], wr=["S.argH"] if i == 3 else ["S.argHp"])
                k.op("dve", lambda e, c0=c0, mcol=mcol: e.tensor_scalar_sub(
                    argM[:, :], Bg[:, c0:c0 + 64], mcol), rd=["S.Bg"], wr=["S.argM"])
                k.op("dve", lambda e: e.tensor_scalar_mul(argM[:, 0:32], argM[:, 0:32], -1.0),
                     wr=["S.argM"])
                k.op("dve", lambda e, c0=c0, scol=scol: e.tensor_scalar_sub(
                    argG[:, 0:64], Bg[:, c0:c0 + 64], scol), rd=["S.Bg", "S.z"], wr=["S.argG1"])
                k.op("dve", lambda e, c0=c0, ecol=ecol: e.tensor_scalar_sub(
                    argG[:, 64:128], Bg[:, c0:c0 + 64], ecol), rd=["S.Bg"], wr=["S.argG2"])
                k.op("dve", lambda e: e.tensor_scalar_mul(argG[:, 64:128], argG[:, 64:128], -1.0),
                     wr=["S.argG2"])
                k.op("dve", lambda e, scol=scol, ecol=ecol: e.tensor_scalar_sub(
                    argG[:, 128:129], ecol, scol), rd=["S.Bg", "S.z"], wr=["S.argG3"])
                k.op("act", lambda e: e.activation(eQ[:, :], argQ[:, :], AF.Exp),
                     rd=["S.argQ", "S.argQp"], wr=["S.eQ"])
                k.op("act", lambda e: e.activation(eK[:, :], argQ[:, :], AF.Exp, scale=-1.0),
                     rd=["S.argQ", "S.argQp"], wr=["S.eK"])
                k.op("act", lambda e: e.activation(eH[:, :], argH[:, :], AF.Exp, scale=-1.0),
                     rd=["S.argH", "S.argHp"], wr=["S.eH"])
                k.op("act", lambda e: e.activation(eM[:, :], argM[:, :], AF.Exp),
                     rd=["S.argM"], wr=["S.eM"])
                k.op("act", lambda e: e.activation(eG[:, :], argG[:, :], AF.Exp),
                     rd=["S.argG1", "S.argG2", "S.argG3"], wr=["S.eG"])
                k.op("dve", lambda e, c0=c0, qs=qs, ks=ks: e.tensor_mul(qcat[:, :], qs[:, c0:c0 + 64], eQ[:, :]),
                     rd=["S.eQ", KQ], wr=["S.qcat"])
                k.op("dve", lambda e, c0=c0, qs=qs, ks=ks: e.tensor_mul(
                    lhsKT[0:64, 0:64], ks[0:64, c0:c0 + 64], eK[0:64, :]),
                    rd=["S.eK", KK], wr=["S.lKT"])
                k.op("dve", lambda e, c0=c0, qs=qs, ks=ks: e.tensor_mul(
                    lhsKT[64:128, 64:128], ks[64:128, c0:c0 + 64], eK[64:128, :]),
                    rd=["S.eK", KK], wr=["S.lKT"])
                k.op("dve", lambda e, c0=c0, qs=qs, ks=ks: e.tensor_mul(
                    lhsKH[0:64, 0:64], ks[0:64, c0:c0 + 64], eH[0:64, :]),
                    rd=["S.eH", KK], wr=["S.lKH"])
                k.op("dve", lambda e, c0=c0, qs=qs, ks=ks: e.tensor_mul(
                    lhsKH[64:128, 64:128], ks[64:128, c0:c0 + 64], eH[64:128, :]),
                    rd=["S.eH", KK], wr=["S.lKH"])
                k.op("dve", lambda e, c0=c0, qs=qs, ks=ks: e.tensor_mul(
                    lhsKG[0:64, 0:32], ks[0:64, c0:c0 + 32], eM[0:64, 0:32]),
                    rd=["S.eM", KK], wr=["S.lKG"])
                k.op("dve", lambda e, c0=c0, qs=qs, ks=ks: e.tensor_mul(
                    lhsKG[64:128, 64:96], ks[64:128, c0:c0 + 32], eM[64:128, 0:32]),
                    rd=["S.eM", KK], wr=["S.lKG"])
                k.op("dve", lambda e, c0=c0, qs=qs, ks=ks: e.tensor_mul(
                    qgz[:, 32:64], qs[:, c0 + 32:c0 + 64], eM[:, 32:64]),
                    rd=["S.eM", KQ], wr=["S.qgz"])
                k.op("dve", lambda e, c0=c0, qs=qs, ks=ks: e.tensor_mul(
                    qglob[:, :], qs[:, c0:c0 + 64], eG[:, 0:64]),
                    rd=["S.eG", KQ], wr=["S.qglob"])
                k.op("dve", lambda e, c0=c0, qs=qs, ks=ks: e.tensor_mul(
                    kglob[:, :], ks[:, c0:c0 + 64], eG[:, 64:128]),
                    rd=["S.eG", KK], wr=["S.kglob"])
                k.mm(psD[:, :], lhsKT[:, :], qcat[:, :], True, True,
                     rd=["S.lKT", "S.qcat"], wr=["S.psD"], pub=True)
                k.mm(psS2[:, :], lhsKH[:, :], qcat[:, :], True, True,
                     rd=["S.lKH", "S.qcat"], wr=["S.psS2"], pub=True)
                k.mm(psQ[:, :], lhsKG[:, :], qgz[:, :], True, True,
                     rd=["S.lKG", "S.qgz"], wr=["S.psQ"], pub=True)
                k.op("dve", lambda e: e.tensor_mul(ptf[:, :], psD[:, :], P["maskD"][:, :]),
                     rd=["S.psD", "maskD"], wr=["S.ptf"])
                k.op("dve", lambda e: e.tensor_mul(ptf2[:, :], psS2[:, :], P["maskS"][:, :]),
                     rd=["S.psS2", "maskS"], wr=["S.ptf2"])
                k.op("dve", lambda e: e.tensor_add(ptf[:, :], ptf[:, :], ptf2[:, :]),
                     rd=["S.ptf2"], wr=["S.ptf"])
                k.op("dve", lambda e: e.tensor_add(PTb[:, :], ptf[:, :], psQ[:, :]),
                     rd=["S.ptf", "S.psQ"], wr=["S.PTb"])
                k.mm(psO[:, :], Sbd[:, :], qglob[:, :], True, False,
                     rd=["S.Sbd", "S.qglob"], wr=[], pub=False)
                k.mm(psO[:, :], vbd[:, c * 128:(c + 1) * 128], PTb[:, :], False, True,
                     rd=[KVB, "S.PTb"], wr=["S.psO"], pub=True)
                k.op("act", lambda e, c0=c0, qs=qs, ks=ks: e.activation(
                    ys[:, c0:c0 + 64], psO[:, :], AF.Copy, scale=SCL_A),
                    rd=["S.psO"], wr=["S.ys"])
                k.op("pe", lambda e: e.transpose(psT[:, :], kglob[:, :], P["eyeb"][:, :]),
                     rd=["S.kglob", "eyeb"], wr=["S.psT"], pub=True)
                k.op("act", lambda e: e.copy(khT[:, :], psT[:, :]),
                     rd=["S.psT"], wr=["S.khT"])
                k.mm(psS[:, :], khT[0:64, :], vsl[0:64, c * 128:(c + 1) * 128], True, True,
                     rd=["S.khT", KVS], wr=["S.psS"], pub=True)
                k.op("dve", lambda e: e.tensor_scalar_mul(Sf[:, :], Sf[:, :], eG[:, 128:129]),
                     rd=["S.eG"], wr=["S.Sf"])
                k.op("dve", lambda e: e.tensor_add(Sf[0:64, 0:64], Sf[0:64, 0:64], psS[0:64, 0:64]),
                     rd=["S.psS"], wr=["S.Sf"])
                k.op("dve", lambda e: e.tensor_add(Sf[64:128, 64:128], Sf[64:128, 64:128],
                                                   psS[64:128, 64:128]),
                     rd=["S.psS"], wr=["S.Sf"])
                k.op("dve", lambda e: e.tensor_copy(Sbd[0:64, 0:64], Sf[0:64, 0:64]),
                     rd=["S.Sf"], wr=["S.Sbd"])
                k.op("dve", lambda e: e.tensor_copy(Sbd[64:128, 64:128], Sf[64:128, 64:128]),
                     rd=["S.Sf"], wr=["S.Sbd"])
                if globals().get("_SCAN_DBG") and p == 0 and c in (0, 1):
                    dbg = D["dbg"]
                    k.dma("sp", dbg[0:128, 256 * c:256 * c + 64], Bg[:, c * 64:(c + 1) * 64],
                          rd=["S.Bg"])
                    k.dma("sp", dbg[0:128, 256 * c + 64:256 * c + 128], ys[:, c * 64:(c + 1) * 64],
                          rd=["S.ys"])
                    k.dma("pool", dbg[128:256, 256 * c:256 * c + 64], PTb[:, :], rd=["S.PTb"])
                    k.dma("pool", dbg[128:256, 256 * c + 64:256 * c + 128], qglob[:, :],
                          rd=["S.qglob"])
                    k.dma("sp", dbg[256:384, 128 * c:128 * (c + 1)], Sf[:, :], rd=["S.Sf"])
                    k.dma("pool", dbg[384:512, 256 * c:256 * c + 64], kglob[:, :], rd=["S.kglob"])
                    k.dma("pool", dbg[384:448, 256 * c + 64:256 * c + 192],
                          khT[0:64, :], rd=["S.khT"])
                    k.dma("pool", dbg[512:640, 256 * c:256 * c + 64], qcat[:, :], rd=["S.qcat"])
                    k.dma("pool", dbg[512:640, 256 * c + 64:256 * c + 192], lhsKT[:, :],
                          rd=["S.lKT"])
                    k.dma("sp", dbg[640:768, 128 * c:128 * (c + 1)], eG[:, 0:128], rd=["S.eG"])
            k.dma("sp", D["ya"][r0:r0 + 128, :], ys[:, :], rd=["S.ys"])
            k.dma("sp", sfin[r0:r0 + 128, :], Sf[:, :], rd=["S.Sf"])
        k.barrier()


def _phase_attn_a_out(k, nc, P, D):
    sfin_cc = D["sfin_cc_t"].ap()
    k.emit("pool", lambda e: e.collective_compute(
        "AllGather", ALU.bypass, replica_groups=[[0, 1], [2, 3], [4, 5], [6, 7]],
        ins=[D["sfin_t"].ap().opt()], outs=[D["sfin_cc_t"].ap().opt()]),
        sem="cc0", inc=1)
    for e in ENGS:
        k._wait(e, "cc0", 1)

    with contextlib.ExitStack() as pst:
        B = alloc_ln_bufs(nc, pst, "Y")
        Sin = pst.enter_context(nc.sbuf_tensor("Y_Sin", [128, NPAIR * 128], BF16))
        Sf32 = pst.enter_context(nc.sbuf_tensor("Y_Sf32", [128, 128], F32))
        qqc = pst.enter_context(nc.sbuf_tensor("Y_qq", [128, 512], BF16))
        ynb = pst.enter_context(nc.sbuf_tensor("Y_ynb", [128, FT * 512], BF16))
        wo = pst.enter_context(nc.sbuf_tensor("Y_wo", [128, FT * C], BF16))
        rx = pst.enter_context(nc.sbuf_tensor("Y_rx", [128, 512], F32))
        ro = pst.enter_context(nc.sbuf_tensor("Y_ro", [128, 512], F32))
        psC = pst.enter_context(nc.psum_tensor("Y_psC", [128, 512], F32))
        psP = [pst.enter_context(nc.psum_tensor(f"Y_psP{i}", [128, 512], F32)) for i in range(2)]

        k.dma("pool", wo[:, :], D["a_ow"][:, :], wr=["Y.wo"])
        for p in range(NPAIR):
            k.dma("sp", Sf32[:, :], sfin_cc[p * 128:(p + 1) * 128, :], wr=["Y.Sf32"])
            k.op("dve", lambda e, p=p: e.tensor_scalar_mul(
                Sin[:, p * 128:(p + 1) * 128], Sf32[:, :], P["parc"][:, 0:1]),
                rd=["Y.Sf32", "parc"], wr=["Y.Sin"])

        for ch in range(4):
            def pre(ch=ch):
                for p in range(NPAIR):
                    k.dma("pool", qqc[:, :],
                          D["qqg"][p * 128:(p + 1) * 128, ch * 512:(ch + 1) * 512],
                          wr=["Y.qqc"])
                    k.mm(psC[:, :], Sin[:, p * 128:(p + 1) * 128], qqc[:, :], True, True,
                         rd=["Y.Sin", "Y.qqc"], wr=["Y.psC"], pub=True)
                    k.op("dve", lambda e, p=p: e.tensor_add(
                        B["xc"][:, p * 512:(p + 1) * 512], B["xc"][:, p * 512:(p + 1) * 512],
                        psC[:, :]), rd=["Y.psC"], wr=["Y.x%d" % p])

            def out_cb(ft, xa):
                k.op("dve", lambda e, a=xa, ft=ft: e.tensor_copy(
                    ynb[:, ft * 512:(ft + 1) * 512], a), rd=["Y.x%d" % ft], wr=["Y.yn%d" % ft])

            ln_chunk(k, B, P, ch, D["ya"], out_cb,
                     gain=P["acols"][:, 9 * FT:10 * FT], bias=P["acols"][:, 10 * FT:11 * FT],
                     pre_cb=lambda _ch: pre())
            for ft2 in range(FT):
                b = ft2 % 2
                for ft in range(FT):
                    k.mm(psP[b][:, :], wo[:, ft * C + ft2 * 128:ft * C + (ft2 + 1) * 128],
                         ynb[:, ft * 512:(ft + 1) * 512], ft == 0, ft == FT - 1,
                         rd=["Y.wo", "Y.yn%d" % ft],
                         wr=["Y.psP%d" % b] if ft == FT - 1 else [],
                         war=["Y.psP%d" % b] if ft == 0 else [], pub=(ft == FT - 1))
                k.dma("sp", rx[:, :], D["xfm"][ft2 * 128:(ft2 + 1) * 128, ch * 512:(ch + 1) * 512],
                      wr=["Y.rx"])
                k.op("dve", lambda e, b=b: e.tensor_add(ro[:, :], psP[b][:, :], rx[:, :]),
                     rd=["Y.psP%d" % b, "Y.rx"], wr=["Y.ro"])
                k.dma("sp", D["r1"][ft2 * 128:(ft2 + 1) * 128, ch * 512:(ch + 1) * 512],
                      ro[:, :], rd=["Y.ro"])
        k.barrier()


def _phase_ffn(k, nc, P, D, rin, rout, tag):
    with contextlib.ExitStack() as pst:
        B = alloc_ln_bufs(nc, pst, tag)
        xln = pst.enter_context(nc.sbuf_tensor(tag + "_xln", [128, FT * T], BF16))
        wgb = [pst.enter_context(nc.sbuf_tensor(f"{tag}_wg{i}", [128, FT * 128], BF16))
               for i in range(2)]
        wub = [pst.enter_context(nc.sbuf_tensor(f"{tag}_wu{i}", [128, FT * 128], BF16))
               for i in range(2)]
        gbt = pst.enter_context(nc.sbuf_tensor(tag + "_gb", [128, T], BF16))
        ubt = pst.enter_context(nc.sbuf_tensor(tag + "_ub", [128, 512], BF16))
        fbt = pst.enter_context(nc.sbuf_tensor(tag + "_fb", [128, T], BF16))
        psG = [pst.enter_context(nc.psum_tensor(f"{tag}_psG{i}", [128, 512], F32))
               for i in range(2)]
        psU = [pst.enter_context(nc.psum_tensor(f"{tag}_psU{i}", [128, 512], F32))
               for i in range(2)]

        for ch in range(4):
            def out_cb(ft, xa, ch=ch):
                k.op("dve", lambda e, a=xa, ft=ft, ch=ch: e.tensor_copy(
                    xln[:, ft * T + ch * 512:ft * T + (ch + 1) * 512], a),
                    rd=[tag + ".x%d" % ft], wr=[tag + ".xln%d" % ft])
            ln_chunk(k, B, P, ch, rin, out_cb)

        if globals().get("_FFN_DBG") == "xln" and tag == "F1":
            for ft in range(FT):
                k.dma("pool", D["dbg"][ft * 128:(ft + 1) * 128, :],
                      xln[:, ft * T:(ft + 1) * T], rd=[tag + ".xln%d" % ft])
        for it in range(NIT):
            b = it % 2
            k.dma("pool", wgb[b][:, :], D["f_wg"][it, :, :], wr=[tag + ".wg%d" % b])
            k.dma("pool", wub[b][:, :], D["f_wu"][it, :, :], wr=[tag + ".wu%d" % b])
            if globals().get("_FFN_DBG") == "wslab" and tag == "F1" and it < FT:
                k.dma("pool", D["dbg"][it * 128:(it + 1) * 128, :], wgb[b][:, :],
                      rd=[tag + ".wg%d" % b])
            for ch in range(4):
                for ft in range(FT):
                    k.mm(psG[b][:, :], wgb[b][:, ft * 128:(ft + 1) * 128],
                         xln[:, ft * T + ch * 512:ft * T + (ch + 1) * 512],
                         ft == 0, ft == FT - 1,
                         rd=[tag + ".wg%d" % b, tag + ".xln%d" % ft],
                         wr=[tag + ".psG%d" % b] if ft == FT - 1 else [],
                         war=[tag + ".psG%d" % b] if ft == 0 else [], pub=(ft == FT - 1))
                    k.mm(psU[b][:, :], wub[b][:, ft * 128:(ft + 1) * 128],
                         xln[:, ft * T + ch * 512:ft * T + (ch + 1) * 512],
                         ft == 0, ft == FT - 1,
                         rd=[tag + ".wu%d" % b, tag + ".xln%d" % ft],
                         wr=[tag + ".psU%d" % b] if ft == FT - 1 else [],
                         war=[tag + ".psU%d" % b] if ft == 0 else [], pub=(ft == FT - 1))
                if globals().get("_FFN_DBG") == "graw" and tag == "F1" and it < FT:
                    k.op("act", lambda e, pg=psG[b]: e.copy(ubt[:, :], pg[:, :]),
                         rd=[tag + ".psG%d" % b], wr=[tag + ".ub"])
                    k.dma("pool", D["dbg"][it * 128:(it + 1) * 128, ch * 512:(ch + 1) * 512],
                          ubt[:, :], rd=[tag + ".ub"])
                k.op("act", lambda e, ch=ch, it=it, pg=psG[b]: e.activation(
                    gbt[:, ch * 512:(ch + 1) * 512], pg[:, :], AF.Silu,
                    scale=P["sgc"][:, it:it + 1]),
                    rd=[tag + ".psG%d" % b, "sgc"], wr=[tag + ".gb%d" % ch])
                k.op("act", lambda e, it=it, pu=psU[b]: e.activation(
                    ubt[:, :], pu[:, :], AF.Copy, scale=P["suc"][:, it:it + 1]),
                    rd=[tag + ".psU%d" % b, "suc"], wr=[tag + ".ub"])
                k.op("dve", lambda e, ch=ch: e.tensor_mul(
                    fbt[:, ch * 512:(ch + 1) * 512], gbt[:, ch * 512:(ch + 1) * 512], ubt[:, :]),
                    rd=[tag + ".gb%d" % ch, tag + ".ub"], wr=[tag + ".fb"])
            if globals().get("_FFN_DBG") == "gate" and tag == "F1" and it < FT:
                k.dma("pool", D["dbg"][it * 128:(it + 1) * 128, :],
                      gbt[:, :], rd=[tag + ".gb0", tag + ".gb1", tag + ".gb2", tag + ".gb3"])
            k.dma("sp", D["fuse"][it * 128:(it + 1) * 128, :], fbt[:, :], rd=[tag + ".fb"])
        k.barrier()

    with contextlib.ExitStack() as pst:
        psGd = [pst.enter_context(nc.psum_tensor(f"{tag}_psGd{i}", [128, 512], F32))
                for i in range(2)]
        fdb = pst.enter_context(nc.sbuf_tensor(tag + "_fdb", [128, NIT * 512], BF16))
        wdb = [pst.enter_context(nc.sbuf_tensor(f"{tag}_wd{i}", [128, NIT * 128], BF16))
               for i in range(2)]
        rxt = pst.enter_context(nc.sbuf_tensor(tag + "_rx", [128, 512], F32))
        rot = pst.enter_context(nc.sbuf_tensor(tag + "_ro", [128, 512], F32))
        for ch in range(4):
            k.dma("sp", fdb[:, :].rearrange("p (j f) -> p j f", f=512),
                  D["fuse"].rearrange("(j p) t -> p j t", p=128, j=64)[:, :, ch * 512:(ch + 1) * 512],
                  wr=[tag + ".fdb"])
            for dtile in range(FT):
                b = dtile % 2
                if ch == 0 or True:
                    pass
                if dtile < 2 and ch == 0:
                    k.dma("pool", wdb[b][:, :], D["f_wd"][dtile, :, :], wr=[tag + ".wd%d" % b])
                for jt in range(NIT):
                    k.mm(psGd[b][:, :], wdb[b][:, jt * 128:(jt + 1) * 128],
                         fdb[:, jt * 512:(jt + 1) * 512], jt == 0, jt == NIT - 1,
                         rd=[tag + ".wd%d" % b, tag + ".fdb"],
                         wr=[tag + ".psG%d" % b] if jt == NIT - 1 else [],
                         war=[tag + ".psG%d" % b] if jt == 0 else [], pub=(jt == NIT - 1))
                nd = dtile + 1 if dtile + 1 < FT else (0 if ch + 1 < 4 else None)
                if nd is not None and not (ch == 0 and dtile == 0):
                    k.dma("pool", wdb[nd % 2][:, :], D["f_wd"][nd, :, :],
                          wr=[tag + ".wd%d" % (nd % 2)])
                k.op("act", lambda e, dtile=dtile, pg=psGd[b]: e.activation(
                    rot[:, :], pg[:, :], AF.Copy, scale=P["sdc"][:, dtile:dtile + 1]),
                    rd=[tag + ".psG%d" % b, "sdc"], wr=[tag + ".rot"])
                k.dma("pool", rxt[:, :],
                      rin[dtile * 128:(dtile + 1) * 128, ch * 512:(ch + 1) * 512],
                      wr=[tag + ".rxt"])
                k.op("dve", lambda e: e.tensor_add(rot[:, :], rot[:, :], rxt[:, :]),
                     rd=[tag + ".rxt"], wr=[tag + ".rot"])
                k.dma("sp", rout[dtile * 128:(dtile + 1) * 128, ch * 512:(ch + 1) * 512],
                      rot[:, :], rd=[tag + ".rot"])
        k.barrier()


def _phase_attn_b(k, nc, P, D, upto):
    with contextlib.ExitStack() as pst:
        B = alloc_ln_bufs(nc, pst, "Xb")
        xbt = pst.enter_context(nc.sbuf_tensor("Xb_x", [128, FT * T], BF16))
        wb = pst.enter_context(nc.sbuf_tensor("Xb_w", [128, FT * C], BF16))
        ob = pst.enter_context(nc.sbuf_tensor("Xb_ob", [128, 512], BF16))
        o2 = pst.enter_context(nc.sbuf_tensor("Xb_o2", [128, 512], BF16))
        h1b = pst.enter_context(nc.sbuf_tensor("Xb_h1", [128, 512], BF16))
        scro = pst.enter_context(nc.sbuf_tensor("Xb_scro", [1, 512], F32))
        ps = [pst.enter_context(nc.psum_tensor(f"Xb_ps{i}", [128, 512], F32)) for i in range(2)]
        psH = pst.enter_context(nc.psum_tensor("Xb_psH", [128, 512], F32))
        psR = pst.enter_context(nc.psum_tensor("Xb_psR", [1, 512], F32))

        for ch in range(4):
            def out_cb(ft, xa, ch=ch):
                k.op("dve", lambda e, a=xa, ft=ft, ch=ch: e.tensor_copy(
                    xbt[:, ft * T + ch * 512:ft * T + (ch + 1) * 512], a),
                    rd=["Xb.x%d" % ft], wr=["Xb.xln%d" % ft])
            ln_chunk(k, B, P, ch, D["r2"], out_cb)

        def gemm_fm(wkey, wap, out_cb):
            k.dma("pool", wb[:, :], wap[:, :], wr=["Xb.wb"])
            for ft2 in range(FT):
                for ch in range(4):
                    b = (ft2 * 4 + ch) % 2
                    for ft in range(FT):
                        k.mm(ps[b][:, :], wb[:, ft * C + ft2 * 128:ft * C + (ft2 + 1) * 128],
                             xbt[:, ft * T + ch * 512:ft * T + (ch + 1) * 512],
                             ft == 0, ft == FT - 1, rd=["Xb.wb", "Xb.xln%d" % ft],
                             wr=["Xb.ps%d" % b] if ft == FT - 1 else [],
                             war=["Xb.ps%d" % b] if ft == 0 else [], pub=(ft == FT - 1))
                    out_cb(ft2, ch, b)

        def out_q(ft2, ch, b):
            k.op("act", lambda e, b=b: e.copy(ob[:, :], ps[b][:, :]),
                 rd=["Xb.ps%d" % b], wr=["Xb.ob"])
            k.dma("sp", D["qb"][ft2 * 128:(ft2 + 1) * 128, ch * 512:(ch + 1) * 512],
                  ob[:, :], rd=["Xb.ob"])
        gemm_fm("q", D["b_wqT"], out_q)

        def out_k(ft2, ch, b):
            k.op("act", lambda e, b=b: e.copy(ob[:, :], ps[b][:, :]),
                 rd=["Xb.ps%d" % b], wr=["Xb.ob"])
            k.mm(psH[:, :], P["sw1b"][:, :], ob[:, :], True, True,
                 rd=["sw1b", "Xb.ob"], wr=["Xb.psH"], pub=True)
            k.op("act", lambda e: e.activation(h1b[:, :], psH[:, :], AF.Relu,
                                               bias=P["sb1c"][:, 0:1]),
                 rd=["Xb.psH", "sb1c"], wr=["Xb.h1"])
            k.mm(psR[0:1, :], P["sw2b"][:, 0:1], h1b[:, :], True, True,
                 rd=["sw2b", "Xb.h1"], wr=["Xb.psR"], pub=True)
            k.op("act", lambda e: e.add(scro[0:1, :], psR[0:1, :], P["sb2c"][0:1, 0:1]),
                 rd=["Xb.psR", "sb2c"], wr=["Xb.scro"])
            k.dma("sp", D["scl_t"].ap()[ft2:ft2 + 1, ch * 512:(ch + 1) * 512],
                  scro[0:1, :], rd=["Xb.scro"])
        gemm_fm("k", D["b_wkT"], out_k)

        def gemm_tm(wap, outdram):
            k.dma("pool", wb[:, :], wap[:, :], wr=["Xb.wb"])
            for tt in range(FT):
                for dv in range(4):
                    b = (tt * 4 + dv) % 2
                    for ft in range(FT):
                        k.mm(ps[b][:, :], xbt[:, ft * T + tt * 128:ft * T + (tt + 1) * 128],
                             wb[:, ft * C + dv * 512:ft * C + (dv + 1) * 512],
                             ft == 0, ft == FT - 1, rd=["Xb.wb", "Xb.xln%d" % ft],
                             wr=["Xb.ps%d" % b] if ft == FT - 1 else [],
                             war=["Xb.ps%d" % b] if ft == 0 else [], pub=(ft == FT - 1))
                    k.op("dve", lambda e, b=b: e.tensor_copy(o2[:, :], ps[b][:, :]),
                         rd=["Xb.ps%d" % b], wr=["Xb.o2"])
                    for sl in range(4):
                        hh = dv * 4 + sl
                        k.dma("sp", outdram[hh, tt * 128:(tt + 1) * 128, :],
                              o2[:, sl * 128:(sl + 1) * 128], rd=["Xb.o2"])
        gemm_tm(D["b_wkT"], D["kbt"])
        gemm_tm(D["b_wvT"], D["vbt"])
        k.barrier()

    k.emit("pool", lambda e: e.collective_compute(
        "AllGather", ALU.bypass, replica_groups=[[0, 1], [2, 3], [4, 5], [6, 7]],
        ins=[D["scl_t"].ap().opt()], outs=[D["scl_cc_t"].ap().opt()]),
        sem="cc1", inc=1)
    for e in ENGS:
        k._wait(e, "cc1", 1)
    _emit_selection(k, nc, P, D, upto)
    if upto in ("xb", "scores", "sel", "ksel"):
        return
    k.emit("pool", lambda e: e.collective_compute(
        "AllReduce", ALU.add, replica_groups=[[0, 1], [2, 3], [4, 5], [6, 7]],
        ins=[D["kvsel_t"].ap().opt()], outs=[D["kvsel_cc_t"].ap().opt()]),
        sem="cc2", inc=1)
    for e in ENGS:
        k._wait(e, "cc2", 1)

    kvsel_cc = D["kvsel_cc_t"].ap()
    with contextlib.ExitStack() as pst:
        ybt = pst.enter_context(nc.sbuf_tensor("Ab_y", [128, FT * T], BF16))
        qh = pst.enter_context(nc.sbuf_tensor("Ab_qh", [128, T], BF16))
        kselb = pst.enter_context(nc.sbuf_tensor("Ab_ks", [128, 128], BF16))
        vselb = pst.enter_context(nc.sbuf_tensor("Ab_vs", [128, 128], BF16))
        kT = pst.enter_context(nc.sbuf_tensor("Ab_kT", [128, 128], BF16))
        expb = pst.enter_context(nc.sbuf_tensor("Ab_ex", [128, 128], BF16))
        expT = pst.enter_context(nc.sbuf_tensor("Ab_exT", [128, T], BF16))
        colsb = pst.enter_context(nc.sbuf_tensor("Ab_cols", [128, 4], F32))
        psP_ = pst.enter_context(nc.psum_tensor("Ab_psP", [128, 128], F32))
        psTb = pst.enter_context(nc.psum_tensor("Ab_psT", [128, 128], BF16))
        psA = pst.enter_context(nc.psum_tensor("Ab_psA", [128, 512], F32))

        for h in range(HB):
            k.dma("pool", qh[:, :], D["qb"][h * 128:(h + 1) * 128, :], wr=["Ab.qh"])
            k.dma("pool", kselb[:, :], kvsel_cc[h * 256:h * 256 + 128, :], wr=["Ab.ks"])
            k.dma("pool", vselb[:, :], kvsel_cc[h * 256 + 128:h * 256 + 256, :], wr=["Ab.vs"])
            k.op("pe", lambda e: e.transpose(psTb[:, :], kselb[:, :], P["eyeb"][:, :]),
                 rd=["Ab.ks", "eyeb"], wr=["Ab.psT"], pub=True)
            k.op("act", lambda e: e.copy(kT[:, :], psTb[:, :]), rd=["Ab.psT"], wr=["Ab.kT"])
            for tt in range(FT):
                k.mm(psP_[:, :], qh[:, tt * 128:(tt + 1) * 128], kT[:, :], True, True,
                     rd=["Ab.qh", "Ab.kT"], wr=["Ab.psP"], pub=True)
                k.op("dve", lambda e: e.reduce_max(colsb[:, 0:1], psP_[:, :], axis=AX.X),
                     rd=["Ab.psP"], wr=["Ab.c0"])
                k.op("dve", lambda e: e.tensor_scalar_mul(colsb[:, 1:2], colsb[:, 0:1], -SCL_B),
                     rd=["Ab.c0"], wr=["Ab.c1"])
                k.op("act", lambda e: e.activation(expb[:, :], psP_[:, :], AF.Exp,
                                                   bias=colsb[:, 1:2], scale=SCL_B,
                                                   accum_out=colsb[:, 2:3]),
                     rd=["Ab.psP", "Ab.c1"], wr=["Ab.exp", "Ab.c2"])
                k.op("dve", lambda e: e.reciprocal(colsb[:, 3:4], colsb[:, 2:3]),
                     rd=["Ab.c2"], wr=["Ab.c3"])
                k.op("dve", lambda e: e.tensor_scalar_mul(expb[:, :], expb[:, :], colsb[:, 3:4]),
                     rd=["Ab.c3"], wr=["Ab.exp"])
                k.op("pe", lambda e: e.transpose(psTb[:, :], expb[:, :], P["eyeb"][:, :]),
                     rd=["Ab.exp", "eyeb"], wr=["Ab.psT"], pub=True)
                k.op("act", lambda e, tt=tt: e.copy(expT[:, tt * 128:(tt + 1) * 128], psTb[:, :]),
                     rd=["Ab.psT"], wr=["Ab.exT%d" % (tt // 4)])
            for g in range(4):
                k.mm(psA[:, :], vselb[:, :], expT[:, g * 512:(g + 1) * 512], True, True,
                     rd=["Ab.vs", "Ab.exT%d" % g], wr=["Ab.psA"], pub=True)
                k.op("act", lambda e, h=h, g=g: e.copy(
                    ybt[:, h * T + g * 512:h * T + (g + 1) * 512], psA[:, :]),
                    rd=["Ab.psA"], wr=["Ab.y%d" % h])
        wb2 = pst.enter_context(nc.sbuf_tensor("Ab_wo", [128, FT * C], BF16))
        rxt = pst.enter_context(nc.sbuf_tensor("Ab_rx", [128, 512], F32))
        rot = pst.enter_context(nc.sbuf_tensor("Ab_ro", [128, 512], F32))
        psO2 = [pst.enter_context(nc.psum_tensor(f"Ab_psO{i}", [128, 512], F32))
                for i in range(2)]
        k.dma("pool", wb2[:, :], D["b_woT"][:, :], wr=["Ab.wo"])
        for ft2 in range(FT):
            for ch in range(4):
                b = (ft2 * 4 + ch) % 2
                for ft in range(FT):
                    k.mm(psO2[b][:, :], wb2[:, ft * C + ft2 * 128:ft * C + (ft2 + 1) * 128],
                         ybt[:, ft * T + ch * 512:ft * T + (ch + 1) * 512],
                         ft == 0, ft == FT - 1, rd=["Ab.wo", "Ab.y%d" % ft],
                         wr=["Ab.psO%d" % b] if ft == FT - 1 else [],
                         war=["Ab.psO%d" % b] if ft == 0 else [], pub=(ft == FT - 1))
                k.dma("sp", rxt[:, :], D["r2"][ft2 * 128:(ft2 + 1) * 128, ch * 512:(ch + 1) * 512],
                      wr=["Ab.rx"])
                if globals().get("_AB_NORES"):
                    k.op("dve", lambda e, po=psO2[b]: e.tensor_scalar_mul(rot[:, :], po[:, :], 1.0),
                         rd=["Ab.psO%d" % b, "Ab.rx"], wr=["Ab.rot"])
                else:
                    k.op("dve", lambda e, po=psO2[b]: e.tensor_add(rot[:, :], po[:, :], rxt[:, :]),
                         rd=["Ab.psO%d" % b, "Ab.rx"], wr=["Ab.rot"])
                k.dma("sp", D["r3"][ft2 * 128:(ft2 + 1) * 128, ch * 512:(ch + 1) * 512],
                      rot[:, :], rd=["Ab.rot"])
        k.barrier()


def _emit_selection(k, nc, P, D, upto):
    with contextlib.ExitStack() as pst:
        def sb(name, shape, dtyp=F32):
            return pst.enter_context(nc.sbuf_tensor(name, shape, dtyp))
        selt = sb("L_sel", [128, 512])
        mb = sb("L_mb", [128, 512])
        ms = sb("L_ms", [128, 512])
        elig = sb("L_el", [128, 512])
        eligv = sb("L_elv", [128, 512])
        ind = sb("L_ind", [128, 512])
        cs = sb("L_cs", [128, 512])
        fill1 = sb("L_f1", [128, 512])
        rankp = sb("L_rk", [128, 512])
        rankT = sb("L_rkT", [128, 512])
        w = sb("L_w", [128, 16])
        cc = sb("L_cc", [128, 4])
        mrow = sb("L_mrow", [4, 128])
        mred = sb("L_mred", [4, 16])
        mcolsT = sb("L_mcT", [16, 4])
        mtall = sb("L_mt", [128, FT * 128], BF16)
        kvsb = sb("L_kv", [128, FT * 128], BF16)
        oselk = sb("L_ok", [128, 128])
        oselv = sb("L_ov", [128, 128])
        psB = pst.enter_context(nc.psum_tensor("L_psB", [128, 4], F32))
        psX = pst.enter_context(nc.psum_tensor("L_psX", [128, 128], F32))
        psKV = pst.enter_context(nc.psum_tensor("L_psKV", [128, 128], F32))

        scl_cc = D["scl_cc_t"].ap()
        k.op("dve", lambda e: e.memset(w[:, :], 0.0), wr=["L.w"])
        k.dma_fn("pool", lambda e: e.indirect_dma_start(
            out=selt[:, :], out_offset=None,
            in_=scl_cc.rearrange("r (c f) -> (r c) f", f=512),
            in_offset=IndirectOffsetOnAxis(P["permt"][:, 0:1], 0)),
            rd=["permt"], wr=["L.sel"])
        if upto == "scores":
            for ft in range(FT):
                k.dma("sp", D["dbg"][ft * 128:(ft + 1) * 128, 0:512], selt[:, :], rd=["L.sel"])
            return

        W = lambda i: w[:, i:i + 1]
        def hsum2(src2, n, outslot, rdk, wrk):
            k.mm(psB[:, 0:n], P["Hm"][:, :], src2, True, True,
                 rd=[rdk, "Hm"], wr=["L.psB"], pub=True)
            k.op("act", lambda e: e.copy(w[:, outslot:outslot + n], psB[:, 0:n]),
                 rd=["L.psB"], wr=[wrk])

        k.op("dve", lambda e: e.reduce_max(cc[:, 0:1], selt[:, :], axis=AX.X),
             rd=["L.sel"], wr=["L.cc"])
        k.op("dve", lambda e: e.tensor_scalar_mul(ms[:, :], selt[:, :], -1.0),
             rd=["L.sel"], wr=["L.ms"])
        k.op("dve", lambda e: e.reduce_max(cc[:, 1:2], ms[:, :], axis=AX.X),
             rd=["L.ms"], wr=["L.cc"])
        k.op("pe", lambda e: e.transpose(psX[0:2, 0:128], cc[:, 0:2], P["eyef"][:, :]),
             rd=["L.cc", "eyef"], wr=["L.psX"], pub=True)
        k.op("act", lambda e: e.copy(mrow[0:2, :], psX[0:2, 0:128]),
             rd=["L.psX"], wr=["L.mrow"])
        k.op("dve", lambda e: e.reduce_max(
            mred[0:2, :], mrow[0:2, :].rearrange("p (h s) -> p h s", s=8), axis=AX.X),
            rd=["L.mrow"], wr=["L.mred"])
        k.op("pe", lambda e: e.transpose(psX[0:16, 0:2], mred[0:2, 0:16], P["eyef"][0:2, 0:2]),
             rd=["L.mred", "eyef"], wr=["L.psX"], pub=True)
        k.op("act", lambda e: e.copy(mcolsT[0:16, 0:2], psX[0:16, 0:2]),
             rd=["L.psX"], wr=["L.mcT"])
        k.mm(psB[:, 0:2], P["Hs"][0:16, :], mcolsT[0:16, 0:2], True, True,
             rd=["L.mcT", "Hs"], wr=["L.psB"], pub=True)
        k.op("dve", lambda e: e.tensor_scalar(W(1), psB[:, 1:2], -1.0, -1.0, ALU.mult, ALU.add),
             rd=["L.psB"], wr=["L.w"])
        k.op("act", lambda e: e.copy(W(2), psB[:, 0:1]), rd=["L.psB"], wr=["L.w"])

        k.op("dve", lambda e: e.reduce_sum(cc[:, 0:1], selt[:, :], axis=AX.X),
             rd=["L.sel"], wr=["L.cc"])
        hsum2(cc[:, 0:1], 1, 5, "L.cc", "L.w")
        k.op("dve", lambda e: e.tensor_scalar(W(0), W(5), -float(KSEL), 1.0 / 4096.0,
                                              ALU.add, ALU.mult), rd=["L.w"], wr=["L.w"])
        for _ in range(16):
            k.op("dve", lambda e: e.tensor_scalar(mb[:, :], selt[:, :], W(0), None, ALU.is_gt),
                 rd=["L.sel", "L.w"], wr=["L.mb"])
            k.op("dve", lambda e: e.tensor_mul(ms[:, :], mb[:, :], selt[:, :]),
                 rd=["L.mb", "L.sel"], wr=["L.ms"])
            k.op("dve", lambda e: e.reduce_sum(cc[:, 0:1], ms[:, :], axis=AX.X),
                 rd=["L.ms"], wr=["L.cc"])
            k.op("dve", lambda e: e.reduce_sum(cc[:, 1:2], mb[:, :], axis=AX.X),
                 rd=["L.mb"], wr=["L.cc"])
            k.mm(psB[:, 0:2], P["Hm"][:, :], cc[:, 0:2], True, True,
                 rd=["L.cc", "Hm"], wr=["L.psB"], pub=True)
            k.op("dve", lambda e: e.tensor_scalar_add(W(11), psB[:, 0:1], -float(KSEL)),
                 rd=["L.psB"], wr=["L.w"])
            k.op("dve", lambda e: e.reciprocal(W(12), psB[:, 1:2]), rd=["L.psB"], wr=["L.w"])
            k.op("dve", lambda e: e.tensor_mul(W(0), W(11), W(12)), rd=["L.w"], wr=["L.w"])

        k.op("dve", lambda e: e.tensor_scalar(mb[:, :], selt[:, :], W(0), None, ALU.is_gt),
             rd=["L.sel", "L.w"], wr=["L.mb"])
        k.op("dve", lambda e: e.reduce_sum(cc[:, 0:1], mb[:, :], axis=AX.X),
             rd=["L.mb"], wr=["L.cc"])
        k.mm(psB[:, 0:1], P["Hm"][:, :], cc[:, 0:1], True, True,
             rd=["L.cc", "Hm"], wr=["L.psB"], pub=True)
        k.op("dve", lambda e: e.tensor_scalar(W(13), psB[:, 0:1], float(KSEL), None, ALU.is_ge),
             rd=["L.psB"], wr=["L.w"])

        for _ in range(40):
            k.op("dve", lambda e: e.tensor_add(W(3), W(1), W(2)), rd=["L.w"], wr=["L.w"])
            k.op("dve", lambda e: e.tensor_scalar_mul(W(3), W(3), 0.5), wr=["L.w"])
            k.op("dve", lambda e: e.tensor_scalar(mb[:, :], selt[:, :], W(3), None, ALU.is_gt),
                 rd=["L.sel", "L.w"], wr=["L.mb"])
            k.op("dve", lambda e: e.reduce_sum(cc[:, 0:1], mb[:, :], axis=AX.X),
                 rd=["L.mb"], wr=["L.cc"])
            k.mm(psB[:, 0:1], P["Hm"][:, :], cc[:, 0:1], True, True,
                 rd=["L.cc", "Hm"], wr=["L.psB"], pub=True)
            k.op("dve", lambda e: e.tensor_scalar(W(6), psB[:, 0:1], float(KSEL), None, ALU.is_ge),
                 rd=["L.psB"], wr=["L.w"])
            k.op("dve", lambda e: e.tensor_sub(W(11), W(3), W(1)), wr=["L.w"])
            k.op("dve", lambda e: e.tensor_mul(W(11), W(11), W(6)), wr=["L.w"])
            k.op("dve", lambda e: e.tensor_add(W(1), W(1), W(11)), wr=["L.w"])
            k.op("dve", lambda e: e.tensor_sub(W(11), W(3), W(2)), wr=["L.w"])
            k.op("dve", lambda e: e.tensor_scalar(W(12), W(6), -1.0, 1.0, ALU.mult, ALU.add),
                 wr=["L.w"])
            k.op("dve", lambda e: e.tensor_mul(W(11), W(11), W(12)), wr=["L.w"])
            k.op("dve", lambda e: e.tensor_add(W(2), W(2), W(11)), wr=["L.w"])

        k.op("dve", lambda e: e.tensor_sub(W(11), W(2), W(0)), rd=["L.w"], wr=["L.w"])
        k.op("dve", lambda e: e.tensor_mul(W(11), W(11), W(13)), wr=["L.w"])
        k.op("dve", lambda e: e.tensor_add(W(7), W(0), W(11)), wr=["L.w"])
        k.op("dve", lambda e: e.tensor_scalar(ind[:, :], selt[:, :], W(7), None, ALU.is_gt),
             rd=["L.sel", "L.w"], wr=["L.ind"])
        k.op("dve", lambda e: e.reduce_sum(cc[:, 0:1], ind[:, :], axis=AX.X),
             rd=["L.ind"], wr=["L.cc"])
        k.mm(psB[:, 0:1], P["Hm"][:, :], cc[:, 0:1], True, True,
             rd=["L.cc", "Hm"], wr=["L.psB"], pub=True)
        k.op("dve", lambda e: e.tensor_scalar(W(8), psB[:, 0:1], -1.0, float(KSEL),
                                              ALU.mult, ALU.add), rd=["L.psB"], wr=["L.w"])
        k.op("dve", lambda e: e.tensor_scalar(elig[:, :], selt[:, :], W(7), None, ALU.is_le),
             rd=["L.sel", "L.w"], wr=["L.el"])

        def gcs(src, srckey, outap):
            k.op("dve", lambda e, s=src, o=outap: e.tensor_tensor_scan(
                o, s, s, 0.0, ALU.add, ALU.bypass), rd=[srckey], wr=["L.cs"])
            k.op("dve", lambda e, s=src: e.reduce_sum(cc[:, 0:1], s, axis=AX.X),
                 rd=[srckey], wr=["L.cc"])
            k.mm(psB[:, 0:1], P["Lm"][:, :], cc[:, 0:1], True, True,
                 rd=["L.cc", "Lm"], wr=["L.psB"], pub=True)
            k.op("act", lambda e: e.copy(W(9), psB[:, 0:1]), rd=["L.psB"], wr=["L.w"])
            k.op("dve", lambda e, o=outap: e.tensor_scalar_add(o, o, W(9)),
                 rd=["L.w"], wr=["L.cs"])
        gcs(elig[:, :], "L.el", cs[:, :])
        k.op("dve", lambda e: e.tensor_scalar(ms[:, :], cs[:, :], W(8), None, ALU.is_le),
             rd=["L.cs", "L.w"], wr=["L.ms"])
        k.op("dve", lambda e: e.tensor_mul(ms[:, :], ms[:, :], elig[:, :]),
             rd=["L.el"], wr=["L.ms"])
        k.op("dve", lambda e: e.tensor_copy(eligv[:, :], elig[:, :]), rd=["L.el"], wr=["L.elv"])
        k.op("dve", lambda e: e.memset(fill1[:, :], 0.0), wr=["L.f1"])
        k.op("act", lambda e: e.copy(W(15), W(8)), rd=["L.w"], wr=["L.w"])
        for _ in range(4):
            k.op("dve", lambda e: e.tensor_scalar(cs[:, :], eligv[:, :], 1.0e9, -1.0e9,
                                                  ALU.mult, ALU.add),
                 rd=["L.elv"], wr=["L.cs"])
            k.op("dve", lambda e: e.tensor_mul(mb[:, :], selt[:, :], eligv[:, :]),
                 rd=["L.sel", "L.elv"], wr=["L.mb"])
            k.op("dve", lambda e: e.tensor_add(mb[:, :], mb[:, :], cs[:, :]),
                 rd=["L.cs"], wr=["L.mb"])
            k.op("dve", lambda e: e.reduce_max(cc[:, 0:1], mb[:, :], axis=AX.X),
                 rd=["L.mb"], wr=["L.cc"])
            k.op("pe", lambda e: e.transpose(psX[0:1, 0:128], cc[:, 0:1], P["eyef"][:, :]),
                 rd=["L.cc", "eyef"], wr=["L.psX"], pub=True)
            k.op("act", lambda e: e.copy(mrow[0:1, :], psX[0:1, 0:128]),
                 rd=["L.psX"], wr=["L.mrow"])
            k.op("dve", lambda e: e.reduce_max(
                mred[0:1, :], mrow[0:1, :].rearrange("p (h s) -> p h s", s=8), axis=AX.X),
                rd=["L.mrow"], wr=["L.mred"])
            k.op("pe", lambda e: e.transpose(psX[0:16, 0:1], mred[0:1, 0:16], P["eyef"][0:1, 0:1]),
                 rd=["L.mred", "eyef"], wr=["L.psX"], pub=True)
            k.op("act", lambda e: e.copy(mcolsT[0:16, 0:1], psX[0:16, 0:1]),
                 rd=["L.psX"], wr=["L.mcT"])
            k.mm(psB[:, 0:1], P["Hs"][0:16, :], mcolsT[0:16, 0:1], True, True,
                 rd=["L.mcT", "Hs"], wr=["L.psB"], pub=True)
            k.op("act", lambda e: e.copy(W(10), psB[:, 0:1]), rd=["L.psB"], wr=["L.w"])
            k.op("dve", lambda e: e.tensor_scalar(mb[:, :], selt[:, :], W(10), None, ALU.is_equal),
                 rd=["L.sel", "L.w"], wr=["L.mb"])
            k.op("dve", lambda e: e.tensor_mul(mb[:, :], mb[:, :], eligv[:, :]),
                 rd=["L.elv"], wr=["L.mb"])
            gcs(mb[:, :], "L.mb", cs[:, :])
            k.op("dve", lambda e: e.tensor_scalar(ind[:, 0:0 + 512] if False else cs[:, :],
                                                  cs[:, :], W(15), None, ALU.is_le),
                 rd=["L.cs", "L.w"], wr=["L.cs"])
            k.op("dve", lambda e: e.tensor_mul(cs[:, :], cs[:, :], mb[:, :]),
                 rd=["L.mb"], wr=["L.cs"])
            k.op("dve", lambda e: e.tensor_add(fill1[:, :], fill1[:, :], cs[:, :]),
                 rd=["L.cs"], wr=["L.f1"])
            k.op("dve", lambda e: e.reduce_sum(cc[:, 0:1], cs[:, :], axis=AX.X),
                 rd=["L.cs"], wr=["L.cc"])
            k.mm(psB[:, 0:1], P["Hm"][:, :], cc[:, 0:1], True, True,
                 rd=["L.cc", "Hm"], wr=["L.psB"], pub=True)
            k.op("dve", lambda e: e.tensor_sub(W(15), W(15), psB[:, 0:1]),
                 rd=["L.psB"], wr=["L.w"])
            k.op("dve", lambda e: e.tensor_sub(eligv[:, :], eligv[:, :], mb[:, :]),
                 rd=["L.mb"], wr=["L.elv"])
        k.op("dve", lambda e: e.tensor_sub(fill1[:, :], fill1[:, :], ms[:, :]),
             rd=["L.ms"], wr=["L.f1"])
        k.op("dve", lambda e: e.tensor_scalar_mul(fill1[:, :], fill1[:, :], W(13)),
             rd=["L.w"], wr=["L.f1"])
        k.op("dve", lambda e: e.tensor_add(fill1[:, :], fill1[:, :], ms[:, :]),
             rd=["L.ms"], wr=["L.f1"])
        k.op("dve", lambda e: e.tensor_add(ind[:, :], ind[:, :], fill1[:, :]),
             rd=["L.f1"], wr=["L.ind"])
        if upto == "sel":
            k.dma("sp", D["dbg"][0:128, 0:512], ind[:, :], rd=["L.ind"])
            k.dma("sp", D["dbg"][128:256, 0:16], w[:, :], rd=["L.w"])
            k.dma("sp", D["dbg"][256:384, 0:512], ms[:, :], rd=["L.ms"])
            k.dma("sp", D["dbg"][384:512, 0:512], fill1[:, :], rd=["L.f1"])
            k.dma("sp", D["dbg"][512:640, 0:512], elig[:, :], rd=["L.el"])
            return

        gcs(ind[:, :], "L.ind", cs[:, :])
        k.op("dve", lambda e: e.tensor_mul(rankp[:, :], cs[:, :], ind[:, :]),
             rd=["L.cs", "L.ind"], wr=["L.rk"])
        for b4 in range(4):
            k.op("pe", lambda e, b4=b4: e.transpose(psX[:, :], rankp[:, b4 * 128:(b4 + 1) * 128],
                                                    P["eyef"][:, :]),
                 rd=["L.rk", "eyef"], wr=["L.psX"], pub=True)
            k.op("act", lambda e, b4=b4: e.copy(rankT[:, b4 * 128:(b4 + 1) * 128], psX[:, :]),
                 rd=["L.psX"], wr=["L.rkT"])

        kvsel = D["kvsel_t"].ap()
        for h in range(HB):
            for lb in range(FT):
                col = 128 * (lb % 4) + 8 * h + (lb // 4)
                k.op("dve", lambda e, lb=lb, col=col: e.tensor_scalar(
                    mtall[:, lb * 128:(lb + 1) * 128], P["Jb"][:, :], rankT[:, col:col + 1],
                    None, ALU.is_equal), rd=["Jb", "L.rkT"], wr=["L.mt"])
            k.dma("pool", kvsb[:, :].rearrange("p (tt d) -> p tt d", d=128),
                  D["kbt"][h].rearrange("(tt p) d -> p tt d", p=128, tt=16), wr=["L.kvs"])
            for lb in range(FT):
                k.mm(psKV[:, :], mtall[:, lb * 128:(lb + 1) * 128],
                     kvsb[:, lb * 128:(lb + 1) * 128], lb == 0, lb == FT - 1,
                     rd=["L.mt", "L.kvs"], wr=["L.psKV"] if lb == FT - 1 else [],
                     war=["L.psKV"] if lb == 0 else [], pub=(lb == FT - 1))
            k.op("act", lambda e: e.copy(oselk[:, :], psKV[:, :]), rd=["L.psKV"], wr=["L.ok"])
            k.dma("sp", kvsel[h * 256:h * 256 + 128, :], oselk[:, :], rd=["L.ok"])
            k.dma("pool", kvsb[:, :].rearrange("p (tt d) -> p tt d", d=128),
                  D["vbt"][h].rearrange("(tt p) d -> p tt d", p=128, tt=16), wr=["L.kvs"])
            for lb in range(FT):
                k.mm(psKV[:, :], mtall[:, lb * 128:(lb + 1) * 128],
                     kvsb[:, lb * 128:(lb + 1) * 128], lb == 0, lb == FT - 1,
                     rd=["L.mt", "L.kvs"], wr=["L.psKV"] if lb == FT - 1 else [],
                     war=["L.psKV"] if lb == 0 else [], pub=(lb == FT - 1))
            k.op("act", lambda e: e.copy(oselv[:, :], psKV[:, :]), rd=["L.psKV"], wr=["L.ov"])
            k.dma("sp", kvsel[h * 256 + 128:h * 256 + 256, :], oselv[:, :], rd=["L.ov"])
        if upto == "ksel":
            for ft in range(FT):
                k.dma("sp", D["dbg"][ft * 128:(ft + 1) * 128, 0:128],
                      kvsel[ft * 256:ft * 256 + 128, :], deps=[("sp_d", k.cnt["sp_d"])])
        k.barrier()



def _np_col(v):
    return np.ascontiguousarray(np.asarray(v, dtype=np.float32).reshape(-1, 128).T)


def _lhsT_pack(W):
    W = np.ascontiguousarray(W)
    din, dout = W.shape
    return np.ascontiguousarray(
        W.reshape(din // 128, 128, dout).transpose(1, 0, 2).reshape(128, -1))


def _host_constants(core):
    half = core % 2
    p = np.arange(128) % 64
    tcol = np.arange(64)
    maskD = ((p[:, None] // LSUB == tcol[None, :] // LSUB) &
             (p[:, None] <= tcol[None, :])).astype(np.float32)
    maskS = np.broadcast_to(
        ((tcol[None, :] // LSUB == p[:, None] // LSUB + 1) &
         np.isin(tcol[None, :] // LSUB, (1, 3))), (128, 64)).astype(np.float32)
    J = np.broadcast_to(np.arange(1, 129, dtype=np.float32)[None, :], (128, 128)).copy()
    seg = np.arange(128) % 8
    h = np.arange(128) // 8
    gseg = np.where(seg < 4, seg + 4 * half, (seg - 4) + 4 * (1 - half))
    Lmask = ((h[:, None] == h[None, :]) & (gseg[:, None] < gseg[None, :])).astype(np.float32)
    Hmask = (h[:, None] == h[None, :]).astype(np.float32)
    Hsel = (np.arange(16)[:, None] == h[None, :]).astype(np.float32)
    iota = (gseg[:, None] * 512 + np.arange(512)[None, :]).astype(np.float32)
    perm = np.empty(128, np.uint32)
    for hh in range(16):
        for sg in range(8):
            rank = half if sg < 4 else 1 - half
            perm[8 * hh + sg] = (rank * HB + hh) * 4 + (sg % 4)
    eye = np.eye(128, dtype=np.float32)
    return dict(c_maskD=maskD, c_maskS=np.ascontiguousarray(maskS), c_J=J,
                c_Lmask=Lmask, c_Hmask=Hmask, c_Hsel=Hsel, c_iota=iota,
                c_perm=perm.reshape(128, 1), c_eye=eye,
                c_par=np.full((128, 1), float(half), np.float32))


def kernel(**inputs):
    g = lambda n: np.asarray(inputs[n])
    x = g("L0_attn_a_x")
    nc = build_program(_BUILD_UPTO)

    acols = np.concatenate([
        _np_col(g("L0_attn_a_maa_x")), _np_col(g("L0_attn_a_maa_r")),
        _np_col(g("L0_attn_a_maa_k")), _np_col(g("L0_attn_a_maa_v")),
        _np_col(g("L0_attn_a_maa_w")), _np_col(g("L0_attn_a_q_b")),
        _np_col(g("L0_attn_a_k_b")), _np_col(g("L0_attn_a_v_b")),
        _np_col(g("L0_attn_a_time_decay")), _np_col(g("L0_attn_a_ln_g")),
        _np_col(g("L0_attn_a_ln_b")), np.zeros((128, FT), np.float32)], axis=1)
    wg = np.asarray(g("L0_ffn_w_gate"))
    wu = np.asarray(g("L0_ffn_w_up"))
    wd = np.asarray(g("L0_ffn_w_down"))
    common = {
        "a_qw": _lhsT_pack(g("L0_attn_a_q_w")), "a_kw": _lhsT_pack(g("L0_attn_a_k_w")),
        "a_vw": _lhsT_pack(g("L0_attn_a_v_w")), "a_ow": _lhsT_pack(g("L0_attn_a_o_w")),
        "a_maa_w1": _lhsT_pack(g("L0_attn_a_maa_w1")),
        "a_maa_w2": g("L0_attn_a_maa_w2"),
        "a_td_w1": _lhsT_pack(g("L0_attn_a_td_w1")), "a_td_w2": g("L0_attn_a_td_w2"),
        "a_cols": acols, "a_vb_row": g("L0_attn_a_v_b").reshape(1, C),
        "b_wqT": _lhsT_pack(g("L0_attn_b_wq").T), "b_wkT": _lhsT_pack(g("L0_attn_b_wk").T),
        "b_wvT": _lhsT_pack(g("L0_attn_b_wv").T), "b_woT": _lhsT_pack(g("L0_attn_b_wo").T),
        "b_sw1T": np.ascontiguousarray(g("L0_attn_b_sw1").T),
        "b_sw2c": np.ascontiguousarray(g("L0_attn_b_sw2").reshape(128, 1)),
        "b_sb1c": np.ascontiguousarray(g("L0_attn_b_sb1").reshape(128, 1)),
        "b_sb2": g("L0_attn_b_sb2").reshape(1, 1),
        "f_wg": np.ascontiguousarray(
            wg.reshape(FT, 128, NIT, 128).transpose(2, 1, 0, 3).reshape(NIT, 128, FT * 128)),
        "f_wu": np.ascontiguousarray(
            wu.reshape(FT, 128, NIT, 128).transpose(2, 1, 0, 3).reshape(NIT, 128, FT * 128)),
        "f_wd": np.ascontiguousarray(
            wd.reshape(NIT, 128, FT, 128).transpose(2, 1, 0, 3).reshape(FT, 128, NIT * 128)),
        "f_sg": _np_col(g("L0_ffn_s_gate")), "f_su": _np_col(g("L0_ffn_s_up")),
        "f_sd": _np_col(g("L0_ffn_s_down")),
    }
    in_maps = []
    for core in range(NCORE):
        b, half = core // 2, core % 2
        m = dict(common)
        m.update(_host_constants(core))
        m["xfm"] = np.ascontiguousarray(x[b, half * T:(half + 1) * T, :].T)
        xp = np.zeros(C, np.float32) if half == 0 else np.ascontiguousarray(x[b, T - 1, :])
        m["xprev"] = _np_col(xp)
        in_maps.append(m)

    res = run_bass_kernel_spmd(nc, in_maps, core_ids=list(range(NCORE)))
    out = np.empty((4, 4096, C), np.float32)
    for core in range(NCORE):
        b, half = core // 2, core % 2
        out[b, half * T:(half + 1) * T, :] = res.results[core]["yfm"].T
    return out


# version 04
# speedup vs baseline: 1.0511x; 1.0474x over previous
import contextlib
import numpy as np
import concourse.bass as bass
import concourse.mybir as mybir
from concourse.bass_utils import run_bass_kernel_spmd
from concourse.bass_core_rust import IndirectOffsetOnAxis

F32 = mybir.dt.float32
BF16 = mybir.dt.bfloat16
I8 = mybir.dt.int8
U32 = mybir.dt.uint32
ALU = mybir.AluOpType
AF = mybir.ActivationFunctionType
AX = mybir.AxisListType

C = 2048
T = 2048
FT = C // 128
NCORE = 8
INTER = 8192
NIT = INTER // 128
LCH = 64
LSUB = 16
NCH = T // LCH
NPAIR = 16
HB = 16
KSEL = 128
EPS = 1e-5
SCL_A = 0.125
SCL_B = 0.08838834764831845
BIG = 1.0e30

_BUILD_UPTO = "full"
_SCAN_DBG = False
_FFN_DBG = None
_AB_NORES = False
_NO_DBGCOPY = False

ENGS = ("pe", "dve", "act", "pool", "sp")
NDSEM = 8
_SPSEMS = tuple("spd%d" % i for i in range(NDSEM))
_PQSEMS = tuple("pqd%d" % i for i in range(NDSEM))
SEMS = ("pe", "dve", "act", "pool_c", "cc0", "cc1", "cc2") + _SPSEMS + _PQSEMS
_OWN = {("pe", "pe"), ("dve", "dve"), ("act", "act"), ("pool", "pool_c")}


class K:

    def __init__(self, nc):
        self.nc = nc
        self.prog = {e: [] for e in ENGS}
        self.cnt = {s: 0 for s in SEMS}
        self.waited = {e: {s: 0 for s in SEMS} for e in ENGS}
        self.sems = {}
        self.wtok = {}
        self.rtok = {}
        self.rr = {"sp": 0, "pool": 0}
        self._pend_rd = set()
        self._pend_wr = set()

    def _wait(self, eng, sem, val):
        if val <= 0 or self.waited[eng][sem] >= val:
            return
        assert val <= self.cnt[sem], f"wait on future {sem}={val} (now {self.cnt[sem]})"
        self.waited[eng][sem] = val
        self.prog[eng].append(("w", sem, val))

    def emit(self, eng, fn, deps=(), pub=False, dma=False, sem=None, inc=None):
        for d in deps:
            if d is not None:
                self._wait(eng, d[0], d[1])
        if sem is None and (pub or dma):
            assert not dma, "dma emits must pass sem explicitly (use K.dma)"
            sem = {"pe": "pe", "dve": "dve", "act": "act", "pool": "pool_c"}[eng]
        if sem is not None:
            step = inc if inc is not None else (16 if dma else 1)
            self.cnt[sem] += step
            self.prog[eng].append(("i", fn, sem, step))
            return (sem, self.cnt[sem])
        self.prog[eng].append(("i", fn, None, 0))
        return None

    def op(self, eng, fn, rd=(), wr=(), deps=(), pub=None, dma=False, sem=None, inc=None,
           war=()):
        dl = list(deps)
        for b in rd:
            dl.append(self.wtok.get(b))
        for b in war:
            dl.append(self.wtok.get(b))
            dl.extend(self.rtok.get(b, ()))
        for b in wr:
            assert b not in self._pend_rd and b not in self._pend_wr, \
                f"write to {b} while unpublished pe access pending"
            dl.append(self.wtok.get(b))
            dl.extend(self.rtok.get(b, ()))
        if pub is None:
            pub = (eng != "pe")
        tok = self.emit(eng, fn, deps=dl, pub=pub, dma=dma, sem=sem, inc=inc)
        if tok is None:
            self._pend_rd.update(rd)
            self._pend_wr.update(wr)
            return None
        if eng == "pe":
            for b in self._pend_rd:
                self.rtok.setdefault(b, []).append(tok)
            for b in self._pend_wr:
                self.wtok[b] = tok
                self.rtok[b] = []
            self._pend_rd.clear()
            self._pend_wr.clear()
        for b in rd:
            self.rtok.setdefault(b, []).append(tok)
        for b in wr:
            self.wtok[b] = tok
            self.rtok[b] = []
        return tok

    def dma_fn(self, q, fn, rd=(), wr=(), deps=()):
        i = self.rr[q]
        self.rr[q] = (i + 1) % NDSEM
        sem = ("spd%d" % i) if q == "sp" else ("pqd%d" % i)
        self._wait(q, sem, self.cnt[sem])
        return self.op(q, fn, rd=rd, wr=wr, deps=deps, dma=True, sem=sem)

    def dma(self, q, out, in_, rd=(), wr=(), deps=()):
        i = self.rr[q]
        self.rr[q] = (i + 1) % NDSEM
        sem = ("spd%d" % i) if q == "sp" else ("pqd%d" % i)
        self._wait(q, sem, self.cnt[sem])
        return self.op(q, lambda e, o=out, i_=in_: e.dma_start(out=o, in_=i_),
                       rd=rd, wr=wr, deps=deps, dma=True, sem=sem)

    def mm(self, out, lhsT, rhs, start, stop, rd=(), wr=(), deps=(), pub=False, war=(),
           tp=None):
        return self.op("pe", lambda e, o=out, l=lhsT, r=rhs, s1=start, s2=stop, tp=tp:
                       e.matmul(o, l, r, start=s1, stop=s2, tile_position=tp),
                       rd=rd, wr=wr, deps=deps, pub=pub, war=war)

    def barrier(self):
        assert not self._pend_rd and not self._pend_wr
        snap = dict(self.cnt)
        for e in ENGS:
            for s, v in snap.items():
                if (e, s) in _OWN:
                    continue
                self._wait(e, s, v)
        self.wtok.clear()
        self.rtok.clear()

    def finalize(self):
        self.barrier()
        deps = [(s, self.cnt[s]) for s in _SPSEMS + _PQSEMS]
        self.emit("sp", lambda e: e.nop(), deps=deps)

    def replay(self, block):
        def mk(eng):
            def body(e):
                for item in self.prog[eng]:
                    if item[0] == "w":
                        e.wait_ge(self.sems[item[1]], item[2])
                    else:
                        _, fn, sem, step = item
                        ins = fn(e)
                        if sem is not None:
                            ins.then_inc(self.sems[sem], step)
            return body
        block.tensor(mk("pe"))
        block.vector(mk("dve"))
        block.scalar(mk("act"))
        block.gpsimd(mk("pool"))
        block.sync(mk("sp"))



def alloc_ln_bufs(nc, pst, tag):
    return {
        "tag": tag,
        "xc": pst.enter_context(nc.sbuf_tensor(tag + "_xc", [128, FT * 512], F32)),
        "sq": pst.enter_context(nc.sbuf_tensor(tag + "_sq", [128, 512], F32)),
        "srow": pst.enter_context(nc.sbuf_tensor(tag + "_srow", [1, 1536], F32)),
        "bc": pst.enter_context(nc.sbuf_tensor(tag + "_bc", [128, 1024], F32)),
        "ps_s": pst.enter_context(nc.psum_tensor(tag + "_pss", [1, 512], F32)),
        "ps_q": pst.enter_context(nc.psum_tensor(tag + "_psq", [1, 512], F32)),
        "ps_b": pst.enter_context(nc.psum_tensor(tag + "_psb", [128, 1024], F32)),
    }


def ln_chunk(k, B, P, ch, rsrc, out_cb, gain=None, bias=None, pre_cb=None):
    tg = B["tag"]
    xc, sq, srow, bc = B["xc"], B["sq"], B["srow"], B["bc"]
    X = lambda ft: (tg + ".x%d" % ft)
    for ft in range(FT):
        k.dma("sp", xc[:, ft * 512:(ft + 1) * 512],
              rsrc[ft * 128:(ft + 1) * 128, ch * 512:(ch + 1) * 512], wr=[X(ft)])
    if pre_cb is not None:
        pre_cb(ch)
    for ft in range(FT):
        xa = xc[:, ft * 512:(ft + 1) * 512]
        k.op("act", lambda e, a=xa: e.activation(sq[:, :], a, AF.Square),
             rd=[X(ft)], wr=[tg + ".sq"])
        k.mm(B["ps_s"][0:1, :], P["onesc"][:, 0:1], xa, ft == 0, ft == FT - 1,
             rd=[X(ft), "onesc"], wr=[tg + ".pss"] if ft == FT - 1 else [],
             war=[tg + ".pss"] if ft == 0 else [], pub=True)
        k.mm(B["ps_q"][0:1, :], P["onesc"][:, 0:1], sq[:, :], ft == 0, ft == FT - 1,
             rd=[tg + ".sq", "onesc"], wr=[tg + ".psq"] if ft == FT - 1 else [],
             war=[tg + ".psq"] if ft == 0 else [], pub=True)
    k.op("dve", lambda e: e.tensor_scalar_mul(srow[0:1, 0:512], B["ps_s"][0:1, :], 1.0 / C),
         rd=[tg + ".pss"], wr=[tg + ".m"])
    k.op("dve", lambda e: e.tensor_scalar_mul(srow[0:1, 512:1024], B["ps_q"][0:1, :], 1.0 / C),
         rd=[tg + ".psq"], wr=[tg + ".q"])
    k.op("dve", lambda e: e.tensor_mul(srow[0:1, 1024:1536], srow[0:1, 0:512], srow[0:1, 0:512]),
         rd=[tg + ".m"], wr=[tg + ".v"])
    k.op("dve", lambda e: e.tensor_sub(srow[0:1, 1024:1536], srow[0:1, 512:1024], srow[0:1, 1024:1536]),
         rd=[tg + ".q", tg + ".v"], wr=[tg + ".v"])
    k.op("dve", lambda e: e.tensor_scalar_add(srow[0:1, 1024:1536], srow[0:1, 1024:1536], EPS),
         rd=[tg + ".v"], wr=[tg + ".v"])
    k.op("act", lambda e: e.activation(srow[0:1, 1024:1536], srow[0:1, 1024:1536], AF.Ln),
         rd=[tg + ".v"], wr=[tg + ".v"])
    k.op("act", lambda e: e.activation(srow[0:1, 512:1024], srow[0:1, 1024:1536], AF.Exp, scale=-0.5),
         rd=[tg + ".v"], wr=[tg + ".q"])
    k.op("dve", lambda e: e.tensor_mul(srow[0:1, 0:512], srow[0:1, 0:512], srow[0:1, 512:1024]),
         rd=[tg + ".q", tg + ".m"], wr=[tg + ".m"])
    k.mm(B["ps_b"][:, 0:512], P["onesr"][0:1, :], srow[0:1, 512:1024], True, True,
         rd=[tg + ".m", tg + ".q", "onesr"], war=[tg + ".psb"], pub=True)
    k.mm(B["ps_b"][:, 512:1024], P["onesr"][0:1, :], srow[0:1, 0:512], True, True,
         rd=[tg + ".m", tg + ".q", "onesr"], wr=[tg + ".psb"], pub=True)
    k.op("act", lambda e: e.copy(bc[:, :], B["ps_b"][:, :]),
         rd=[tg + ".psb"], wr=[tg + ".bc"])
    for ft in range(FT):
        xa = xc[:, ft * 512:(ft + 1) * 512]
        k.op("dve", lambda e, a=xa: e.tensor_mul(a, a, bc[:, 0:512]),
             rd=[X(ft), tg + ".bc"], wr=[X(ft)])
        k.op("dve", lambda e, a=xa: e.tensor_sub(a, a, bc[:, 512:1024]),
             rd=[tg + ".bc"], wr=[X(ft)])
        if gain is not None:
            k.op("dve", lambda e, a=xa, ft=ft: e.tensor_scalar(
                a, a, gain[:, ft:ft + 1], bias[:, ft:ft + 1], ALU.mult, ALU.add),
                rd=["acols"], wr=[X(ft)])
        out_cb(ft, xa)



def build_program(upto="full"):
    nc = bass.Bass("TRN2", target_bir_lowering=False, debug=False)
    dt = nc.dram_tensor

    xfm = dt("xfm", [C, T], F32, kind="ExternalInput").ap()
    xprev = dt("xprev", [128, FT], F32, kind="ExternalInput").ap()
    a_qw = dt("a_qw", [128, FT * C], F32, kind="ExternalInput").ap()
    a_kw = dt("a_kw", [128, FT * C], F32, kind="ExternalInput").ap()
    a_vw = dt("a_vw", [128, FT * C], F32, kind="ExternalInput").ap()
    a_ow = dt("a_ow", [128, FT * C], F32, kind="ExternalInput").ap()
    a_maa_w1 = dt("a_maa_w1", [128, FT * 128], F32, kind="ExternalInput").ap()
    a_maa_w2 = dt("a_maa_w2", [4, 32, C], F32, kind="ExternalInput").ap()
    a_td_w1 = dt("a_td_w1", [128, FT * 64], F32, kind="ExternalInput").ap()
    a_td_w2 = dt("a_td_w2", [64, C], F32, kind="ExternalInput").ap()
    a_cols = dt("a_cols", [128, FT * 12], F32, kind="ExternalInput").ap()
    a_vb_row = dt("a_vb_row", [1, C], F32, kind="ExternalInput").ap()
    b_wqT = dt("b_wqT", [128, FT * C], F32, kind="ExternalInput").ap()
    b_wkT = dt("b_wkT", [128, FT * C], F32, kind="ExternalInput").ap()
    b_wvT = dt("b_wvT", [128, FT * C], F32, kind="ExternalInput").ap()
    b_woT = dt("b_woT", [128, FT * C], F32, kind="ExternalInput").ap()
    b_sw1T = dt("b_sw1T", [128, 128], F32, kind="ExternalInput").ap()
    b_sw2c = dt("b_sw2c", [128, 1], F32, kind="ExternalInput").ap()
    b_sb1c = dt("b_sb1c", [128, 1], F32, kind="ExternalInput").ap()
    b_sb2 = dt("b_sb2", [1, 1], F32, kind="ExternalInput").ap()
    f_wg = dt("f_wg", [NIT, 128, FT * 128], I8, kind="ExternalInput").ap()
    f_wu = dt("f_wu", [NIT, 128, FT * 128], I8, kind="ExternalInput").ap()
    f_wd = dt("f_wd", [FT, 128, NIT * 128], I8, kind="ExternalInput").ap()
    f_sg = dt("f_sg", [128, NIT], F32, kind="ExternalInput").ap()
    f_su = dt("f_su", [128, NIT], F32, kind="ExternalInput").ap()
    f_sd = dt("f_sd", [128, FT], F32, kind="ExternalInput").ap()
    c_eye = dt("c_eye", [128, 128], F32, kind="ExternalInput").ap()
    c_maskD = dt("c_maskD", [128, 64], F32, kind="ExternalInput").ap()
    c_maskS = dt("c_maskS", [128, 64], F32, kind="ExternalInput").ap()
    c_J = dt("c_J", [128, 128], F32, kind="ExternalInput").ap()
    c_Lmask = dt("c_Lmask", [128, 128], F32, kind="ExternalInput").ap()
    c_Hmask = dt("c_Hmask", [128, 128], F32, kind="ExternalInput").ap()
    c_Hsel = dt("c_Hsel", [16, 128], F32, kind="ExternalInput").ap()
    c_iota = dt("c_iota", [128, 512], F32, kind="ExternalInput").ap()
    c_perm = dt("c_perm", [128, 1], U32, kind="ExternalInput").ap()
    c_par = dt("c_par", [128, 1], F32, kind="ExternalInput").ap()

    yout = dt("yfm", [C, T], F32, kind="ExternalOutput").ap()

    ln0 = dt("s_ln0", [C, T], F32).ap()
    xmix = dt("s_xmix", [4, C, T], BF16).ap()
    qa = dt("s_qa", [C, T], BF16).ap()
    kga = dt("s_kga", [C, T], BF16).ap()
    wlog = dt("s_wlog", [C, T], F32).ap()
    agate = dt("s_agate", [C, T], BF16).ap()
    va = dt("s_va", [NPAIR, T, 128], BF16).ap()
    qqg = dt("s_qqg", [C, T], BF16).ap()
    ya = dt("s_ya", [C, T], F32).ap()
    r1 = dt("s_r1", [C, T], F32).ap()
    r2 = dt("s_r2", [C, T], F32).ap()
    r3 = dt("s_r3", [C, T], F32).ap()
    fuse = dt("s_fuse", [INTER, T], BF16).ap()
    qb = dt("s_qb", [C, T], BF16).ap()
    kbt = dt("s_kbt", [HB, T, 128], BF16).ap()
    vbt = dt("s_vbt", [HB, T, 128], BF16).ap()
    dbg = dt("s_dbg", [C, T], F32).ap()
    sfin_t = dt("s_sfin", [NPAIR * 128, 128], F32)
    sfin_cc_t = dt("s_sfin_cc", [2 * NPAIR * 128, 128], F32)
    scl_t = dt("s_scl", [HB, T], F32)
    scl_cc_t = dt("s_scl_cc", [2 * HB, T], F32)
    kvsel_t = dt("s_kvsel", [HB * 256, 128], F32)
    kvsel_cc_t = dt("s_kvsel_cc", [HB * 256, 128], F32)

    D = dict(xfm=xfm, xprev=xprev, a_qw=a_qw, a_kw=a_kw, a_vw=a_vw, a_ow=a_ow,
             a_maa_w1=a_maa_w1, a_maa_w2=a_maa_w2, a_td_w1=a_td_w1, a_td_w2=a_td_w2,
             a_cols=a_cols, a_vb_row=a_vb_row, b_wqT=b_wqT, b_wkT=b_wkT, b_wvT=b_wvT,
             b_woT=b_woT, b_sw1T=b_sw1T, b_sw2c=b_sw2c, b_sb1c=b_sb1c, b_sb2=b_sb2,
             f_wg=f_wg, f_wu=f_wu, f_wd=f_wd, f_sg=f_sg, f_su=f_su, f_sd=f_sd,
             c_eye=c_eye, c_maskD=c_maskD, c_maskS=c_maskS, c_J=c_J, c_Lmask=c_Lmask,
             c_Hmask=c_Hmask, c_Hsel=c_Hsel, c_iota=c_iota, c_perm=c_perm, c_par=c_par,
             yout=yout, ln0=ln0, xmix=xmix, qa=qa, kga=kga, wlog=wlog, agate=agate,
             va=va, qqg=qqg, ya=ya, r1=r1, r2=r2, r3=r3, fuse=fuse, qb=qb, kbt=kbt,
             vbt=vbt, dbg=dbg, sfin_t=sfin_t, sfin_cc_t=sfin_cc_t, scl_t=scl_t,
             scl_cc_t=scl_cc_t, kvsel_t=kvsel_t, kvsel_cc_t=kvsel_cc_t)

    k = K(nc)
    with contextlib.ExitStack() as st:
        for s in SEMS:
            k.sems[s] = st.enter_context(nc.semaphore(s))
        block = st.enter_context(nc.Block())

        P = {}

        def ap(name, shape, dtyp):
            P[name] = st.enter_context(nc.sbuf_tensor(name, shape, dtyp))
            return P[name]

        for name, shape, dtyp in [
                ("ones1", [1, 1], F32), ("onesr", [1, 128], F32), ("onesc", [128, 1], F32),
                ("acols", [128, FT * 12], F32), ("eyeb", [128, 128], BF16),
                ("eyef", [128, 128], F32), ("maskD", [128, 64], F32),
                ("maskS", [128, 64], F32), ("Jb", [128, 128], F32),
                ("Lm", [128, 128], F32), ("Hm", [128, 128], F32),
                ("Hs", [16, 128], F32), ("iotg", [128, 512], F32),
                ("permt", [128, 1], U32), ("parc", [128, 1], F32),
                ("sgc", [128, NIT], F32), ("suc", [128, NIT], F32),
                ("sdc", [128, FT], F32), ("sw1b", [128, 128], BF16),
                ("sw2b", [128, 1], BF16), ("sb1c", [128, 1], F32),
                ("sb2c", [1, 1], F32), ("xprevt", [128, FT], F32),
                ("lnprev", [128, FT], F32), ("vbias", [128, C], F32)]:
            ap(name, shape, dtyp)

        for name, src in [("acols", a_cols), ("Jb", c_J), ("Lm", c_Lmask),
                          ("Hm", c_Hmask), ("iotg", c_iota), ("maskD", c_maskD),
                          ("maskS", c_maskS), ("sgc", f_sg), ("suc", f_su),
                          ("sdc", f_sd), ("sb1c", b_sb1c), ("sb2c", b_sb2),
                          ("xprevt", xprev), ("eyef", c_eye), ("parc", c_par),
                          ("Hs", c_Hsel), ("permt", c_perm)]:
            k.dma("sp", P[name][:, :], src[:, :], wr=[name])
        k.dma("pool", P["eyeb"][:, :], c_eye[:, :], wr=["eyeb"])
        k.dma("pool", P["sw1b"][:, :], b_sw1T[:, :], wr=["sw1b"])
        k.dma("pool", P["sw2b"][:, :], b_sw2c[:, :], wr=["sw2b"])
        k.op("dve", lambda e: e.memset(P["ones1"][:, :], 1.0), wr=["ones1"])
        k.op("dve", lambda e: e.memset(P["onesr"][:, :], 1.0), wr=["onesr"])
        k.op("dve", lambda e: e.memset(P["onesc"][:, :], 1.0), wr=["onesc"])
        with nc.sbuf_tensor("vb_row", [1, C], F32) as vb_row, \
                nc.psum_tensor("vb_ps", [128, 512], F32) as vb_ps:
            k.dma("sp", vb_row[0:1, :], a_vb_row[0:1, :], wr=["vbrow"])
            for chv in range(4):
                k.mm(vb_ps[:, :], P["onesr"][0:1, :], vb_row[0:1, chv * 512:(chv + 1) * 512],
                     True, True, rd=["onesr", "vbrow"], wr=["vbps"], pub=True)
                k.op("act", lambda e, chv=chv: e.copy(P["vbias"][:, chv * 512:(chv + 1) * 512],
                                                      vb_ps[:, :]),
                     rd=["vbps"], wr=["vbias"])
        k.barrier()

        _phase_ln0(k, nc, P, D)
        if upto == "ln0":
            _dbg_copy(k, ln0, yout, F32); k.finalize(); k.replay(block); return nc
        _phase_mix(k, nc, P, D)
        if upto == "mix":
            _dbg_copy(k, xmix[0], yout, BF16); k.finalize(); k.replay(block); return nc
        _phase_proj_a(k, nc, P, D)
        if upto == "proj_a":
            _dbg_copy(k, wlog, yout, F32); k.finalize(); k.replay(block); return nc
        _phase_scan(k, nc, P, D)
        if upto == "scan":
            _dbg_copy(k, dbg if globals().get("_SCAN_DBG") else ya, yout, F32); k.finalize(); k.replay(block); return nc
        _phase_attn_a_out(k, nc, P, D)
        if upto == "attn_a":
            _dbg_copy(k, r1, yout, F32); k.finalize(); k.replay(block); return nc
        _phase_ffn(k, nc, P, D, r1, r2, "F1")
        if upto == "fuse1":
            if globals().get("_FFN_DBG"):
                _dbg_copy(k, dbg, yout, F32)
            else:
                _dbg_copy(k, fuse[0:C, :], yout, BF16)
            k.finalize(); k.replay(block); return nc
        if upto == "ffn1":
            _dbg_copy(k, r2, yout, F32); k.finalize(); k.replay(block); return nc
        _phase_attn_b(k, nc, P, D, upto)
        if upto in ("xb", "scores", "sel", "ksel"):
            if not globals().get("_NO_DBGCOPY"):
                _dbg_copy(k, dbg, yout, F32)
            k.finalize(); k.replay(block); return nc
        if upto == "attn_b":
            _dbg_copy(k, r3, yout, F32); k.finalize(); k.replay(block); return nc
        _phase_ffn(k, nc, P, D, r3, yout, "F2")
        k.finalize()
        k.replay(block)
    return nc


def _dbg_copy(k, src, dstap, sdtype):
    nc = k.nc
    k.barrier()
    with nc.sbuf_tensor("dbg_t", [128, 2048], F32) as tt:
        for ft in range(FT):
            q = "pool" if sdtype != F32 else "sp"
            k.dma(q, tt[:, :], src[ft * 128:(ft + 1) * 128, 0:T], wr=["dbgt"])
            k.dma("sp", dstap[ft * 128:(ft + 1) * 128, 0:T], tt[:, :], rd=["dbgt"])
    k.barrier()


def _phase_ln0(k, nc, P, D):
    with contextlib.ExitStack() as pst:
        B = alloc_ln_bufs(nc, pst, "A")
        for ch in range(4):
            def out_cb(ft, xa, ch=ch):
                k.dma("sp", D["ln0"][ft * 128:(ft + 1) * 128, ch * 512:(ch + 1) * 512],
                      xa, rd=["A.x%d" % ft])
            ln_chunk(k, B, P, ch, D["xfm"], out_cb)
        sq, srow, bc = B["sq"], B["srow"], B["bc"]
        k.op("act", lambda e: e.activation(sq[:, 0:FT], P["xprevt"][:, :], AF.Square),
             rd=["xprevt"], wr=["A.sq"])
        k.op("dve", lambda e: e.reduce_sum(sq[:, 16:17], P["xprevt"][:, :], axis=AX.X),
             rd=["xprevt"], wr=["A.c1"])
        k.op("dve", lambda e: e.reduce_sum(sq[:, 17:18], sq[:, 0:FT], axis=AX.X),
             rd=["A.sq"], wr=["A.c2"])
        k.mm(B["ps_s"][0:1, 0:2], P["onesc"][:, 0:1], sq[:, 16:18], True, True,
             rd=["A.c1", "A.c2", "onesc"], wr=["A.pss"], pub=True)
        k.op("dve", lambda e: e.tensor_scalar_mul(srow[0:1, 0:2], B["ps_s"][0:1, 0:2], 1.0 / C),
             rd=["A.pss"], wr=["A.m"])
        k.op("dve", lambda e: e.tensor_mul(srow[0:1, 2:3], srow[0:1, 0:1], srow[0:1, 0:1]),
             rd=["A.m"], wr=["A.v"])
        k.op("dve", lambda e: e.tensor_sub(srow[0:1, 2:3], srow[0:1, 1:2], srow[0:1, 2:3]),
             rd=["A.m"], wr=["A.v"])
        k.op("dve", lambda e: e.tensor_scalar_add(srow[0:1, 2:3], srow[0:1, 2:3], EPS),
             wr=["A.v"])
        k.op("act", lambda e: e.activation(srow[0:1, 2:3], srow[0:1, 2:3], AF.Ln), wr=["A.v"])
        k.op("act", lambda e: e.activation(srow[0:1, 2:3], srow[0:1, 2:3], AF.Exp, scale=-0.5),
             wr=["A.v"])
        k.op("dve", lambda e: e.tensor_mul(srow[0:1, 3:4], srow[0:1, 0:1], srow[0:1, 2:3]),
             rd=["A.m", "A.v"], wr=["A.mr"])
        k.mm(B["ps_b"][:, 0:2], P["onesr"][0:1, :], srow[0:1, 2:4], True, True,
             rd=["A.v", "A.mr", "onesr"], wr=["A.psb"], pub=True)
        k.op("act", lambda e: e.copy(bc[:, 0:2], B["ps_b"][:, 0:2]), rd=["A.psb"], wr=["A.bc"])
        k.op("dve", lambda e: e.tensor_scalar(P["lnprev"][:, :], P["xprevt"][:, :],
                                              bc[:, 0:1], bc[:, 1:2], ALU.mult, ALU.subtract),
             rd=["A.bc", "xprevt"], wr=["lnprev"])
        k.barrier()


def _phase_mix(k, nc, P, D):
    acols = P["acols"]
    with contextlib.ExitStack() as pst:
        lx = pst.enter_context(nc.sbuf_tensor("M_lx", [128, FT * 513], F32))
        dx = pst.enter_context(nc.sbuf_tensor("M_dx", [128, FT * 512], F32))
        xxxb = pst.enter_context(nc.sbuf_tensor("M_xxx", [128, FT * 512], BF16))
        w1b = pst.enter_context(nc.sbuf_tensor("M_w1", [128, FT * 128], BF16))
        w2b = pst.enter_context(nc.sbuf_tensor("M_w2", [128, FT * 128], BF16))
        mmb = pst.enter_context(nc.sbuf_tensor("M_mm", [128, 512], BF16))
        tmp = pst.enter_context(nc.sbuf_tensor("M_tmp", [128, 512], F32))
        outt = pst.enter_context(nc.sbuf_tensor("M_out", [128, 512], BF16))
        ps_mm = pst.enter_context(nc.psum_tensor("M_psmm", [128, 512], F32))
        ps_m = pst.enter_context(nc.psum_tensor("M_psm", [128, 512], F32))

        k.dma("pool", w1b[:, :], D["a_maa_w1"][:, :], wr=["M.w1"])
        for r in range(4):
            k.dma("pool", w2b[32 * r:32 * (r + 1), :], D["a_maa_w2"][r, :, :],
                  wr=["M.w2_%d" % r])
        for ch in range(4):
            for ft in range(FT):
                base = ft * 513
                if ch == 0:
                    k.dma("sp", lx[:, base + 1:base + 513],
                          D["ln0"][ft * 128:(ft + 1) * 128, 0:512], wr=["M.lx%d" % ft])
                    k.op("dve", lambda e, b=base, ft=ft:
                         e.tensor_copy(lx[:, b:b + 1], P["lnprev"][:, ft:ft + 1]),
                         rd=["lnprev"], wr=["M.lx%d" % ft])
                else:
                    k.dma("sp", lx[:, base:base + 513],
                          D["ln0"][ft * 128:(ft + 1) * 128, ch * 512 - 1:(ch + 1) * 512],
                          wr=["M.lx%d" % ft])
            for ft in range(FT):
                base, o = ft * 513, ft * 512
                k.op("dve", lambda e, b=base, o=o:
                     e.tensor_sub(dx[:, o:o + 512], lx[:, b:b + 512], lx[:, b + 1:b + 513]),
                     rd=["M.lx%d" % ft], wr=["M.dx%d" % ft])
                k.op("dve", lambda e, b=base, o=o, ft=ft: e.scalar_tensor_tensor(
                    tmp[:, :], dx[:, o:o + 512], acols[:, ft:ft + 1], lx[:, b + 1:b + 513],
                    ALU.mult, ALU.add), rd=["M.dx%d" % ft, "M.lx%d" % ft, "acols"], wr=["M.tmp"])
                k.op("dve", lambda e, o=o: e.tensor_copy(xxxb[:, o:o + 512], tmp[:, :]),
                     rd=["M.tmp"], wr=["M.xxx%d" % ft])
            for ft in range(FT):
                k.mm(ps_mm[:, :], w1b[:, ft * 128:(ft + 1) * 128],
                     xxxb[:, ft * 512:(ft + 1) * 512], ft == 0, ft == FT - 1,
                     rd=["M.w1", "M.xxx%d" % ft],
                     wr=["M.psmm"] if ft == FT - 1 else [],
                     war=["M.psmm"] if ft == 0 else [], pub=(ft == FT - 1))
            k.op("act", lambda e: e.activation(mmb[:, :], ps_mm[:, :], AF.Tanh),
                 rd=["M.psmm"], wr=["M.mmb"])
            for r in range(4):
                for ft in range(FT):
                    k.mm(ps_m[:, :], w2b[32 * r:32 * (r + 1), ft * 128:(ft + 1) * 128],
                         mmb[32 * r:32 * (r + 1), :], True, True,
                         rd=["M.w2_%d" % r, "M.mmb"], wr=["M.psm"], pub=True,
                         tp=(32 * r, 0) if r == 3 else None)
                    k.op("dve", lambda e, r=r, ft=ft: e.tensor_scalar_add(
                        tmp[:, :], ps_m[:, :], acols[:, (1 + r) * FT + ft:(1 + r) * FT + ft + 1]),
                        rd=["M.psm", "acols"], wr=["M.tmp"])
                    k.op("dve", lambda e, o=ft * 512: e.tensor_mul(tmp[:, :], tmp[:, :], dx[:, o:o + 512]),
                         rd=["M.dx%d" % ft], wr=["M.tmp"])
                    k.op("dve", lambda e, b=ft * 513: e.tensor_add(
                        outt[:, :], tmp[:, :], lx[:, b + 1:b + 513]),
                        rd=["M.tmp", "M.lx%d" % ft], wr=["M.out"])
                    k.dma("sp", D["xmix"][r, ft * 128:(ft + 1) * 128, ch * 512:(ch + 1) * 512],
                          outt[:, :], rd=["M.out"])
        k.barrier()


def _phase_proj_a(k, nc, P, D):
    acols = P["acols"]
    with contextlib.ExitStack() as pst:
        xs = pst.enter_context(nc.sbuf_tensor("P_x", [128, FT * T], BF16))
        wb = pst.enter_context(nc.sbuf_tensor("P_w", [128, FT * C], BF16))
        ob = pst.enter_context(nc.sbuf_tensor("P_ob", [128, 512], BF16))
        o2 = pst.enter_context(nc.sbuf_tensor("P_o2", [128, 512], BF16))
        of = pst.enter_context(nc.sbuf_tensor("P_of", [128, 512], F32))
        gt = pst.enter_context(nc.sbuf_tensor("P_g", [128, 512], BF16))
        gf = pst.enter_context(nc.sbuf_tensor("P_gf", [128, 512], F32))
        td1b = pst.enter_context(nc.sbuf_tensor("P_td1", [128, FT * 64], BF16))
        td2b = pst.enter_context(nc.sbuf_tensor("P_td2", [64, C], BF16))
        tdb = pst.enter_context(nc.sbuf_tensor("P_tdb", [64, 512], BF16))
        ps = [pst.enter_context(nc.psum_tensor(f"P_ps{i}", [128, 512], F32)) for i in range(2)]
        ps_td = pst.enter_context(nc.psum_tensor("P_pstd", [64, 512], F32))

        def load_plane(r):
            for ft in range(FT):
                k.dma("pool", xs[:, ft * T:(ft + 1) * T],
                      D["xmix"][r, ft * 128:(ft + 1) * 128, :], wr=["P.x%d" % ft])

        load_plane(3)
        k.dma("pool", td1b[:, :], D["a_td_w1"][:, :], wr=["P.td1"])
        k.dma("pool", td2b[:, :], D["a_td_w2"][:, :], wr=["P.td2"])
        for ch in range(4):
            for ft in range(FT):
                k.mm(ps_td[:, :], td1b[:, ft * 64:(ft + 1) * 64],
                     xs[:, ft * T + ch * 512:ft * T + (ch + 1) * 512], ft == 0, ft == FT - 1,
                     rd=["P.td1", "P.x%d" % ft], wr=["P.pstd"] if ft == FT - 1 else [],
                     war=["P.pstd"] if ft == 0 else [], pub=(ft == FT - 1))
            k.op("act", lambda e: e.activation(tdb[:, :], ps_td[:, :], AF.Tanh),
                 rd=["P.pstd"], wr=["P.tdb"])
            for ft2 in range(FT):
                b = ft2 % 2
                k.mm(ps[b][:, :], td2b[0:64, ft2 * 128:(ft2 + 1) * 128], tdb[:, :], True, True,
                     rd=["P.td2", "P.tdb"], wr=["P.ps%d" % b], pub=True)
                k.op("act", lambda e, b=b, ft2=ft2: e.activation(
                    of[:, :], ps[b][:, :], AF.Exp,
                    bias=acols[:, 8 * FT + ft2:8 * FT + ft2 + 1]),
                    rd=["P.ps%d" % b, "acols"], wr=["P.of"])
                k.op("dve", lambda e: e.tensor_scalar(of[:, :], of[:, :], -1.0, -5.0,
                                                      ALU.mult, ALU.max), wr=["P.of"])
                k.dma("sp", D["wlog"][ft2 * 128:(ft2 + 1) * 128, ch * 512:(ch + 1) * 512],
                      of[:, :], rd=["P.of"])
                k.op("act", lambda e: e.activation(gf[:, :], of[:, :], AF.Exp),
                     rd=["P.of"], wr=["P.gf"])
                k.op("dve", lambda e: e.tensor_scalar(gt[:, :], gf[:, :], -1.0, 1.0,
                                                      ALU.mult, ALU.add),
                     rd=["P.gf"], wr=["P.gt"])
                k.dma("sp", D["agate"][ft2 * 128:(ft2 + 1) * 128, ch * 512:(ch + 1) * 512],
                      gt[:, :], rd=["P.gt"])
        k.barrier()

        load_plane(0)
        k.dma("pool", wb[:, :], D["a_qw"][:, :], wr=["P.wb"])
        for ft2 in range(FT):
            for ch in range(4):
                b = (ft2 * 4 + ch) % 2
                for ft in range(FT):
                    k.mm(ps[b][:, :], wb[:, ft * C + ft2 * 128:ft * C + (ft2 + 1) * 128],
                         xs[:, ft * T + ch * 512:ft * T + (ch + 1) * 512],
                         ft == 0, ft == FT - 1, rd=["P.wb", "P.x%d" % ft],
                         wr=["P.ps%d" % b] if ft == FT - 1 else [],
                         war=["P.ps%d" % b] if ft == 0 else [], pub=(ft == FT - 1))
                k.op("act", lambda e, b=b, ft2=ft2: e.activation(
                    ob[:, :], ps[b][:, :], AF.Identity,
                    bias=acols[:, 5 * FT + ft2:5 * FT + ft2 + 1]),
                    rd=["P.ps%d" % b, "acols"], wr=["P.obf"])
                k.dma("sp", D["qa"][ft2 * 128:(ft2 + 1) * 128, ch * 512:(ch + 1) * 512],
                      ob[:, :], rd=["P.obf"])
        k.barrier()

        load_plane(1)
        k.dma("pool", wb[:, :], D["a_kw"][:, :], wr=["P.wb"])
        for ft2 in range(FT):
            for ch in range(4):
                b = (ft2 * 4 + ch) % 2
                for ft in range(FT):
                    k.mm(ps[b][:, :], wb[:, ft * C + ft2 * 128:ft * C + (ft2 + 1) * 128],
                         xs[:, ft * T + ch * 512:ft * T + (ch + 1) * 512],
                         ft == 0, ft == FT - 1, rd=["P.wb", "P.x%d" % ft],
                         wr=["P.ps%d" % b] if ft == FT - 1 else [],
                         war=["P.ps%d" % b] if ft == 0 else [], pub=(ft == FT - 1))
                k.op("act", lambda e, b=b, ft2=ft2: e.activation(
                    of[:, :], ps[b][:, :], AF.Identity,
                    bias=acols[:, 6 * FT + ft2:6 * FT + ft2 + 1]),
                    rd=["P.ps%d" % b, "acols"], wr=["P.of"])
                k.dma("pool", gt[:, :],
                      D["agate"][ft2 * 128:(ft2 + 1) * 128, ch * 512:(ch + 1) * 512],
                      wr=["P.gt"])
                k.op("dve", lambda e: e.tensor_mul(ob[:, :], of[:, :], gt[:, :]),
                     rd=["P.of", "P.gt"], wr=["P.obf"])
                k.dma("sp", D["kga"][ft2 * 128:(ft2 + 1) * 128, ch * 512:(ch + 1) * 512],
                      ob[:, :], rd=["P.obf"])
        k.barrier()

        load_plane(2)
        k.dma("pool", wb[:, :], D["a_vw"][:, :], wr=["P.wb"])
        for tt in range(FT):
            for dv in range(4):
                b = (tt * 4 + dv) % 2
                for ft in range(FT):
                    k.mm(ps[b][:, :], xs[:, ft * T + tt * 128:ft * T + (tt + 1) * 128],
                         wb[:, ft * C + dv * 512:ft * C + (dv + 1) * 512],
                         ft == 0, ft == FT - 1, rd=["P.wb", "P.x%d" % ft],
                         wr=["P.ps%d" % b] if ft == FT - 1 else [],
                         war=["P.ps%d" % b] if ft == 0 else [], pub=(ft == FT - 1))
                k.op("dve", lambda e, b=b, dv=dv: e.tensor_add(
                    of[:, :], ps[b][:, :], P["vbias"][:, dv * 512:(dv + 1) * 512]),
                    rd=["P.ps%d" % b, "vbias"], wr=["P.of"])
                k.op("dve", lambda e: e.tensor_copy(o2[:, :], of[:, :]),
                     rd=["P.of"], wr=["P.o2"])
                for sl in range(4):
                    p = dv * 4 + sl
                    k.dma("sp", D["va"][p, tt * 128:(tt + 1) * 128, :],
                          o2[:, sl * 128:(sl + 1) * 128], rd=["P.o2"])
        k.barrier()


def _phase_scan(k, nc, P, D):
    sfin = D["sfin_t"].ap()
    with contextlib.ExitStack() as pst:
        def sb(name, shape, dtyp):
            return pst.enter_context(nc.sbuf_tensor(name, shape, dtyp))
        qs2 = [sb("S_q0", [128, T], BF16), sb("S_q1", [128, T], BF16)]
        ks2 = [sb("S_k0", [128, T], BF16), sb("S_k1", [128, T], BF16)]
        ws2 = [sb("S_w0", [128, T], F32), sb("S_w1", [128, T], F32)]
        Bg = sb("S_Bg", [128, T], F32)
        zs = sb("S_z", [128, T], F32)
        ys = sb("S_y", [128, T], F32)
        qq = sb("S_qq", [128, T], BF16)
        eBg = sb("S_eBg", [128, T], F32)
        negB = sb("S_negB", [128, T], F32)
        vbd2 = [sb("S_vbd0", [128, NCH * 128], BF16), sb("S_vbd1", [128, NCH * 128], BF16)]
        vsl2 = [sb("S_vsl0", [64, NCH * 128], BF16), sb("S_vsl1", [64, NCH * 128], BF16)]
        argQ = sb("S_aq", [128, 64], F32)
        argH = sb("S_ah", [128, 64], F32)
        argM = sb("S_am", [128, 64], F32)
        argG = sb("S_ag", [128, 129], F32)
        eQ = sb("S_eq", [128, 64], F32)
        eK = sb("S_ek", [128, 64], F32)
        eH = sb("S_eh", [128, 64], F32)
        eM = sb("S_em", [128, 64], F32)
        eG = sb("S_eg", [128, 129], F32)
        qcat = sb("S_qcat", [128, 64], BF16)
        lhsKT = sb("S_lkt", [128, 128], BF16)
        lhsKH = sb("S_lkh", [128, 128], BF16)
        lhsKG = sb("S_lkg", [128, 128], BF16)
        qgz = sb("S_qgz", [128, 64], BF16)
        qglob = sb("S_qgl", [128, 64], BF16)
        kglob = sb("S_kgl", [128, 64], BF16)
        khT = sb("S_khT", [64, 128], BF16)
        ptf = sb("S_ptf", [128, 64], F32)
        ptf2 = sb("S_ptf2", [128, 64], F32)
        PTb = sb("S_PTb", [128, 64], BF16)
        Sf = sb("S_Sf", [128, 128], F32)
        Sbd = sb("S_Sbd", [128, 128], BF16)
        psD = pst.enter_context(nc.psum_tensor("S_psD", [128, 64], F32))
        psS2 = pst.enter_context(nc.psum_tensor("S_psS", [128, 64], F32))
        psQ = pst.enter_context(nc.psum_tensor("S_psQ", [128, 64], F32))
        psO = pst.enter_context(nc.psum_tensor("S_psO", [128, 64], F32))
        psT = pst.enter_context(nc.psum_tensor("S_psT", [64, 128], BF16))
        psS = pst.enter_context(nc.psum_tensor("S_psSt", [128, 128], F32))

        for t_ in (lhsKT, lhsKH, lhsKG, qgz, Sbd):
            k.op("dve", lambda e, t=t_: e.memset(t[:, :], 0.0),
                 wr=["S.init"])
        for _vb in vbd2:
            k.op("dve", lambda e, v=_vb: e.memset(
                v[0:64, :].rearrange("s (c d) -> s c d", d=128)[:, :, 64:128], 0.0),
                wr=["S.init"])
            k.op("dve", lambda e, v=_vb: e.memset(
                v[64:128, :].rearrange("s (c d) -> s c d", d=128)[:, :, 0:64], 0.0),
                wr=["S.init"])
        k.op("dve", lambda e: e.memset(zs[:, :], 0.0), wr=["S.z"])
        k.barrier()

        zcol = zs[:, 0:1]
        for p in range(NPAIR):
            r0 = p * 128
            bi = p % 2
            qs, ks, ws, vbd, vsl = qs2[bi], ks2[bi], ws2[bi], vbd2[bi], vsl2[bi]
            KQ, KK, KW = "S.qs%d" % bi, "S.ks%d" % bi, "S.ws%d" % bi
            KVB, KVS = "S.vbd%d" % bi, "S.vsl%d" % bi
            k.dma("pool", qs[:, :], D["qa"][r0:r0 + 128, :], wr=[KQ])
            k.dma("pool", ks[:, :], D["kga"][r0:r0 + 128, :], wr=[KK])
            k.dma("sp", ws[:, :], D["wlog"][r0:r0 + 128, :], wr=[KW])
            k.dma("sp", vbd[0:64, :].rearrange("s (c d) -> s c d", d=128)[:, :, 0:64],
                  D["va"][p].rearrange("(c s) d -> s c d", s=64, c=32)[:, :, 0:64], wr=[KVB])
            k.dma("sp", vbd[64:128, :].rearrange("s (c d) -> s c d", d=128)[:, :, 64:128],
                  D["va"][p].rearrange("(c s) d -> s c d", s=64, c=32)[:, :, 64:128], wr=[KVB])
            k.dma("sp", vsl[0:64, :].rearrange("s (c d) -> s c d", d=128),
                  D["va"][p].rearrange("(c s) d -> s c d", s=64, c=32), wr=[KVS])
            k.op("dve", lambda e, ws=ws: e.tensor_tensor_scan(
                Bg[:, :], ws[:, :], zs[:, :], 0.0, ALU.add, ALU.add),
                rd=[KW, "S.z"], wr=["S.Bg"])
            k.op("act", lambda e: e.activation(eBg[:, :], Bg[:, :], AF.Exp),
                 rd=["S.Bg"], wr=["S.eBg"])
            k.op("dve", lambda e: e.tensor_scalar_mul(negB[:, :], Bg[:, :], -1.0),
                 rd=["S.Bg"], wr=["S.negB"])
            k.op("dve", lambda e, qs=qs: e.scalar_tensor_tensor(
                qq[:, :], qs[:, :], SCL_A, eBg[:, :], ALU.mult, ALU.mult),
                rd=["S.eBg", KQ], wr=["S.qqb"])
            k.dma("sp", D["qqg"][r0:r0 + 128, :], qq[:, :], rd=["S.qqb"])
            k.op("dve", lambda e: e.memset(Sf[:, :], 0.0), wr=["S.Sf"])
            k.op("dve", lambda e: e.memset(Sbd[:, :], 0.0), wr=["S.Sbd"])

            for c in range(NCH):
                c0 = c * LCH
                scol = Bg[:, c0 - 1:c0] if c > 0 else zcol
                mcol = Bg[:, c0 + 31:c0 + 32]
                ecol = Bg[:, c0 + 63:c0 + 64]
                for i in range(4):
                    nbcol = (negB[:, c0 + 16 * i - 1:c0 + 16 * i] if (c0 + 16 * i) > 0
                             else zcol)
                    necj = negB[:, c0 + 16 * (i + 1) - 1:c0 + 16 * (i + 1)]
                    k.op("act", lambda e, i=i, nbcol=nbcol, c0=c0: e.activation(
                        argQ[:, 16 * i:16 * (i + 1)], Bg[:, c0 + 16 * i:c0 + 16 * (i + 1)],
                        AF.Identity, bias=nbcol),
                        rd=["S.Bg", "S.negB", "S.z"], wr=["S.argQ"] if i == 3 else ["S.argQp"])
                    k.op("act", lambda e, i=i, necj=necj, c0=c0: e.activation(
                        argH[:, 16 * i:16 * (i + 1)], Bg[:, c0 + 16 * i:c0 + 16 * (i + 1)],
                        AF.Identity, bias=necj),
                        rd=["S.Bg", "S.negB"], wr=["S.argH"] if i == 3 else ["S.argHp"])
                k.op("dve", lambda e, c0=c0, mcol=mcol: e.tensor_scalar_sub(
                    argM[:, :], Bg[:, c0:c0 + 64], mcol), rd=["S.Bg"], wr=["S.argM"])
                k.op("dve", lambda e: e.tensor_scalar_mul(argM[:, 0:32], argM[:, 0:32], -1.0),
                     wr=["S.argM"])
                k.op("dve", lambda e, c0=c0, scol=scol: e.tensor_scalar_sub(
                    argG[:, 0:64], Bg[:, c0:c0 + 64], scol), rd=["S.Bg", "S.z"], wr=["S.argG1"])
                k.op("dve", lambda e, c0=c0, ecol=ecol: e.tensor_scalar_sub(
                    argG[:, 64:128], Bg[:, c0:c0 + 64], ecol), rd=["S.Bg"], wr=["S.argG2"])
                k.op("dve", lambda e: e.tensor_scalar_mul(argG[:, 64:128], argG[:, 64:128], -1.0),
                     wr=["S.argG2"])
                k.op("dve", lambda e, scol=scol, ecol=ecol: e.tensor_scalar_sub(
                    argG[:, 128:129], ecol, scol), rd=["S.Bg", "S.z"], wr=["S.argG3"])
                k.op("act", lambda e: e.activation(eQ[:, :], argQ[:, :], AF.Exp),
                     rd=["S.argQ", "S.argQp"], wr=["S.eQ"])
                k.op("act", lambda e: e.activation(eK[:, :], argQ[:, :], AF.Exp, scale=-1.0),
                     rd=["S.argQ", "S.argQp"], wr=["S.eK"])
                k.op("act", lambda e: e.activation(eH[:, :], argH[:, :], AF.Exp, scale=-1.0),
                     rd=["S.argH", "S.argHp"], wr=["S.eH"])
                k.op("act", lambda e: e.activation(eM[:, :], argM[:, :], AF.Exp),
                     rd=["S.argM"], wr=["S.eM"])
                k.op("act", lambda e: e.activation(eG[:, :], argG[:, :], AF.Exp),
                     rd=["S.argG1", "S.argG2", "S.argG3"], wr=["S.eG"])
                k.op("dve", lambda e, c0=c0, qs=qs, ks=ks: e.tensor_mul(qcat[:, :], qs[:, c0:c0 + 64], eQ[:, :]),
                     rd=["S.eQ", KQ], wr=["S.qcat"])
                k.op("dve", lambda e, c0=c0, qs=qs, ks=ks: e.tensor_mul(
                    lhsKT[0:64, 0:64], ks[0:64, c0:c0 + 64], eK[0:64, :]),
                    rd=["S.eK", KK], wr=["S.lKT"])
                k.op("dve", lambda e, c0=c0, qs=qs, ks=ks: e.tensor_mul(
                    lhsKT[64:128, 64:128], ks[64:128, c0:c0 + 64], eK[64:128, :]),
                    rd=["S.eK", KK], wr=["S.lKT"])
                k.op("pool", lambda e, c0=c0, qs=qs, ks=ks: e.tensor_mul(
                    lhsKH[0:64, 0:64], ks[0:64, c0:c0 + 64], eH[0:64, :]),
                    rd=["S.eH", KK], wr=["S.lKH"])
                k.op("pool", lambda e, c0=c0, qs=qs, ks=ks: e.tensor_mul(
                    lhsKH[64:128, 64:128], ks[64:128, c0:c0 + 64], eH[64:128, :]),
                    rd=["S.eH", KK], wr=["S.lKH"])
                k.op("pool", lambda e, c0=c0, qs=qs, ks=ks: e.tensor_mul(
                    lhsKG[0:64, 0:32], ks[0:64, c0:c0 + 32], eM[0:64, 0:32]),
                    rd=["S.eM", KK], wr=["S.lKG"])
                k.op("pool", lambda e, c0=c0, qs=qs, ks=ks: e.tensor_mul(
                    lhsKG[64:128, 64:96], ks[64:128, c0:c0 + 32], eM[64:128, 0:32]),
                    rd=["S.eM", KK], wr=["S.lKG"])
                k.op("dve", lambda e, c0=c0, qs=qs, ks=ks: e.tensor_mul(
                    qgz[:, 32:64], qs[:, c0 + 32:c0 + 64], eM[:, 32:64]),
                    rd=["S.eM", KQ], wr=["S.qgz"])
                k.op("dve", lambda e, c0=c0, qs=qs, ks=ks: e.tensor_mul(
                    qglob[:, :], qs[:, c0:c0 + 64], eG[:, 0:64]),
                    rd=["S.eG", KQ], wr=["S.qglob"])
                k.op("dve", lambda e, c0=c0, qs=qs, ks=ks: e.tensor_mul(
                    kglob[:, :], ks[:, c0:c0 + 64], eG[:, 64:128]),
                    rd=["S.eG", KK], wr=["S.kglob"])
                k.mm(psD[:, :], lhsKT[:, :], qcat[:, :], True, True,
                     rd=["S.lKT", "S.qcat"], wr=["S.psD"], pub=True)
                k.mm(psS2[:, :], lhsKH[:, :], qcat[:, :], True, True,
                     rd=["S.lKH", "S.qcat"], wr=["S.psS2"], pub=True)
                k.mm(psQ[:, :], lhsKG[:, :], qgz[:, :], True, True,
                     rd=["S.lKG", "S.qgz"], wr=["S.psQ"], pub=True)
                k.op("dve", lambda e: e.tensor_mul(ptf[:, :], psD[:, :], P["maskD"][:, :]),
                     rd=["S.psD", "maskD"], wr=["S.ptf"])
                k.op("dve", lambda e: e.tensor_mul(ptf2[:, :], psS2[:, :], P["maskS"][:, :]),
                     rd=["S.psS2", "maskS"], wr=["S.ptf2"])
                k.op("dve", lambda e: e.tensor_add(ptf[:, :], ptf[:, :], ptf2[:, :]),
                     rd=["S.ptf2"], wr=["S.ptf"])
                k.op("dve", lambda e: e.tensor_add(PTb[:, :], ptf[:, :], psQ[:, :]),
                     rd=["S.ptf", "S.psQ"], wr=["S.PTb"])
                k.mm(psO[:, :], Sbd[:, :], qglob[:, :], True, False,
                     rd=["S.Sbd", "S.qglob"], wr=[], pub=False)
                k.mm(psO[:, :], vbd[:, c * 128:(c + 1) * 128], PTb[:, :], False, True,
                     rd=[KVB, "S.PTb"], wr=["S.psO"], pub=True)
                k.op("act", lambda e, c0=c0, qs=qs, ks=ks: e.activation(
                    ys[:, c0:c0 + 64], psO[:, :], AF.Copy, scale=SCL_A),
                    rd=["S.psO"], wr=["S.ys"])
                k.op("pe", lambda e: e.transpose(psT[:, :], kglob[:, :], P["eyeb"][:, :]),
                     rd=["S.kglob", "eyeb"], wr=["S.psT"], pub=True)
                k.op("act", lambda e: e.copy(khT[:, :], psT[:, :]),
                     rd=["S.psT"], wr=["S.khT"])
                k.mm(psS[:, :], khT[0:64, :], vsl[0:64, c * 128:(c + 1) * 128], True, True,
                     rd=["S.khT", KVS], wr=["S.psS"], pub=True)
                k.op("dve", lambda e: e.tensor_scalar_mul(Sf[:, :], Sf[:, :], eG[:, 128:129]),
                     rd=["S.eG"], wr=["S.Sf"])
                k.op("dve", lambda e: e.tensor_add(Sf[0:64, 0:64], Sf[0:64, 0:64], psS[0:64, 0:64]),
                     rd=["S.psS"], wr=["S.Sf"])
                k.op("dve", lambda e: e.tensor_add(Sf[64:128, 64:128], Sf[64:128, 64:128],
                                                   psS[64:128, 64:128]),
                     rd=["S.psS"], wr=["S.Sf"])
                k.op("dve", lambda e: e.tensor_copy(Sbd[0:64, 0:64], Sf[0:64, 0:64]),
                     rd=["S.Sf"], wr=["S.Sbd"])
                k.op("dve", lambda e: e.tensor_copy(Sbd[64:128, 64:128], Sf[64:128, 64:128]),
                     rd=["S.Sf"], wr=["S.Sbd"])
                if globals().get("_SCAN_DBG") and p == 0 and c in (0, 1):
                    dbg = D["dbg"]
                    k.dma("sp", dbg[0:128, 256 * c:256 * c + 64], Bg[:, c * 64:(c + 1) * 64],
                          rd=["S.Bg"])
                    k.dma("sp", dbg[0:128, 256 * c + 64:256 * c + 128], ys[:, c * 64:(c + 1) * 64],
                          rd=["S.ys"])
                    k.dma("pool", dbg[128:256, 256 * c:256 * c + 64], PTb[:, :], rd=["S.PTb"])
                    k.dma("pool", dbg[128:256, 256 * c + 64:256 * c + 128], qglob[:, :],
                          rd=["S.qglob"])
                    k.dma("sp", dbg[256:384, 128 * c:128 * (c + 1)], Sf[:, :], rd=["S.Sf"])
                    k.dma("pool", dbg[384:512, 256 * c:256 * c + 64], kglob[:, :], rd=["S.kglob"])
                    k.dma("pool", dbg[384:448, 256 * c + 64:256 * c + 192],
                          khT[0:64, :], rd=["S.khT"])
                    k.dma("pool", dbg[512:640, 256 * c:256 * c + 64], qcat[:, :], rd=["S.qcat"])
                    k.dma("pool", dbg[512:640, 256 * c + 64:256 * c + 192], lhsKT[:, :],
                          rd=["S.lKT"])
                    k.dma("sp", dbg[640:768, 128 * c:128 * (c + 1)], eG[:, 0:128], rd=["S.eG"])
            k.dma("sp", D["ya"][r0:r0 + 128, :], ys[:, :], rd=["S.ys"])
            k.dma("sp", sfin[r0:r0 + 128, :], Sf[:, :], rd=["S.Sf"])
        k.barrier()


def _phase_attn_a_out(k, nc, P, D):
    sfin_cc = D["sfin_cc_t"].ap()
    k.emit("pool", lambda e: e.collective_compute(
        "AllGather", ALU.bypass, replica_groups=[[0, 1], [2, 3], [4, 5], [6, 7]],
        ins=[D["sfin_t"].ap().opt()], outs=[D["sfin_cc_t"].ap().opt()]),
        sem="cc0", inc=1)
    for e in ENGS:
        k._wait(e, "cc0", 1)

    with contextlib.ExitStack() as pst:
        B = alloc_ln_bufs(nc, pst, "Y")
        Sin = pst.enter_context(nc.sbuf_tensor("Y_Sin", [128, NPAIR * 128], BF16))
        Sf32 = pst.enter_context(nc.sbuf_tensor("Y_Sf32", [128, 128], F32))
        qqc = pst.enter_context(nc.sbuf_tensor("Y_qq", [128, 512], BF16))
        ynb = pst.enter_context(nc.sbuf_tensor("Y_ynb", [128, FT * 512], BF16))
        wo = pst.enter_context(nc.sbuf_tensor("Y_wo", [128, FT * C], BF16))
        rx = pst.enter_context(nc.sbuf_tensor("Y_rx", [128, 512], F32))
        ro = pst.enter_context(nc.sbuf_tensor("Y_ro", [128, 512], F32))
        psC = pst.enter_context(nc.psum_tensor("Y_psC", [128, 512], F32))
        psP = [pst.enter_context(nc.psum_tensor(f"Y_psP{i}", [128, 512], F32)) for i in range(2)]

        k.dma("pool", wo[:, :], D["a_ow"][:, :], wr=["Y.wo"])
        for p in range(NPAIR):
            k.dma("sp", Sf32[:, :], sfin_cc[p * 128:(p + 1) * 128, :], wr=["Y.Sf32"])
            k.op("dve", lambda e, p=p: e.tensor_scalar_mul(
                Sin[:, p * 128:(p + 1) * 128], Sf32[:, :], P["parc"][:, 0:1]),
                rd=["Y.Sf32", "parc"], wr=["Y.Sin"])

        for ch in range(4):
            def pre(ch=ch):
                for p in range(NPAIR):
                    k.dma("pool", qqc[:, :],
                          D["qqg"][p * 128:(p + 1) * 128, ch * 512:(ch + 1) * 512],
                          wr=["Y.qqc"])
                    k.mm(psC[:, :], Sin[:, p * 128:(p + 1) * 128], qqc[:, :], True, True,
                         rd=["Y.Sin", "Y.qqc"], wr=["Y.psC"], pub=True)
                    k.op("dve", lambda e, p=p: e.tensor_add(
                        B["xc"][:, p * 512:(p + 1) * 512], B["xc"][:, p * 512:(p + 1) * 512],
                        psC[:, :]), rd=["Y.psC"], wr=["Y.x%d" % p])

            def out_cb(ft, xa):
                k.op("dve", lambda e, a=xa, ft=ft: e.tensor_copy(
                    ynb[:, ft * 512:(ft + 1) * 512], a), rd=["Y.x%d" % ft], wr=["Y.yn%d" % ft])

            ln_chunk(k, B, P, ch, D["ya"], out_cb,
                     gain=P["acols"][:, 9 * FT:10 * FT], bias=P["acols"][:, 10 * FT:11 * FT],
                     pre_cb=lambda _ch: pre())
            for ft2 in range(FT):
                b = ft2 % 2
                for ft in range(FT):
                    k.mm(psP[b][:, :], wo[:, ft * C + ft2 * 128:ft * C + (ft2 + 1) * 128],
                         ynb[:, ft * 512:(ft + 1) * 512], ft == 0, ft == FT - 1,
                         rd=["Y.wo", "Y.yn%d" % ft],
                         wr=["Y.psP%d" % b] if ft == FT - 1 else [],
                         war=["Y.psP%d" % b] if ft == 0 else [], pub=(ft == FT - 1))
                k.dma("sp", rx[:, :], D["xfm"][ft2 * 128:(ft2 + 1) * 128, ch * 512:(ch + 1) * 512],
                      wr=["Y.rx"])
                k.op("dve", lambda e, b=b: e.tensor_add(ro[:, :], psP[b][:, :], rx[:, :]),
                     rd=["Y.psP%d" % b, "Y.rx"], wr=["Y.ro"])
                k.dma("sp", D["r1"][ft2 * 128:(ft2 + 1) * 128, ch * 512:(ch + 1) * 512],
                      ro[:, :], rd=["Y.ro"])
        k.barrier()


def _phase_ffn(k, nc, P, D, rin, rout, tag):
    with contextlib.ExitStack() as pst:
        B = alloc_ln_bufs(nc, pst, tag)
        xln = pst.enter_context(nc.sbuf_tensor(tag + "_xln", [128, FT * T], BF16))
        wgb = [pst.enter_context(nc.sbuf_tensor(f"{tag}_wg{i}", [128, FT * 128], BF16))
               for i in range(2)]
        wub = [pst.enter_context(nc.sbuf_tensor(f"{tag}_wu{i}", [128, FT * 128], BF16))
               for i in range(2)]
        gbt = pst.enter_context(nc.sbuf_tensor(tag + "_gb", [128, T], BF16))
        ubt = pst.enter_context(nc.sbuf_tensor(tag + "_ub", [128, 512], BF16))
        fbt = pst.enter_context(nc.sbuf_tensor(tag + "_fb", [128, T], BF16))
        psG = [pst.enter_context(nc.psum_tensor(f"{tag}_psG{i}", [128, 512], F32))
               for i in range(2)]
        psU = [pst.enter_context(nc.psum_tensor(f"{tag}_psU{i}", [128, 512], F32))
               for i in range(2)]

        for ch in range(4):
            def out_cb(ft, xa, ch=ch):
                k.op("dve", lambda e, a=xa, ft=ft, ch=ch: e.tensor_copy(
                    xln[:, ft * T + ch * 512:ft * T + (ch + 1) * 512], a),
                    rd=[tag + ".x%d" % ft], wr=[tag + ".xln%d" % ft])
            ln_chunk(k, B, P, ch, rin, out_cb)

        if globals().get("_FFN_DBG") == "xln" and tag == "F1":
            for ft in range(FT):
                k.dma("pool", D["dbg"][ft * 128:(ft + 1) * 128, :],
                      xln[:, ft * T:(ft + 1) * T], rd=[tag + ".xln%d" % ft])
        for it in range(NIT):
            b = it % 2
            k.dma("pool", wgb[b][:, :], D["f_wg"][it, :, :], wr=[tag + ".wg%d" % b])
            k.dma("pool", wub[b][:, :], D["f_wu"][it, :, :], wr=[tag + ".wu%d" % b])
            if globals().get("_FFN_DBG") == "wslab" and tag == "F1" and it < FT:
                k.dma("pool", D["dbg"][it * 128:(it + 1) * 128, :], wgb[b][:, :],
                      rd=[tag + ".wg%d" % b])
            for ch in range(4):
                for ft in range(FT):
                    k.mm(psG[b][:, :], wgb[b][:, ft * 128:(ft + 1) * 128],
                         xln[:, ft * T + ch * 512:ft * T + (ch + 1) * 512],
                         ft == 0, ft == FT - 1,
                         rd=[tag + ".wg%d" % b, tag + ".xln%d" % ft],
                         wr=[tag + ".psG%d" % b] if ft == FT - 1 else [],
                         war=[tag + ".psG%d" % b] if ft == 0 else [], pub=(ft == FT - 1))
                    k.mm(psU[b][:, :], wub[b][:, ft * 128:(ft + 1) * 128],
                         xln[:, ft * T + ch * 512:ft * T + (ch + 1) * 512],
                         ft == 0, ft == FT - 1,
                         rd=[tag + ".wu%d" % b, tag + ".xln%d" % ft],
                         wr=[tag + ".psU%d" % b] if ft == FT - 1 else [],
                         war=[tag + ".psU%d" % b] if ft == 0 else [], pub=(ft == FT - 1))
                if globals().get("_FFN_DBG") == "graw" and tag == "F1" and it < FT:
                    k.op("act", lambda e, pg=psG[b]: e.copy(ubt[:, :], pg[:, :]),
                         rd=[tag + ".psG%d" % b], wr=[tag + ".ub"])
                    k.dma("pool", D["dbg"][it * 128:(it + 1) * 128, ch * 512:(ch + 1) * 512],
                          ubt[:, :], rd=[tag + ".ub"])
                k.op("act", lambda e, ch=ch, it=it, pg=psG[b]: e.activation(
                    gbt[:, ch * 512:(ch + 1) * 512], pg[:, :], AF.Silu,
                    scale=P["sgc"][:, it:it + 1]),
                    rd=[tag + ".psG%d" % b, "sgc"], wr=[tag + ".gb%d" % ch])
                k.op("act", lambda e, it=it, pu=psU[b]: e.activation(
                    ubt[:, :], pu[:, :], AF.Copy, scale=P["suc"][:, it:it + 1]),
                    rd=[tag + ".psU%d" % b, "suc"], wr=[tag + ".ub"])
                k.op("dve", lambda e, ch=ch: e.tensor_mul(
                    fbt[:, ch * 512:(ch + 1) * 512], gbt[:, ch * 512:(ch + 1) * 512], ubt[:, :]),
                    rd=[tag + ".gb%d" % ch, tag + ".ub"], wr=[tag + ".fb"])
            if globals().get("_FFN_DBG") == "gate" and tag == "F1" and it < FT:
                k.dma("pool", D["dbg"][it * 128:(it + 1) * 128, :],
                      gbt[:, :], rd=[tag + ".gb0", tag + ".gb1", tag + ".gb2", tag + ".gb3"])
            k.dma("sp", D["fuse"][it * 128:(it + 1) * 128, :], fbt[:, :], rd=[tag + ".fb"])
        k.barrier()

    with contextlib.ExitStack() as pst:
        psGd = [pst.enter_context(nc.psum_tensor(f"{tag}_psGd{i}", [128, 512], F32))
                for i in range(2)]
        fdb = pst.enter_context(nc.sbuf_tensor(tag + "_fdb", [128, NIT * 512], BF16))
        wdb = [pst.enter_context(nc.sbuf_tensor(f"{tag}_wd{i}", [128, NIT * 128], BF16))
               for i in range(2)]
        rxt = pst.enter_context(nc.sbuf_tensor(tag + "_rx", [128, 512], F32))
        rot = pst.enter_context(nc.sbuf_tensor(tag + "_ro", [128, 512], F32))
        for ch in range(4):
            k.dma("sp", fdb[:, :].rearrange("p (j f) -> p j f", f=512),
                  D["fuse"].rearrange("(j p) t -> p j t", p=128, j=64)[:, :, ch * 512:(ch + 1) * 512],
                  wr=[tag + ".fdb"])
            for dtile in range(FT):
                b = dtile % 2
                if ch == 0 or True:
                    pass
                if dtile < 2 and ch == 0:
                    k.dma("pool", wdb[b][:, :], D["f_wd"][dtile, :, :], wr=[tag + ".wd%d" % b])
                for jt in range(NIT):
                    k.mm(psGd[b][:, :], wdb[b][:, jt * 128:(jt + 1) * 128],
                         fdb[:, jt * 512:(jt + 1) * 512], jt == 0, jt == NIT - 1,
                         rd=[tag + ".wd%d" % b, tag + ".fdb"],
                         wr=[tag + ".psG%d" % b] if jt == NIT - 1 else [],
                         war=[tag + ".psG%d" % b] if jt == 0 else [], pub=(jt == NIT - 1))
                nd = dtile + 1 if dtile + 1 < FT else (0 if ch + 1 < 4 else None)
                if nd is not None and not (ch == 0 and dtile == 0):
                    k.dma("pool", wdb[nd % 2][:, :], D["f_wd"][nd, :, :],
                          wr=[tag + ".wd%d" % (nd % 2)])
                k.op("act", lambda e, dtile=dtile, pg=psGd[b]: e.activation(
                    rot[:, :], pg[:, :], AF.Copy, scale=P["sdc"][:, dtile:dtile + 1]),
                    rd=[tag + ".psG%d" % b, "sdc"], wr=[tag + ".rot"])
                k.dma("pool", rxt[:, :],
                      rin[dtile * 128:(dtile + 1) * 128, ch * 512:(ch + 1) * 512],
                      wr=[tag + ".rxt"])
                k.op("dve", lambda e: e.tensor_add(rot[:, :], rot[:, :], rxt[:, :]),
                     rd=[tag + ".rxt"], wr=[tag + ".rot"])
                k.dma("sp", rout[dtile * 128:(dtile + 1) * 128, ch * 512:(ch + 1) * 512],
                      rot[:, :], rd=[tag + ".rot"])
        k.barrier()


def _phase_attn_b(k, nc, P, D, upto):
    with contextlib.ExitStack() as pst:
        B = alloc_ln_bufs(nc, pst, "Xb")
        xbt = pst.enter_context(nc.sbuf_tensor("Xb_x", [128, FT * T], BF16))
        wb = pst.enter_context(nc.sbuf_tensor("Xb_w", [128, FT * C], BF16))
        ob = pst.enter_context(nc.sbuf_tensor("Xb_ob", [128, 512], BF16))
        o2 = pst.enter_context(nc.sbuf_tensor("Xb_o2", [128, 512], BF16))
        h1b = pst.enter_context(nc.sbuf_tensor("Xb_h1", [128, 512], BF16))
        scro = pst.enter_context(nc.sbuf_tensor("Xb_scro", [1, 512], F32))
        ps = [pst.enter_context(nc.psum_tensor(f"Xb_ps{i}", [128, 512], F32)) for i in range(2)]
        psH = pst.enter_context(nc.psum_tensor("Xb_psH", [128, 512], F32))
        psR = pst.enter_context(nc.psum_tensor("Xb_psR", [1, 512], F32))

        for ch in range(4):
            def out_cb(ft, xa, ch=ch):
                k.op("dve", lambda e, a=xa, ft=ft, ch=ch: e.tensor_copy(
                    xbt[:, ft * T + ch * 512:ft * T + (ch + 1) * 512], a),
                    rd=["Xb.x%d" % ft], wr=["Xb.xln%d" % ft])
            ln_chunk(k, B, P, ch, D["r2"], out_cb)

        def gemm_fm(wkey, wap, out_cb):
            k.dma("pool", wb[:, :], wap[:, :], wr=["Xb.wb"])
            for ft2 in range(FT):
                for ch in range(4):
                    b = (ft2 * 4 + ch) % 2
                    for ft in range(FT):
                        k.mm(ps[b][:, :], wb[:, ft * C + ft2 * 128:ft * C + (ft2 + 1) * 128],
                             xbt[:, ft * T + ch * 512:ft * T + (ch + 1) * 512],
                             ft == 0, ft == FT - 1, rd=["Xb.wb", "Xb.xln%d" % ft],
                             wr=["Xb.ps%d" % b] if ft == FT - 1 else [],
                             war=["Xb.ps%d" % b] if ft == 0 else [], pub=(ft == FT - 1))
                    out_cb(ft2, ch, b)

        def out_q(ft2, ch, b):
            k.op("act", lambda e, b=b: e.copy(ob[:, :], ps[b][:, :]),
                 rd=["Xb.ps%d" % b], wr=["Xb.ob"])
            k.dma("sp", D["qb"][ft2 * 128:(ft2 + 1) * 128, ch * 512:(ch + 1) * 512],
                  ob[:, :], rd=["Xb.ob"])
        gemm_fm("q", D["b_wqT"], out_q)

        def out_k(ft2, ch, b):
            k.op("act", lambda e, b=b: e.copy(ob[:, :], ps[b][:, :]),
                 rd=["Xb.ps%d" % b], wr=["Xb.ob"])
            k.mm(psH[:, :], P["sw1b"][:, :], ob[:, :], True, True,
                 rd=["sw1b", "Xb.ob"], wr=["Xb.psH"], pub=True)
            k.op("act", lambda e: e.activation(h1b[:, :], psH[:, :], AF.Relu,
                                               bias=P["sb1c"][:, 0:1]),
                 rd=["Xb.psH", "sb1c"], wr=["Xb.h1"])
            k.mm(psR[0:1, :], P["sw2b"][:, 0:1], h1b[:, :], True, True,
                 rd=["sw2b", "Xb.h1"], wr=["Xb.psR"], pub=True)
            k.op("act", lambda e: e.add(scro[0:1, :], psR[0:1, :], P["sb2c"][0:1, 0:1]),
                 rd=["Xb.psR", "sb2c"], wr=["Xb.scro"])
            k.dma("sp", D["scl_t"].ap()[ft2:ft2 + 1, ch * 512:(ch + 1) * 512],
                  scro[0:1, :], rd=["Xb.scro"])
        gemm_fm("k", D["b_wkT"], out_k)

        def gemm_tm(wap, outdram):
            k.dma("pool", wb[:, :], wap[:, :], wr=["Xb.wb"])
            for tt in range(FT):
                for dv in range(4):
                    b = (tt * 4 + dv) % 2
                    for ft in range(FT):
                        k.mm(ps[b][:, :], xbt[:, ft * T + tt * 128:ft * T + (tt + 1) * 128],
                             wb[:, ft * C + dv * 512:ft * C + (dv + 1) * 512],
                             ft == 0, ft == FT - 1, rd=["Xb.wb", "Xb.xln%d" % ft],
                             wr=["Xb.ps%d" % b] if ft == FT - 1 else [],
                             war=["Xb.ps%d" % b] if ft == 0 else [], pub=(ft == FT - 1))
                    k.op("dve", lambda e, b=b: e.tensor_copy(o2[:, :], ps[b][:, :]),
                         rd=["Xb.ps%d" % b], wr=["Xb.o2"])
                    for sl in range(4):
                        hh = dv * 4 + sl
                        k.dma("sp", outdram[hh, tt * 128:(tt + 1) * 128, :],
                              o2[:, sl * 128:(sl + 1) * 128], rd=["Xb.o2"])
        gemm_tm(D["b_wkT"], D["kbt"])
        gemm_tm(D["b_wvT"], D["vbt"])
        k.barrier()

    k.emit("pool", lambda e: e.collective_compute(
        "AllGather", ALU.bypass, replica_groups=[[0, 1], [2, 3], [4, 5], [6, 7]],
        ins=[D["scl_t"].ap().opt()], outs=[D["scl_cc_t"].ap().opt()]),
        sem="cc1", inc=1)
    for e in ENGS:
        k._wait(e, "cc1", 1)
    _emit_selection(k, nc, P, D, upto)
    if upto in ("xb", "scores", "sel", "ksel"):
        return
    k.emit("pool", lambda e: e.collective_compute(
        "AllReduce", ALU.add, replica_groups=[[0, 1], [2, 3], [4, 5], [6, 7]],
        ins=[D["kvsel_t"].ap().opt()], outs=[D["kvsel_cc_t"].ap().opt()]),
        sem="cc2", inc=1)
    for e in ENGS:
        k._wait(e, "cc2", 1)

    kvsel_cc = D["kvsel_cc_t"].ap()
    with contextlib.ExitStack() as pst:
        ybt = pst.enter_context(nc.sbuf_tensor("Ab_y", [128, FT * T], BF16))
        qh = pst.enter_context(nc.sbuf_tensor("Ab_qh", [128, T], BF16))
        kselb = pst.enter_context(nc.sbuf_tensor("Ab_ks", [128, 128], BF16))
        vselb = pst.enter_context(nc.sbuf_tensor("Ab_vs", [128, 128], BF16))
        kT = pst.enter_context(nc.sbuf_tensor("Ab_kT", [128, 128], BF16))
        expb = pst.enter_context(nc.sbuf_tensor("Ab_ex", [128, 128], BF16))
        expT = pst.enter_context(nc.sbuf_tensor("Ab_exT", [128, T], BF16))
        colsb = pst.enter_context(nc.sbuf_tensor("Ab_cols", [128, 4], F32))
        psP_ = pst.enter_context(nc.psum_tensor("Ab_psP", [128, 128], F32))
        psTb = pst.enter_context(nc.psum_tensor("Ab_psT", [128, 128], BF16))
        psA = pst.enter_context(nc.psum_tensor("Ab_psA", [128, 512], F32))

        for h in range(HB):
            k.dma("pool", qh[:, :], D["qb"][h * 128:(h + 1) * 128, :], wr=["Ab.qh"])
            k.dma("pool", kselb[:, :], kvsel_cc[h * 256:h * 256 + 128, :], wr=["Ab.ks"])
            k.dma("pool", vselb[:, :], kvsel_cc[h * 256 + 128:h * 256 + 256, :], wr=["Ab.vs"])
            k.op("pe", lambda e: e.transpose(psTb[:, :], kselb[:, :], P["eyeb"][:, :]),
                 rd=["Ab.ks", "eyeb"], wr=["Ab.psT"], pub=True)
            k.op("act", lambda e: e.copy(kT[:, :], psTb[:, :]), rd=["Ab.psT"], wr=["Ab.kT"])
            for tt in range(FT):
                k.mm(psP_[:, :], qh[:, tt * 128:(tt + 1) * 128], kT[:, :], True, True,
                     rd=["Ab.qh", "Ab.kT"], wr=["Ab.psP"], pub=True)
                k.op("dve", lambda e: e.reduce_max(colsb[:, 0:1], psP_[:, :], axis=AX.X),
                     rd=["Ab.psP"], wr=["Ab.c0"])
                k.op("dve", lambda e: e.tensor_scalar_mul(colsb[:, 1:2], colsb[:, 0:1], -SCL_B),
                     rd=["Ab.c0"], wr=["Ab.c1"])
                k.op("act", lambda e: e.activation(expb[:, :], psP_[:, :], AF.Exp,
                                                   bias=colsb[:, 1:2], scale=SCL_B,
                                                   accum_out=colsb[:, 2:3]),
                     rd=["Ab.psP", "Ab.c1"], wr=["Ab.exp", "Ab.c2"])
                k.op("dve", lambda e: e.reciprocal(colsb[:, 3:4], colsb[:, 2:3]),
                     rd=["Ab.c2"], wr=["Ab.c3"])
                k.op("dve", lambda e: e.tensor_scalar_mul(expb[:, :], expb[:, :], colsb[:, 3:4]),
                     rd=["Ab.c3"], wr=["Ab.exp"])
                k.op("pe", lambda e: e.transpose(psTb[:, :], expb[:, :], P["eyeb"][:, :]),
                     rd=["Ab.exp", "eyeb"], wr=["Ab.psT"], pub=True)
                k.op("act", lambda e, tt=tt: e.copy(expT[:, tt * 128:(tt + 1) * 128], psTb[:, :]),
                     rd=["Ab.psT"], wr=["Ab.exT%d" % (tt // 4)])
            for g in range(4):
                k.mm(psA[:, :], vselb[:, :], expT[:, g * 512:(g + 1) * 512], True, True,
                     rd=["Ab.vs", "Ab.exT%d" % g], wr=["Ab.psA"], pub=True)
                k.op("act", lambda e, h=h, g=g: e.copy(
                    ybt[:, h * T + g * 512:h * T + (g + 1) * 512], psA[:, :]),
                    rd=["Ab.psA"], wr=["Ab.y%d" % h])
        wb2 = pst.enter_context(nc.sbuf_tensor("Ab_wo", [128, FT * C], BF16))
        rxt = pst.enter_context(nc.sbuf_tensor("Ab_rx", [128, 512], F32))
        rot = pst.enter_context(nc.sbuf_tensor("Ab_ro", [128, 512], F32))
        psO2 = [pst.enter_context(nc.psum_tensor(f"Ab_psO{i}", [128, 512], F32))
                for i in range(2)]
        k.dma("pool", wb2[:, :], D["b_woT"][:, :], wr=["Ab.wo"])
        for ft2 in range(FT):
            for ch in range(4):
                b = (ft2 * 4 + ch) % 2
                for ft in range(FT):
                    k.mm(psO2[b][:, :], wb2[:, ft * C + ft2 * 128:ft * C + (ft2 + 1) * 128],
                         ybt[:, ft * T + ch * 512:ft * T + (ch + 1) * 512],
                         ft == 0, ft == FT - 1, rd=["Ab.wo", "Ab.y%d" % ft],
                         wr=["Ab.psO%d" % b] if ft == FT - 1 else [],
                         war=["Ab.psO%d" % b] if ft == 0 else [], pub=(ft == FT - 1))
                k.dma("sp", rxt[:, :], D["r2"][ft2 * 128:(ft2 + 1) * 128, ch * 512:(ch + 1) * 512],
                      wr=["Ab.rx"])
                if globals().get("_AB_NORES"):
                    k.op("dve", lambda e, po=psO2[b]: e.tensor_scalar_mul(rot[:, :], po[:, :], 1.0),
                         rd=["Ab.psO%d" % b, "Ab.rx"], wr=["Ab.rot"])
                else:
                    k.op("dve", lambda e, po=psO2[b]: e.tensor_add(rot[:, :], po[:, :], rxt[:, :]),
                         rd=["Ab.psO%d" % b, "Ab.rx"], wr=["Ab.rot"])
                k.dma("sp", D["r3"][ft2 * 128:(ft2 + 1) * 128, ch * 512:(ch + 1) * 512],
                      rot[:, :], rd=["Ab.rot"])
        k.barrier()


def _emit_selection(k, nc, P, D, upto):
    with contextlib.ExitStack() as pst:
        def sb(name, shape, dtyp=F32):
            return pst.enter_context(nc.sbuf_tensor(name, shape, dtyp))
        selt = sb("L_sel", [128, 512])
        mb = sb("L_mb", [128, 512])
        ms = sb("L_ms", [128, 512])
        elig = sb("L_el", [128, 512])
        eligv = sb("L_elv", [128, 512])
        ind = sb("L_ind", [128, 512])
        cs = sb("L_cs", [128, 512])
        fill1 = sb("L_f1", [128, 512])
        rankp = sb("L_rk", [128, 512])
        rankT = sb("L_rkT", [128, 512])
        w = sb("L_w", [128, 16])
        cc = sb("L_cc", [128, 4])
        mrow = sb("L_mrow", [4, 128])
        mred = sb("L_mred", [4, 16])
        mcolsT = sb("L_mcT", [16, 4])
        mtall = sb("L_mt", [128, FT * 128], BF16)
        kvsb = sb("L_kv", [128, FT * 128], BF16)
        oselk = sb("L_ok", [128, 128])
        oselv = sb("L_ov", [128, 128])
        psB = pst.enter_context(nc.psum_tensor("L_psB", [128, 4], F32))
        psX = pst.enter_context(nc.psum_tensor("L_psX", [128, 128], F32))
        psKV = pst.enter_context(nc.psum_tensor("L_psKV", [128, 128], F32))

        scl_cc = D["scl_cc_t"].ap()
        k.op("dve", lambda e: e.memset(w[:, :], 0.0), wr=["L.w"])
        k.dma_fn("pool", lambda e: e.indirect_dma_start(
            out=selt[:, :], out_offset=None,
            in_=scl_cc.rearrange("r (c f) -> (r c) f", f=512),
            in_offset=IndirectOffsetOnAxis(P["permt"][:, 0:1], 0)),
            rd=["permt"], wr=["L.sel"])
        if upto == "scores":
            for ft in range(FT):
                k.dma("sp", D["dbg"][ft * 128:(ft + 1) * 128, 0:512], selt[:, :], rd=["L.sel"])
            return

        W = lambda i: w[:, i:i + 1]
        def hsum2(src2, n, outslot, rdk, wrk):
            k.mm(psB[:, 0:n], P["Hm"][:, :], src2, True, True,
                 rd=[rdk, "Hm"], wr=["L.psB"], pub=True)
            k.op("act", lambda e: e.copy(w[:, outslot:outslot + n], psB[:, 0:n]),
                 rd=["L.psB"], wr=[wrk])

        k.op("dve", lambda e: e.reduce_max(cc[:, 0:1], selt[:, :], axis=AX.X),
             rd=["L.sel"], wr=["L.cc"])
        k.op("dve", lambda e: e.tensor_scalar_mul(ms[:, :], selt[:, :], -1.0),
             rd=["L.sel"], wr=["L.ms"])
        k.op("dve", lambda e: e.reduce_max(cc[:, 1:2], ms[:, :], axis=AX.X),
             rd=["L.ms"], wr=["L.cc"])
        k.op("pe", lambda e: e.transpose(psX[0:2, 0:128], cc[:, 0:2], P["eyef"][:, :]),
             rd=["L.cc", "eyef"], wr=["L.psX"], pub=True)
        k.op("act", lambda e: e.copy(mrow[0:2, :], psX[0:2, 0:128]),
             rd=["L.psX"], wr=["L.mrow"])
        k.op("dve", lambda e: e.reduce_max(
            mred[0:2, :], mrow[0:2, :].rearrange("p (h s) -> p h s", s=8), axis=AX.X),
            rd=["L.mrow"], wr=["L.mred"])
        k.op("pe", lambda e: e.transpose(psX[0:16, 0:2], mred[0:2, 0:16], P["eyef"][0:2, 0:2]),
             rd=["L.mred", "eyef"], wr=["L.psX"], pub=True)
        k.op("act", lambda e: e.copy(mcolsT[0:16, 0:2], psX[0:16, 0:2]),
             rd=["L.psX"], wr=["L.mcT"])
        k.mm(psB[:, 0:2], P["Hs"][0:16, :], mcolsT[0:16, 0:2], True, True,
             rd=["L.mcT", "Hs"], wr=["L.psB"], pub=True)
        k.op("dve", lambda e: e.tensor_scalar(W(1), psB[:, 1:2], -1.0, -1.0, ALU.mult, ALU.add),
             rd=["L.psB"], wr=["L.w"])
        k.op("act", lambda e: e.copy(W(2), psB[:, 0:1]), rd=["L.psB"], wr=["L.w"])

        k.op("dve", lambda e: e.reduce_sum(cc[:, 0:1], selt[:, :], axis=AX.X),
             rd=["L.sel"], wr=["L.cc"])
        hsum2(cc[:, 0:1], 1, 5, "L.cc", "L.w")
        k.op("dve", lambda e: e.tensor_scalar(W(0), W(5), -float(KSEL), 1.0 / 4096.0,
                                              ALU.add, ALU.mult), rd=["L.w"], wr=["L.w"])
        for _ in range(16):
            k.op("dve", lambda e: e.tensor_scalar(mb[:, :], selt[:, :], W(0), None, ALU.is_gt),
                 rd=["L.sel", "L.w"], wr=["L.mb"])
            k.op("dve", lambda e: e.tensor_mul(ms[:, :], mb[:, :], selt[:, :]),
                 rd=["L.mb", "L.sel"], wr=["L.ms"])
            k.op("dve", lambda e: e.reduce_sum(cc[:, 0:1], ms[:, :], axis=AX.X),
                 rd=["L.ms"], wr=["L.cc"])
            k.op("dve", lambda e: e.reduce_sum(cc[:, 1:2], mb[:, :], axis=AX.X),
                 rd=["L.mb"], wr=["L.cc"])
            k.mm(psB[:, 0:2], P["Hm"][:, :], cc[:, 0:2], True, True,
                 rd=["L.cc", "Hm"], wr=["L.psB"], pub=True)
            k.op("dve", lambda e: e.tensor_scalar_add(W(11), psB[:, 0:1], -float(KSEL)),
                 rd=["L.psB"], wr=["L.w"])
            k.op("dve", lambda e: e.reciprocal(W(12), psB[:, 1:2]), rd=["L.psB"], wr=["L.w"])
            k.op("dve", lambda e: e.tensor_mul(W(0), W(11), W(12)), rd=["L.w"], wr=["L.w"])

        k.op("dve", lambda e: e.tensor_scalar(mb[:, :], selt[:, :], W(0), None, ALU.is_gt),
             rd=["L.sel", "L.w"], wr=["L.mb"])
        k.op("dve", lambda e: e.reduce_sum(cc[:, 0:1], mb[:, :], axis=AX.X),
             rd=["L.mb"], wr=["L.cc"])
        k.mm(psB[:, 0:1], P["Hm"][:, :], cc[:, 0:1], True, True,
             rd=["L.cc", "Hm"], wr=["L.psB"], pub=True)
        k.op("dve", lambda e: e.tensor_scalar(W(13), psB[:, 0:1], float(KSEL), None, ALU.is_ge),
             rd=["L.psB"], wr=["L.w"])

        for _ in range(40):
            k.op("dve", lambda e: e.tensor_add(W(3), W(1), W(2)), rd=["L.w"], wr=["L.w"])
            k.op("dve", lambda e: e.tensor_scalar_mul(W(3), W(3), 0.5), wr=["L.w"])
            k.op("dve", lambda e: e.tensor_scalar(mb[:, :], selt[:, :], W(3), None, ALU.is_gt),
                 rd=["L.sel", "L.w"], wr=["L.mb"])
            k.op("dve", lambda e: e.reduce_sum(cc[:, 0:1], mb[:, :], axis=AX.X),
                 rd=["L.mb"], wr=["L.cc"])
            k.mm(psB[:, 0:1], P["Hm"][:, :], cc[:, 0:1], True, True,
                 rd=["L.cc", "Hm"], wr=["L.psB"], pub=True)
            k.op("dve", lambda e: e.tensor_scalar(W(6), psB[:, 0:1], float(KSEL), None, ALU.is_ge),
                 rd=["L.psB"], wr=["L.w"])
            k.op("dve", lambda e: e.tensor_sub(W(11), W(3), W(1)), wr=["L.w"])
            k.op("dve", lambda e: e.tensor_mul(W(11), W(11), W(6)), wr=["L.w"])
            k.op("dve", lambda e: e.tensor_add(W(1), W(1), W(11)), wr=["L.w"])
            k.op("dve", lambda e: e.tensor_sub(W(11), W(3), W(2)), wr=["L.w"])
            k.op("dve", lambda e: e.tensor_scalar(W(12), W(6), -1.0, 1.0, ALU.mult, ALU.add),
                 wr=["L.w"])
            k.op("dve", lambda e: e.tensor_mul(W(11), W(11), W(12)), wr=["L.w"])
            k.op("dve", lambda e: e.tensor_add(W(2), W(2), W(11)), wr=["L.w"])

        k.op("dve", lambda e: e.tensor_sub(W(11), W(2), W(0)), rd=["L.w"], wr=["L.w"])
        k.op("dve", lambda e: e.tensor_mul(W(11), W(11), W(13)), wr=["L.w"])
        k.op("dve", lambda e: e.tensor_add(W(7), W(0), W(11)), wr=["L.w"])
        k.op("dve", lambda e: e.tensor_scalar(ind[:, :], selt[:, :], W(7), None, ALU.is_gt),
             rd=["L.sel", "L.w"], wr=["L.ind"])
        k.op("dve", lambda e: e.reduce_sum(cc[:, 0:1], ind[:, :], axis=AX.X),
             rd=["L.ind"], wr=["L.cc"])
        k.mm(psB[:, 0:1], P["Hm"][:, :], cc[:, 0:1], True, True,
             rd=["L.cc", "Hm"], wr=["L.psB"], pub=True)
        k.op("dve", lambda e: e.tensor_scalar(W(8), psB[:, 0:1], -1.0, float(KSEL),
                                              ALU.mult, ALU.add), rd=["L.psB"], wr=["L.w"])
        k.op("dve", lambda e: e.tensor_scalar(elig[:, :], selt[:, :], W(7), None, ALU.is_le),
             rd=["L.sel", "L.w"], wr=["L.el"])

        def gcs(src, srckey, outap):
            k.op("dve", lambda e, s=src, o=outap: e.tensor_tensor_scan(
                o, s, s, 0.0, ALU.add, ALU.bypass), rd=[srckey], wr=["L.cs"])
            k.op("dve", lambda e, s=src: e.reduce_sum(cc[:, 0:1], s, axis=AX.X),
                 rd=[srckey], wr=["L.cc"])
            k.mm(psB[:, 0:1], P["Lm"][:, :], cc[:, 0:1], True, True,
                 rd=["L.cc", "Lm"], wr=["L.psB"], pub=True)
            k.op("act", lambda e: e.copy(W(9), psB[:, 0:1]), rd=["L.psB"], wr=["L.w"])
            k.op("dve", lambda e, o=outap: e.tensor_scalar_add(o, o, W(9)),
                 rd=["L.w"], wr=["L.cs"])
        gcs(elig[:, :], "L.el", cs[:, :])
        k.op("dve", lambda e: e.tensor_scalar(ms[:, :], cs[:, :], W(8), None, ALU.is_le),
             rd=["L.cs", "L.w"], wr=["L.ms"])
        k.op("dve", lambda e: e.tensor_mul(ms[:, :], ms[:, :], elig[:, :]),
             rd=["L.el"], wr=["L.ms"])
        k.op("dve", lambda e: e.tensor_copy(eligv[:, :], elig[:, :]), rd=["L.el"], wr=["L.elv"])
        k.op("dve", lambda e: e.memset(fill1[:, :], 0.0), wr=["L.f1"])
        k.op("act", lambda e: e.copy(W(15), W(8)), rd=["L.w"], wr=["L.w"])
        for _ in range(4):
            k.op("dve", lambda e: e.tensor_scalar(cs[:, :], eligv[:, :], 1.0e9, -1.0e9,
                                                  ALU.mult, ALU.add),
                 rd=["L.elv"], wr=["L.cs"])
            k.op("dve", lambda e: e.tensor_mul(mb[:, :], selt[:, :], eligv[:, :]),
                 rd=["L.sel", "L.elv"], wr=["L.mb"])
            k.op("dve", lambda e: e.tensor_add(mb[:, :], mb[:, :], cs[:, :]),
                 rd=["L.cs"], wr=["L.mb"])
            k.op("dve", lambda e: e.reduce_max(cc[:, 0:1], mb[:, :], axis=AX.X),
                 rd=["L.mb"], wr=["L.cc"])
            k.op("pe", lambda e: e.transpose(psX[0:1, 0:128], cc[:, 0:1], P["eyef"][:, :]),
                 rd=["L.cc", "eyef"], wr=["L.psX"], pub=True)
            k.op("act", lambda e: e.copy(mrow[0:1, :], psX[0:1, 0:128]),
                 rd=["L.psX"], wr=["L.mrow"])
            k.op("dve", lambda e: e.reduce_max(
                mred[0:1, :], mrow[0:1, :].rearrange("p (h s) -> p h s", s=8), axis=AX.X),
                rd=["L.mrow"], wr=["L.mred"])
            k.op("pe", lambda e: e.transpose(psX[0:16, 0:1], mred[0:1, 0:16], P["eyef"][0:1, 0:1]),
                 rd=["L.mred", "eyef"], wr=["L.psX"], pub=True)
            k.op("act", lambda e: e.copy(mcolsT[0:16, 0:1], psX[0:16, 0:1]),
                 rd=["L.psX"], wr=["L.mcT"])
            k.mm(psB[:, 0:1], P["Hs"][0:16, :], mcolsT[0:16, 0:1], True, True,
                 rd=["L.mcT", "Hs"], wr=["L.psB"], pub=True)
            k.op("act", lambda e: e.copy(W(10), psB[:, 0:1]), rd=["L.psB"], wr=["L.w"])
            k.op("dve", lambda e: e.tensor_scalar(mb[:, :], selt[:, :], W(10), None, ALU.is_equal),
                 rd=["L.sel", "L.w"], wr=["L.mb"])
            k.op("dve", lambda e: e.tensor_mul(mb[:, :], mb[:, :], eligv[:, :]),
                 rd=["L.elv"], wr=["L.mb"])
            gcs(mb[:, :], "L.mb", cs[:, :])
            k.op("dve", lambda e: e.tensor_scalar(ind[:, 0:0 + 512] if False else cs[:, :],
                                                  cs[:, :], W(15), None, ALU.is_le),
                 rd=["L.cs", "L.w"], wr=["L.cs"])
            k.op("dve", lambda e: e.tensor_mul(cs[:, :], cs[:, :], mb[:, :]),
                 rd=["L.mb"], wr=["L.cs"])
            k.op("dve", lambda e: e.tensor_add(fill1[:, :], fill1[:, :], cs[:, :]),
                 rd=["L.cs"], wr=["L.f1"])
            k.op("dve", lambda e: e.reduce_sum(cc[:, 0:1], cs[:, :], axis=AX.X),
                 rd=["L.cs"], wr=["L.cc"])
            k.mm(psB[:, 0:1], P["Hm"][:, :], cc[:, 0:1], True, True,
                 rd=["L.cc", "Hm"], wr=["L.psB"], pub=True)
            k.op("dve", lambda e: e.tensor_sub(W(15), W(15), psB[:, 0:1]),
                 rd=["L.psB"], wr=["L.w"])
            k.op("dve", lambda e: e.tensor_sub(eligv[:, :], eligv[:, :], mb[:, :]),
                 rd=["L.mb"], wr=["L.elv"])
        k.op("dve", lambda e: e.tensor_sub(fill1[:, :], fill1[:, :], ms[:, :]),
             rd=["L.ms"], wr=["L.f1"])
        k.op("dve", lambda e: e.tensor_scalar_mul(fill1[:, :], fill1[:, :], W(13)),
             rd=["L.w"], wr=["L.f1"])
        k.op("dve", lambda e: e.tensor_add(fill1[:, :], fill1[:, :], ms[:, :]),
             rd=["L.ms"], wr=["L.f1"])
        k.op("dve", lambda e: e.tensor_add(ind[:, :], ind[:, :], fill1[:, :]),
             rd=["L.f1"], wr=["L.ind"])
        if upto == "sel":
            k.dma("sp", D["dbg"][0:128, 0:512], ind[:, :], rd=["L.ind"])
            k.dma("sp", D["dbg"][128:256, 0:16], w[:, :], rd=["L.w"])
            k.dma("sp", D["dbg"][256:384, 0:512], ms[:, :], rd=["L.ms"])
            k.dma("sp", D["dbg"][384:512, 0:512], fill1[:, :], rd=["L.f1"])
            k.dma("sp", D["dbg"][512:640, 0:512], elig[:, :], rd=["L.el"])
            return

        gcs(ind[:, :], "L.ind", cs[:, :])
        k.op("dve", lambda e: e.tensor_mul(rankp[:, :], cs[:, :], ind[:, :]),
             rd=["L.cs", "L.ind"], wr=["L.rk"])
        for b4 in range(4):
            k.op("pe", lambda e, b4=b4: e.transpose(psX[:, :], rankp[:, b4 * 128:(b4 + 1) * 128],
                                                    P["eyef"][:, :]),
                 rd=["L.rk", "eyef"], wr=["L.psX"], pub=True)
            k.op("act", lambda e, b4=b4: e.copy(rankT[:, b4 * 128:(b4 + 1) * 128], psX[:, :]),
                 rd=["L.psX"], wr=["L.rkT"])

        kvsel = D["kvsel_t"].ap()
        for h in range(HB):
            for lb in range(FT):
                col = 128 * (lb % 4) + 8 * h + (lb // 4)
                k.op("dve", lambda e, lb=lb, col=col: e.tensor_scalar(
                    mtall[:, lb * 128:(lb + 1) * 128], P["Jb"][:, :], rankT[:, col:col + 1],
                    None, ALU.is_equal), rd=["Jb", "L.rkT"], wr=["L.mt"])
            k.dma("pool", kvsb[:, :].rearrange("p (tt d) -> p tt d", d=128),
                  D["kbt"][h].rearrange("(tt p) d -> p tt d", p=128, tt=16), wr=["L.kvs"])
            for lb in range(FT):
                k.mm(psKV[:, :], mtall[:, lb * 128:(lb + 1) * 128],
                     kvsb[:, lb * 128:(lb + 1) * 128], lb == 0, lb == FT - 1,
                     rd=["L.mt", "L.kvs"], wr=["L.psKV"] if lb == FT - 1 else [],
                     war=["L.psKV"] if lb == 0 else [], pub=(lb == FT - 1))
            k.op("act", lambda e: e.copy(oselk[:, :], psKV[:, :]), rd=["L.psKV"], wr=["L.ok"])
            k.dma("sp", kvsel[h * 256:h * 256 + 128, :], oselk[:, :], rd=["L.ok"])
            k.dma("pool", kvsb[:, :].rearrange("p (tt d) -> p tt d", d=128),
                  D["vbt"][h].rearrange("(tt p) d -> p tt d", p=128, tt=16), wr=["L.kvs"])
            for lb in range(FT):
                k.mm(psKV[:, :], mtall[:, lb * 128:(lb + 1) * 128],
                     kvsb[:, lb * 128:(lb + 1) * 128], lb == 0, lb == FT - 1,
                     rd=["L.mt", "L.kvs"], wr=["L.psKV"] if lb == FT - 1 else [],
                     war=["L.psKV"] if lb == 0 else [], pub=(lb == FT - 1))
            k.op("act", lambda e: e.copy(oselv[:, :], psKV[:, :]), rd=["L.psKV"], wr=["L.ov"])
            k.dma("sp", kvsel[h * 256 + 128:h * 256 + 256, :], oselv[:, :], rd=["L.ov"])
        if upto == "ksel":
            for ft in range(FT):
                k.dma("sp", D["dbg"][ft * 128:(ft + 1) * 128, 0:128],
                      kvsel[ft * 256:ft * 256 + 128, :], deps=[("sp_d", k.cnt["sp_d"])])
        k.barrier()



def _np_col(v):
    return np.ascontiguousarray(np.asarray(v, dtype=np.float32).reshape(-1, 128).T)


def _lhsT_pack(W):
    W = np.ascontiguousarray(W)
    din, dout = W.shape
    return np.ascontiguousarray(
        W.reshape(din // 128, 128, dout).transpose(1, 0, 2).reshape(128, -1))


def _host_constants(core):
    half = core % 2
    p = np.arange(128) % 64
    tcol = np.arange(64)
    maskD = ((p[:, None] // LSUB == tcol[None, :] // LSUB) &
             (p[:, None] <= tcol[None, :])).astype(np.float32)
    maskS = np.broadcast_to(
        ((tcol[None, :] // LSUB == p[:, None] // LSUB + 1) &
         np.isin(tcol[None, :] // LSUB, (1, 3))), (128, 64)).astype(np.float32)
    J = np.broadcast_to(np.arange(1, 129, dtype=np.float32)[None, :], (128, 128)).copy()
    seg = np.arange(128) % 8
    h = np.arange(128) // 8
    gseg = np.where(seg < 4, seg + 4 * half, (seg - 4) + 4 * (1 - half))
    Lmask = ((h[:, None] == h[None, :]) & (gseg[:, None] < gseg[None, :])).astype(np.float32)
    Hmask = (h[:, None] == h[None, :]).astype(np.float32)
    Hsel = (np.arange(16)[:, None] == h[None, :]).astype(np.float32)
    iota = (gseg[:, None] * 512 + np.arange(512)[None, :]).astype(np.float32)
    perm = np.empty(128, np.uint32)
    for hh in range(16):
        for sg in range(8):
            rank = half if sg < 4 else 1 - half
            perm[8 * hh + sg] = (rank * HB + hh) * 4 + (sg % 4)
    eye = np.eye(128, dtype=np.float32)
    return dict(c_maskD=maskD, c_maskS=np.ascontiguousarray(maskS), c_J=J,
                c_Lmask=Lmask, c_Hmask=Hmask, c_Hsel=Hsel, c_iota=iota,
                c_perm=perm.reshape(128, 1), c_eye=eye,
                c_par=np.full((128, 1), float(half), np.float32))


def kernel(**inputs):
    g = lambda n: np.asarray(inputs[n])
    x = g("L0_attn_a_x")
    nc = build_program(_BUILD_UPTO)

    acols = np.concatenate([
        _np_col(g("L0_attn_a_maa_x")), _np_col(g("L0_attn_a_maa_r")),
        _np_col(g("L0_attn_a_maa_k")), _np_col(g("L0_attn_a_maa_v")),
        _np_col(g("L0_attn_a_maa_w")), _np_col(g("L0_attn_a_q_b")),
        _np_col(g("L0_attn_a_k_b")), _np_col(g("L0_attn_a_v_b")),
        _np_col(g("L0_attn_a_time_decay")), _np_col(g("L0_attn_a_ln_g")),
        _np_col(g("L0_attn_a_ln_b")), np.zeros((128, FT), np.float32)], axis=1)
    wg = np.asarray(g("L0_ffn_w_gate"))
    wu = np.asarray(g("L0_ffn_w_up"))
    wd = np.asarray(g("L0_ffn_w_down"))
    common = {
        "a_qw": _lhsT_pack(g("L0_attn_a_q_w")), "a_kw": _lhsT_pack(g("L0_attn_a_k_w")),
        "a_vw": _lhsT_pack(g("L0_attn_a_v_w")), "a_ow": _lhsT_pack(g("L0_attn_a_o_w")),
        "a_maa_w1": _lhsT_pack(g("L0_attn_a_maa_w1")),
        "a_maa_w2": g("L0_attn_a_maa_w2"),
        "a_td_w1": _lhsT_pack(g("L0_attn_a_td_w1")), "a_td_w2": g("L0_attn_a_td_w2"),
        "a_cols": acols, "a_vb_row": g("L0_attn_a_v_b").reshape(1, C),
        "b_wqT": _lhsT_pack(g("L0_attn_b_wq").T), "b_wkT": _lhsT_pack(g("L0_attn_b_wk").T),
        "b_wvT": _lhsT_pack(g("L0_attn_b_wv").T), "b_woT": _lhsT_pack(g("L0_attn_b_wo").T),
        "b_sw1T": np.ascontiguousarray(g("L0_attn_b_sw1").T),
        "b_sw2c": np.ascontiguousarray(g("L0_attn_b_sw2").reshape(128, 1)),
        "b_sb1c": np.ascontiguousarray(g("L0_attn_b_sb1").reshape(128, 1)),
        "b_sb2": g("L0_attn_b_sb2").reshape(1, 1),
        "f_wg": np.ascontiguousarray(
            wg.reshape(FT, 128, NIT, 128).transpose(2, 1, 0, 3).reshape(NIT, 128, FT * 128)),
        "f_wu": np.ascontiguousarray(
            wu.reshape(FT, 128, NIT, 128).transpose(2, 1, 0, 3).reshape(NIT, 128, FT * 128)),
        "f_wd": np.ascontiguousarray(
            wd.reshape(NIT, 128, FT, 128).transpose(2, 1, 0, 3).reshape(FT, 128, NIT * 128)),
        "f_sg": _np_col(g("L0_ffn_s_gate")), "f_su": _np_col(g("L0_ffn_s_up")),
        "f_sd": _np_col(g("L0_ffn_s_down")),
    }
    in_maps = []
    for core in range(NCORE):
        b, half = core // 2, core % 2
        m = dict(common)
        m.update(_host_constants(core))
        m["xfm"] = np.ascontiguousarray(x[b, half * T:(half + 1) * T, :].T)
        xp = np.zeros(C, np.float32) if half == 0 else np.ascontiguousarray(x[b, T - 1, :])
        m["xprev"] = _np_col(xp)
        in_maps.append(m)

    res = run_bass_kernel_spmd(nc, in_maps, core_ids=list(range(NCORE)))
    out = np.empty((4, 4096, C), np.float32)
    for core in range(NCORE):
        b, half = core // 2, core % 2
        out[b, half * T:(half + 1) * T, :] = res.results[core]["yfm"].T
    return out
